# version 21
# baseline (speedup 1.0000x reference)
import contextlib

import numpy as np
import ml_dtypes

import concourse.bass as bass
import concourse.mybir as mybir
from concourse.bass_utils import run_bass_kernel_spmd

F32 = mybir.dt.float32
BF16 = mybir.dt.bfloat16
OP = mybir.AluOpType
AF = mybir.ActivationFunctionType
AX = mybir.AxisListType.X

NCORES = 8
S = 2048
D = 1024
NT = S // 128
NHG = 8
NQG = 4
NKT = 16

_N_SUB = 4


def _tile2(m):
    r, c = m.shape
    return np.ascontiguousarray(m.reshape(r // 128, 128, c).transpose(1, 0, 2).reshape(128, (r // 128) * c))


def _hg_stream(wT):
    return np.ascontiguousarray(np.stack([_tile2(wT[:, hg * 128:(hg + 1) * 128]) for hg in range(8)]))


def _bias_tile(b):
    return np.ascontiguousarray(b.reshape(8, 128).T)



class _Op:
    __slots__ = ("eng", "fn", "deps", "drains", "eidx", "needs_inc", "ord", "is_dma", "sem", "tag")

    def __init__(self, eng, fn, tag=""):
        self.eng = eng
        self.fn = fn
        self.deps = []
        self.drains = []
        self.eidx = -1
        self.needs_inc = False
        self.ord = None
        self.is_dma = False
        self.sem = None
        self.tag = tag


class Prog:
    ENGS = ("tensor", "vector", "scalar", "sync", "gpsimd")
    DRAIN_ENGS = ("vector", "scalar", "gpsimd")

    def __init__(self):
        self.eng_ops = {e: [] for e in self.ENGS}
        self.bufw = {}
        self.bufr = {}
        self.dma_cum = {}

    def _resolve(self, eng, reads, writes):
        deps, drains = [], []

        def add(tok):
            if tok is None:
                return
            if tok[0] == "op" and tok[1].eng == eng and eng in self.DRAIN_ENGS:
                drains.append(tok[1])
            elif tok[0] == "op" and tok[1].eng == eng:
                pass
            else:
                deps.append(tok)

        for k in reads:
            add(self.bufw.get(k))
        for k in writes:
            add(self.bufw.get(k))
            for t in self.bufr.get(k, ()):
                add(t)
        seen, d2 = set(), []
        for t in deps:
            key = id(t[1]) if t[0] == "op" else (t[1], t[2])
            if key not in seen:
                seen.add(key)
                d2.append(t)
        return d2, list({id(o): o for o in drains}.values())

    def op(self, eng, fn, reads=(), writes=(), tag=""):
        o = _Op(eng, fn, tag)
        o.deps, o.drains = self._resolve(eng, reads, writes)
        tok = ("op", o)
        for k in reads:
            self.bufr.setdefault(k, []).append(tok)
        for k in writes:
            self.bufw[k] = tok
            self.bufr[k] = []
        o.eidx = len(self.eng_ops[eng])
        self.eng_ops[eng].append(o)
        return o

    def dma(self, queue, sem_name, fn, reads=(), writes=(), tag=""):
        o = _Op(queue, fn, tag)
        o.is_dma = True
        o.sem = sem_name
        o.deps, o.drains = self._resolve(queue, reads, writes)
        self.dma_cum[sem_name] = self.dma_cum.get(sem_name, 0) + 16
        tok = ("dma", sem_name, self.dma_cum[sem_name])
        for k in reads:
            self.bufr.setdefault(k, []).append(tok)
        for k in writes:
            self.bufw[k] = tok
            self.bufr[k] = []
        o.eidx = len(self.eng_ops[queue])
        self.eng_ops[queue].append(o)
        return o

    def settle(self, sem_name, keys):
        tok = ("dma", sem_name, self.dma_cum[sem_name])
        for k in keys:
            w = self.bufw.get(k)
            if w is not None and w[0] == "dma" and w[1] == sem_name:
                self.bufw[k] = tok
            rl = self.bufr.get(k)
            if rl:
                self.bufr[k] = [tok if (t[0] == "dma" and t[1] == sem_name) else t for t in rl]

    def finalize(self):
        for ops in self.eng_ops.values():
            for o in ops:
                for t in o.deps:
                    if t[0] == "op":
                        t[1].needs_inc = True
        for ops in self.eng_ops.values():
            n = 0
            for o in ops:
                if o.needs_inc:
                    if o.is_dma:
                        raise RuntimeError(f"dma op {o.tag} marked needs_inc")
                    n += 1
                    o.ord = n

    def emit(self, block, eng_sems, dma_sems, post=None):
        prog = self

        def runner(eng_name):
            def run(q):
                waited = {}
                last_drain = -1
                for pos, o in enumerate(prog.eng_ops[eng_name]):
                    if o.drains and max(dd.eidx for dd in o.drains) > last_drain:
                        q.drain()
                        last_drain = pos - 1
                    for t in o.deps:
                        if t[0] == "op":
                            dep = t[1]
                            if dep.ord is None:
                                raise RuntimeError(f"{o.tag} depends on un-inc'd {dep.tag}")
                            sem, val = eng_sems[dep.eng], dep.ord
                        else:
                            sem, val = dma_sems[t[1]], t[2]
                        sid = id(sem)
                        if waited.get(sid, -1) < val:
                            q.wait_ge(sem, val)
                            waited[sid] = val
                    inst = o.fn(q)
                    if o.needs_inc:
                        inst.then_inc(eng_sems[o.eng], 1)
                if post and eng_name in post:
                    post[eng_name](q)
            return run

        block.tensor(runner("tensor"))
        block.vector(runner("vector"))
        block.scalar(runner("scalar"))
        block.sync(runner("sync"))
        block.gpsimd(runner("gpsimd"))



def build_nc(n_sub=4):
    nc = bass.Bass("TRN2", target_bir_lowering=False)
    ctx = contextlib.ExitStack()

    hx = nc.dram_tensor("x", [2, S, D], F32, kind="ExternalInput").ap()
    hy = nc.dram_tensor("y", [2, S, D], F32, kind="ExternalOutput").ap()
    hwq = {k: nc.dram_tensor(f"wq_{k}", [8, 128, 1024], F32, kind="ExternalInput").ap() for k in "ab"}
    hwk = {k: nc.dram_tensor(f"wk_{k}", [8, 128, 1024], F32, kind="ExternalInput").ap() for k in "ab"}
    hwv = {k: nc.dram_tensor(f"wv_{k}", [8, 128, 1024], F32, kind="ExternalInput").ap() for k in "ab"}
    hwo = {k: nc.dram_tensor(f"wo_{k}", [128, 8192], F32, kind="ExternalInput").ap() for k in "ab"}
    hbase = nc.dram_tensor("base", [128, 8192], F32, kind="ExternalInput").ap()
    hast = nc.dram_tensor("ast", [128, 1024], F32, kind="ExternalInput").ap()
    hbst = nc.dram_tensor("bst", [128, 1024], F32, kind="ExternalInput").ap()
    hrouter = nc.dram_tensor("router", [128, 64], F32, kind="ExternalInput").ap()
    hbb = nc.dram_tensor("base_b", [1, D], F32, kind="ExternalInput").ap()
    hrb = nc.dram_tensor("router_b", [1, 8], F32, kind="ExternalInput").ap()
    hbo = nc.dram_tensor("bo", [1, D], F32, kind="ExternalInput").ap()
    hbvr = nc.dram_tensor("bvrow", [1, D], F32, kind="ExternalInput").ap()
    hbqt = nc.dram_tensor("bqt", [128, 8], F32, kind="ExternalInput").ap()
    hbkt = nc.dram_tensor("bkt", [128, 8], F32, kind="ExternalInput").ap()
    hbvt = nc.dram_tensor("bvt", [128, 8], F32, kind="ExternalInput").ap()
    hmask = nc.dram_tensor("head_mask", [1, 16], F32, kind="ExternalInput").ap()
    hbigi = nc.dram_tensor("bigi", [128, 896], BF16, kind="ExternalInput").ap()
    hrep = nc.dram_tensor("repmat", [8, 128], BF16, kind="ExternalInput").ap()
    hrr = [nc.dram_tensor(f"hbm_rr{i}", [2, 2048], BF16, kind="Internal").ap() for i in range(2)]

    def alloc(name, shape, dtype):
        return ctx.enter_context(nc.sbuf_tensor("s_" + name, shape, dtype))

    x = alloc("xbuf", [128, NT * 1024], F32)
    xnT = alloc("xnT", [128, 8 * 2048], BF16)
    aoT = alloc("aoT", [128, 8 * 2048], BF16)
    qt = alloc("qt", [128, 4096], BF16)
    kt_ = alloc("kt", [128, 4096], BF16)
    vaug = alloc("vaug", [128, 2 * NT * 192], BF16)
    pbuf = alloc("pbuf", [128, 2048], BF16)
    wpool = alloc("wpool", [128, 14 * 1024], BF16)
    rexp = alloc("rexp", [128, 2048], BF16)
    rrow = alloc("rrow", [128, 1024], BF16)
    sumsT = alloc("sumsT", [128, 512], F32)
    grid2 = alloc("grid2", [128, 512], BF16)
    rtokb = alloc("rtokb", [128, 16], F32)
    xns = alloc("xns", [128, 1024], BF16)
    bn6 = alloc("bn6", [128, 12], F32)
    mv = alloc("mv", [128, 32], F32)
    lnv = alloc("lnv", [128, 16], F32)
    rstd = alloc("rstd", [128, 16], F32)
    bigi = alloc("bigi", [128, 896], BF16)
    rep = alloc("rep", [128, 128], BF16)
    e2 = alloc("e2", [128, 2], BF16)
    onesr = alloc("onesr", [1, 144], BF16)
    rbrow = onesr[0:1, 128:136]
    maskb = alloc("maskb", [128, 16], F32)
    negbb = alloc("negbb", [128, 16], F32)
    bqt = alloc("bqt", [128, 8], F32)
    bkt = alloc("bkt", [128, 8], F32)
    rows = alloc("rows", [128, 1152], BF16)
    borow = rows[0:1, 128:1152]
    bvrow = rows[32:33, 128:1152]
    bbrow = rows[64:65, 128:1152]
    logb = alloc("logb", [128, 8], F32)
    max8 = alloc("max8", [128, 8], F32)
    negm1 = alloc("negm1", [128, 1], F32)
    expw = alloc("expw", [128, 8], F32)
    keep = alloc("keep", [128, 8], F32)
    sumw = alloc("sumw", [128, 1], F32)
    wt8 = alloc("wt8", [128, 32], BF16)
    wtrow = alloc("wtrow", [128, 512], BF16)
    axwS = alloc("axwS", [128, 1024], BF16)
    mq = alloc("mq", [128, 1], F32)
    mq4 = alloc("mq4", [128, 4], F32)
    mk4 = alloc("mk4", [128, 4], F32)
    mk = alloc("mk", [128, 1], F32)
    tprod = alloc("tprod", [128, 1], BF16)
    nbtmp = alloc("nbtmp", [1, 2], F32)
    nb1 = alloc("nb1", [1, 1], F32)
    nbbf = alloc("nbbf", [1, 1], BF16)
    epsb = alloc("epsb", [128, 1], F32)

    psA = ctx.enter_context(nc.psum_tensor("psA", [128, 1024], F32))
    psB = ctx.enter_context(nc.psum_tensor("psB", [128, 1024], F32))
    psC = ctx.enter_context(nc.psum_tensor("psC", [128, 1024], F32))
    psD = ctx.enter_context(nc.psum_tensor("psD", [128, 512], F32))
    psE = ctx.enter_context(nc.psum_tensor("psE", [128, 512], F32))
    bank = {
        "psA0": psA[:, 0:512], "psA1": psA[:, 512:1024],
        "psB0": psB[:, 0:512], "psB1": psB[:, 512:1024],
        "psC0": psC[:, 0:512], "psC1": psC[:, 512:1024],
        "psD": psD[:, :], "psE": psE[:, :],
    }

    eng_sems = {e: ctx.enter_context(nc.semaphore(f"sem_{e}")) for e in Prog.ENGS}
    dma_names = (["consts", "constg", "ffnw", "wo", "rrout", "rrin"]
                 + [f"ws{i}" for i in range(6)]
                 + [f"xi{t}" for t in range(NT)] + [f"xo{t}" for t in range(NT)])
    dma_sems = {n: ctx.enter_context(nc.semaphore(f"dma_{n}")) for n in dma_names}

    P = Prog()

    def d(queue, sem, out_ap, in_ap, reads=(), writes=(), tag=""):
        def fn(q, out_ap=out_ap, in_ap=in_ap, sem=sem):
            return q.dma_start(out=out_ap, in_=in_ap).then_inc(dma_sems[sem], 16)
        return P.dma(queue, sem, fn, reads=reads, writes=writes, tag=tag)

    d("sync", "consts", bigi[:, :], hbigi, writes=["bigi"], tag="c.bigi")
    d("sync", "consts", rep[0:8, :], hrep, writes=["rep"], tag="c.rep")
    d("sync", "consts", maskb[:, :], hmask.to_broadcast([128, 16]), writes=["maskb"], tag="c.mask")
    d("sync", "consts", bqt[:, :], hbqt, writes=["bqt"], tag="c.bqt")
    d("sync", "consts", bkt[:, :], hbkt, writes=["bkt"], tag="c.bkt")
    d("gpsimd", "constg", borow, hbo, writes=["borow"], tag="c.bo")
    d("gpsimd", "constg", bvrow, hbvr, writes=["bvrow"], tag="c.bvr")
    P.settle("constg", ["borow", "bvrow"])
    P.settle("consts", ["bigi", "rep", "maskb", "bqt", "bkt"])

    P.op("vector", lambda q: q.memset(e2[:, :], 0.0), writes=["e2"], tag="ms.e2a")
    P.op("vector", lambda q: q.memset(e2[0:64, 0:1], 1.0), reads=["e2"], writes=["e2"], tag="ms.e2b")
    P.op("vector", lambda q: q.memset(e2[64:128, 1:2], 1.0), reads=["e2"], writes=["e2"], tag="ms.e2c")
    P.op("vector", lambda q: q.memset(onesr[0:1, 0:128], 1.0), writes=["onesr"], tag="ms.ones")
    P.op("vector", lambda q: q.memset(rows[0:1, 0:128], 1.0), writes=["ones0"], tag="ms.r0")
    P.op("vector", lambda q: q.memset(rows[32:33, 0:128], 1.0), writes=["ones32"], tag="ms.r32")
    P.op("vector", lambda q: q.memset(rows[64:65, 0:128], 1.0), writes=["ones64"], tag="ms.r64")
    P.op("vector", lambda q: q.memset(epsb[:, :], 1e-5), writes=["epsb"], tag="ms.eps")
    P.op("vector", lambda q: q.memset(grid2[:, :], 0.0), writes=["grid2a", "grid2b"], tag="ms.g2")
    for par in range(2):
        for t in range(NT):
            b0 = par * NT * 192 + t * 192
            P.op("vector", lambda q, b=b0: q.memset(vaug[:, b + 64:b + 96], 1.0),
                 writes=[f"vaug{par}.{t}"], tag=f"ms.v1.{par}.{t}")
            P.op("vector", lambda q, b=b0: q.memset(vaug[:, b + 96:b + 128], 0.0),
                 reads=[f"vaug{par}.{t}"], writes=[f"vaug{par}.{t}"], tag=f"ms.v0.{par}.{t}")

    def xin_tile(seq, t):
        def fn(q, t=t, seq=seq):
            return q.dma_start(out=x[:, t * 1024:(t + 1) * 1024],
                               in_=hx[seq, t * 128:(t + 1) * 128, :]).then_inc(dma_sems[f"xi{t}"], 16)
        P.dma("sync", f"xi{t}", fn, writes=[f"x.{t}"], tag=f"xin{seq}.{t}")

    def xout_tile(seq, t):
        def fn(q, t=t, seq=seq):
            return q.dma_start(out=hy[seq, t * 128:(t + 1) * 128, :],
                               in_=x[:, t * 1024:(t + 1) * 1024]).then_inc(dma_sems[f"xo{t}"], 16)
        P.dma("sync", f"xo{t}", fn, reads=[f"x.{t}"], tag=f"xout{seq}.{t}")

    xnT3 = xnT[:, :].rearrange("p (c j) -> p c j", c=8)

    from collections import deque

    def ln_tile_stats(tag, t):
        P.op("vector", lambda q, t=t: q.bn_stats(bn6[:, 0:6], x[:, t * 1024:t * 1024 + 512]),
             reads=[f"x.{t}"], writes=["bn6a"], tag=f"{tag}.bns0.{t}")
        P.op("vector", lambda q, t=t: q.bn_stats(bn6[:, 6:12], x[:, t * 1024 + 512:t * 1024 + 1024]),
             reads=[f"x.{t}"], writes=["bn6b"], tag=f"{tag}.bns1.{t}")
        P.op("vector", lambda q, t=t: q.bn_aggr(mv[:, 2 * t:2 * t + 2], bn6[:, 0:12]),
             reads=["bn6a", "bn6b"], writes=[f"mv.{t}"], tag=f"{tag}.bna.{t}")

    def ln_group(tag, g):
        P.op("scalar", lambda q, g=g: q.activation(lnv[:, 4 * g:4 * g + 4],
                                                   mv[:, 8 * g + 1:8 * g + 8:2],
                                                   AF.Ln, bias=epsb[:, 0:1], scale=1.0),
             reads=[f"mv.{t}" for t in range(g * 4, g * 4 + 4)] + ["epsb"],
             writes=[f"lnv.{g}"], tag=f"{tag}.lnv.{g}")
        P.op("scalar", lambda q, g=g: q.activation(rstd[:, 4 * g:4 * g + 4], lnv[:, 4 * g:4 * g + 4],
                                                   AF.Exp, bias=0.0, scale=-0.5),
             reads=[f"lnv.{g}"], writes=[f"rstd.{g}"], tag=f"{tag}.rstd.{g}")
        for t in range(g * 4, g * 4 + 4):
            P.op("vector", lambda q, t=t: q.tensor_scalar(
                xns[:, :], x[:, t * 1024:(t + 1) * 1024],
                mv[:, 2 * t:2 * t + 1], rstd[:, t:t + 1], OP.subtract, OP.mult),
                reads=[f"x.{t}", f"mv.{t}", f"rstd.{g}"], writes=["xns"],
                tag=f"{tag}.norm.{t}")
            for half in range(2):
                bk = ("psD", "psE")[(t * 2 + half) % 2]
                for c in range(4):
                    cc = half * 4 + c
                    P.op("tensor", lambda q, cc=cc, c=c, bk=bk: q.matmul(
                        bank[bk], xns[:, cc * 128:(cc + 1) * 128],
                        bigi[:, 384 - c * 128:896 - c * 128],
                        start=(c == 0), stop=(c == 3)),
                        reads=["xns", "bigi", bk], writes=[bk], tag=f"{tag}.T.{t}.{cc}")
                P.op("vector", lambda q, t=t, half=half, bk=bk: q.tensor_copy(
                    xnT3[:, 4 * half:4 * half + 4, t * 128:(t + 1) * 128], bank[bk]),
                    reads=[bk], writes=[f"xnT.w{t // 4}"], tag=f"{tag}.Tev.{t}.{half}")

    def ln_xnT(tag):
        for g in range(4):
            for t in range(g * 4, g * 4 + 4):
                ln_tile_stats(tag, t)
            ln_group(tag, g)

    def make_ln_hook(tag):
        def hook(t):
            ln_tile_stats(tag, t)
        return hook

    def attn_load(kind, tag):
        d("gpsimd", "wo", wpool[:, 6 * 1024:14 * 1024], hwo[kind],
          writes=[f"wp{i}" for i in range(6, 14)], tag=f"{tag}.dwo")
        for hg01 in (0, 1):
            s0 = (hg01 % 2) * 3
            for j, (hn, hsrc) in enumerate((("wq", hwq), ("wk", hwk), ("wv", hwv))):
                d("gpsimd", f"ws{s0 + j}", wpool[:, (s0 + j) * 1024:(s0 + j + 1) * 1024],
                  hsrc[kind][hg01], writes=[f"wp{s0 + j}"], tag=f"{tag}.dw{hn}.{hg01}")

    def attn(seq, kind, tag, hook=None, next_loader=None, preloaded=False, inline_ln_tag=None, ln_rest_tag=None):
        lcol = 0 if kind == "a" else 8
        fillers = deque()

        def pump(n):
            while n > 0 and fillers:
                f = fillers.popleft()
                if f is None:
                    continue
                f()
                n -= 1

        def flush():
            while fillers:
                f = fillers.popleft()
                if f is not None:
                    f()

        def wslices(hg):
            s0 = (hg % 2) * 3
            for j, (hn, hsrc) in enumerate((("wq", hwq), ("wk", hwk), ("wv", hwv))):
                d("gpsimd", f"ws{s0 + j}", wpool[:, (s0 + j) * 1024:(s0 + j + 1) * 1024],
                  hsrc[kind][hg], writes=[f"wp{s0 + j}"], tag=f"{tag}.dw{hn}.{hg}")

        def emit_qkv(hg, into):
            s0 = (hg % 2) * 3
            par = hg % 2
            wq_ap = wpool[:, s0 * 1024:(s0 + 1) * 1024]
            wk_ap = wpool[:, (s0 + 1) * 1024:(s0 + 2) * 1024]
            wv_ap = wpool[:, (s0 + 2) * 1024:(s0 + 3) * 1024]
            wqk, wkk, wvk = f"wp{s0}", f"wp{s0 + 1}", f"wp{s0 + 2}"
            for g in range(4):
                for mi, (wap, wkey, dstb, dkey, bt, bkey) in enumerate((
                        (wq_ap, wqk, qt, "qt", bqt if kind == "a" else None, "bqt"),
                        (wk_ap, wkk, kt_, "kt", bkt if kind == "a" else None, "bkt"))):
                    bk = ("psD", "psE")[(2 * g + mi) % 2]
                    for c in range(8):
                        into.append(lambda wap=wap, c=c, g=g, bk=bk, wkey=wkey: P.op(
                            "tensor", lambda q: q.matmul(
                                bank[bk], wap[:, c * 128:(c + 1) * 128],
                                xnT[:, c * 2048 + g * 512:c * 2048 + (g + 1) * 512],
                                start=(c == 0), stop=(c == 7)),
                            reads=[wkey, f"xnT.w{g}", bk], writes=[bk],
                            tag=f"{tag}.qk.{hg}.{g}.{c}"))
                    if bt is not None:
                        into.append(lambda dstb=dstb, g=g, bk=bk, bt=bt, hg=hg, par=par,
                                    dkey=dkey, bkey=bkey: P.op(
                            "vector", lambda q: q.tensor_scalar(
                                dstb[:, par * 2048 + g * 512:par * 2048 + (g + 1) * 512],
                                bank[bk], bt[:, hg:hg + 1], None, OP.add),
                            reads=[bk, bkey], writes=[f"{dkey}{par}.{g}"],
                            tag=f"{tag}.qkev.{hg}.{g}"))
                    else:
                        into.append(lambda dstb=dstb, g=g, bk=bk, par=par, dkey=dkey: P.op(
                            "vector", lambda q: q.tensor_copy(
                                dstb[:, par * 2048 + g * 512:par * 2048 + (g + 1) * 512],
                                bank[bk]),
                            reads=[bk], writes=[f"{dkey}{par}.{g}"],
                            tag=f"{tag}.qkev.{hg}.{g}"))
                if seq == 0:
                    into.append(lambda par=par, g=g: P.op(
                        "vector", lambda q: q.tensor_reduce(
                            mq4[:, g:g + 1], qt[:, par * 2048 + g * 512:par * 2048 + (g + 1) * 512],
                            AX, OP.max, apply_absolute_value=True),
                        reads=[f"qt{par}.{g}"], writes=[f"mq4.{g}"], tag=f"{tag}.mqp.{hg}.{g}"))
                    into.append(lambda par=par, g=g: P.op(
                        "vector", lambda q: q.tensor_reduce(
                            mk4[:, g:g + 1], kt_[:, par * 2048 + g * 512:par * 2048 + (g + 1) * 512],
                            AX, OP.max, apply_absolute_value=True),
                        reads=[f"kt{par}.{g}"], writes=[f"mk4.{g}"], tag=f"{tag}.mkp.{hg}.{g}"))
                for t in range(g * 4, g * 4 + 4):
                    bk = ("psD", "psE")[t % 2]
                    for c in range(8):
                        into.append(lambda t=t, c=c, wv_ap=wv_ap, bk=bk, wvk=wvk, kind=kind: P.op(
                            "tensor", lambda q: q.matmul(
                                bank[bk][:, 0:128], xnT[:, c * 2048 + t * 128:c * 2048 + (t + 1) * 128],
                                wv_ap[:, c * 128:(c + 1) * 128],
                                start=(c == 0), stop=(c == 7 and kind != "a")),
                            reads=[wvk, f"xnT.w{t // 4}", bk], writes=[bk],
                            tag=f"{tag}.v.{hg}.{t}.{c}"))
                    if kind == "a":
                        into.append(lambda hg=hg, bk=bk, t=t: P.op(
                            "tensor", lambda q: q.matmul(
                                bank[bk][:, 0:128], rows[32:33, 0:128],
                                rows[32:33, 128 + hg * 128:128 + (hg + 1) * 128],
                                start=False, stop=True),
                            reads=["ones32", "bvrow", bk], writes=[bk], tag=f"{tag}.vb.{hg}.{t}"))

                    def vev_mk(t=t, bk=bk, par=par):
                        def vev(q):
                            dst = bass.AP(tensor=vaug[:, 0:1].tensor, offset=par * NT * 192 + t * 192,
                                          ap=[[2 * NT * 192, 128], [128, 2], [1, 64]])
                            return q.tensor_copy(dst, bank[bk][:, 0:128])
                        return vev
                    into.append(lambda t=t, bk=bk, par=par: P.op(
                        "vector", vev_mk(t, bk, par), reads=[bk], writes=[f"vaug{par}.{t}"],
                        tag=f"{tag}.vev.{hg}.{t}"))
                into.append(None)
            if seq == 0:
                into.append(lambda: P.op(
                    "vector", lambda q: q.reduce_max(mq[:, :], mq4[:, :], AX),
                    reads=[f"mq4.{g}" for g in range(4)], writes=["mq"], tag=f"{tag}.mq.{hg}"))
                into.append(lambda: P.op(
                    "vector", lambda q: q.reduce_max(mk[:, :], mk4[:, :], AX),
                    reads=[f"mk4.{g}" for g in range(4)], writes=["mk"], tag=f"{tag}.mk.{hg}"))
                into.append(lambda: P.op(
                    "vector", lambda q: q.tensor_mul(tprod[:, :], mq[:, :], mk[:, :]),
                    reads=["mq", "mk"], writes=["tprod"], tag=f"{tag}.tp.{hg}"))
                into.append(lambda: P.op(
                    "tensor", lambda q: q.matmul(psE[0:1, 0:2], tprod[:, :], e2[:, :],
                                                 start=True, stop=True),
                    reads=["tprod", "e2", "psE"], writes=["psE"], tag=f"{tag}.bsum.{hg}"))
                into.append(lambda: P.op(
                    "vector", lambda q: q.tensor_scalar(nbtmp[0:1, :], psE[0:1, 0:2],
                                                        480.0, None, OP.min),
                    reads=["psE"], writes=["nbtmp"], tag=f"{tag}.bclamp.{hg}"))
                into.append(lambda: P.op(
                    "vector", lambda q: q.reduce_max(nb1[0:1, :], nbtmp[0:1, :], AX),
                    reads=["nbtmp"], writes=["nb1"], tag=f"{tag}.bmax.{hg}"))
                into.append(lambda: P.op(
                    "vector", lambda q: q.tensor_copy(nbbf[0:1, :], nb1[0:1, :]),
                    reads=["nb1"], writes=["nbbf"], tag=f"{tag}.bbf.{hg}"))
                into.append(lambda: P.op(
                    "tensor", lambda q: q.matmul(psE[:, 8:9], onesr[0:1, 0:128], nbbf[0:1, :],
                                                 start=True, stop=True),
                    reads=["nbbf", "onesr", "psE"], writes=["psE"], tag=f"{tag}.bbc.{hg}"))
                into.append(lambda hg=hg: P.op(
                    "vector", lambda q: q.tensor_scalar(
                        negbb[:, lcol + hg:lcol + hg + 1], psE[:, 8:9], -0.125, None, OP.mult),
                    reads=["psE"], writes=[f"negbb.{lcol + hg}"], tag=f"{tag}.bneg.{hg}"))

        def emit_wo_pass(into, cs, with_bias, passtag):
            c0, c1 = cs
            for t in range(NT):
                for f in range(2):
                    bk = ("psD", "psE")[(t * 2 + f) % 2]
                    for c in range(c0, c1):
                        laststop = (c == c1 - 1 and not with_bias)
                        into.append(lambda t=t, f=f, c=c, bk=bk, laststop=laststop, c0=c0: P.op(
                            "tensor", lambda q: q.matmul(
                                bank[bk], aoT[:, c * 2048 + t * 128:c * 2048 + (t + 1) * 128],
                                wpool[:, 6 * 1024 + c * 1024 + f * 512:
                                       6 * 1024 + c * 1024 + (f + 1) * 512],
                                start=(c == c0), stop=laststop),
                            reads=[f"aoT.{c}", f"wp{6 + c}", bk], writes=[bk],
                            tag=f"{tag}.{passtag}.{t}.{f}.{c}"))
                    if with_bias:
                        into.append(lambda f=f, bk=bk, t=t: P.op(
                            "tensor", lambda q: q.matmul(
                                bank[bk], rows[0:1, 0:128], rows[0:1, 128 + f * 512:128 + (f + 1) * 512],
                                start=False, stop=True),
                            reads=["ones0", "borow", bk], writes=[bk],
                            tag=f"{tag}.{passtag}b.{t}.{f}"))
                    into.append(lambda t=t, f=f, bk=bk: P.op(
                        "vector", lambda q: q.tensor_tensor(
                            x[:, t * 1024 + f * 512:t * 1024 + (f + 1) * 512],
                            x[:, t * 1024 + f * 512:t * 1024 + (f + 1) * 512],
                            bank[bk], OP.add),
                        reads=[bk, f"x.{t}"], writes=[f"x.{t}"],
                        tag=f"{tag}.{passtag}add.{t}.{f}"))

        def emit_scores(hg, mid=()):
            par = hg % 2
            qo = par * 2048
            for qg in range(NQG):
                if qg == 2:
                    for m in mid:
                        m()
                def emit_st(kt):
                    kpar = kt % 2
                    ps = psA if kpar == 0 else psB
                    sk0, sk1 = (("psA0", "psA1") if kpar == 0 else ("psB0", "psB1"))
                    P.op("tensor", lambda q, kt=kt, qg=qg, ps=ps, qo=qo: q.matmul(
                        ps[:, 0:512], kt_[0:64, qo + kt * 128:qo + (kt + 1) * 128],
                        qt[0:64, qo + qg * 512:qo + (qg + 1) * 512],
                        start=True, stop=True, tile_position=(0, 0)),
                        reads=[f"kt{par}.{kt // 4}", f"qt{par}.{qg}", sk0], writes=[sk0],
                        tag=f"{tag}.s0.{hg}.{qg}.{kt}")
                    P.op("tensor", lambda q, kt=kt, qg=qg, ps=ps, qo=qo: q.matmul(
                        ps[:, 512:1024], kt_[64:128, qo + kt * 128:qo + (kt + 1) * 128],
                        qt[64:128, qo + qg * 512:qo + (qg + 1) * 512],
                        start=True, stop=True, tile_position=(64, 0)),
                        reads=[f"kt{par}.{kt // 4}", f"qt{par}.{qg}", sk1], writes=[sk1],
                        tag=f"{tag}.s1.{hg}.{qg}.{kt}")
                    P.op("scalar", lambda q, ps=ps, kpar=kpar, hg=hg: q.activation(
                        pbuf[:, kpar * 1024:(kpar + 1) * 1024], ps[:, 0:1024], AF.Exp,
                        bias=negbb[:, lcol + hg:lcol + hg + 1], scale=0.125),
                        reads=[sk0, sk1, f"negbb.{lcol + hg}"], writes=[f"pbuf.{kpar}"],
                        tag=f"{tag}.exp.{hg}.{qg}.{kt}")

                def emit_pv(kt):
                    kpar = kt % 2
                    P.op("tensor", lambda q, kt=kt, kpar=kpar, par=par: q.matmul(
                        psC[0:128, 0:512],
                        vaug[:, par * NT * 192 + kt * 192:par * NT * 192 + kt * 192 + 128],
                        pbuf[:, kpar * 1024:kpar * 1024 + 512],
                        start=(kt == 0), stop=(kt == NKT - 1)),
                        reads=[f"pbuf.{kpar}", f"vaug{par}.{kt}", "psC0"], writes=["psC0"],
                        tag=f"{tag}.pv0.{hg}.{qg}.{kt}")
                    P.op("tensor", lambda q, kt=kt, kpar=kpar, par=par: q.matmul(
                        psC[0:128, 512:1024],
                        vaug[:, par * NT * 192 + kt * 192 + 64:par * NT * 192 + (kt + 1) * 192],
                        pbuf[:, kpar * 1024 + 512:kpar * 1024 + 1024],
                        start=(kt == 0), stop=(kt == NKT - 1)),
                        reads=[f"pbuf.{kpar}", f"vaug{par}.{kt}", "psC1"], writes=["psC1"],
                        tag=f"{tag}.pv1.{hg}.{qg}.{kt}")

                for kt in range(NKT + 1):
                    if kt < NKT:
                        emit_st(kt)
                    if kt > 0:
                        emit_pv(kt - 1)
                    pump(3 if kt % 4 == 3 else 4)
                P.op("scalar", lambda q, hg=hg, qg=qg: q.copy(
                    aoT[0:64, hg * 2048 + qg * 512:hg * 2048 + (qg + 1) * 512], psC[0:64, 0:512]),
                    reads=["psC0"], writes=[f"aoT.{hg}"], tag=f"{tag}.aev0.{hg}.{qg}")
                P.op("scalar", lambda q, hg=hg, qg=qg: q.copy(
                    aoT[64:128, hg * 2048 + qg * 512:hg * 2048 + (qg + 1) * 512],
                    psC[64:128, 512:1024]),
                    reads=["psC1"], writes=[f"aoT.{hg}"], tag=f"{tag}.aev1.{hg}.{qg}")
                P.op("vector", lambda q: q.transpose(sumsT[64:96, 0:512], psC[64:96, 0:512]),
                     reads=["psC0"], writes=["sumsT0"], tag=f"{tag}.st0.{hg}.{qg}")
                P.op("vector", lambda q: q.transpose(sumsT[0:32, 0:512], psC[0:32, 512:1024]),
                     reads=["psC1"], writes=["sumsT1"], tag=f"{tag}.st1.{hg}.{qg}")
                P.op("vector", lambda q: q.reciprocal(rtokb[64:96, 0:16], sumsT[64:96, 0:512:32]),
                     reads=["sumsT0"], writes=["rtok0"], tag=f"{tag}.rc0.{hg}.{qg}")
                P.op("vector", lambda q: q.reciprocal(rtokb[0:32, 0:16], sumsT[0:32, 0:512:32]),
                     reads=["sumsT1"], writes=["rtok1"], tag=f"{tag}.rc1.{hg}.{qg}")
                if kind == "a":
                    P.op("vector", lambda q, hg=hg: q.tensor_scalar(
                        rtokb[64:96, 0:16], rtokb[64:96, 0:16],
                        maskb[64:96, 2 * hg:2 * hg + 1], None, OP.mult),
                        reads=["rtok0", "maskb"], writes=["rtok0"], tag=f"{tag}.msk0.{hg}.{qg}")
                    P.op("vector", lambda q, hg=hg: q.tensor_scalar(
                        rtokb[0:32, 0:16], rtokb[0:32, 0:16],
                        maskb[0:32, 2 * hg + 1:2 * hg + 2], None, OP.mult),
                        reads=["rtok1", "maskb"], writes=["rtok1"], tag=f"{tag}.msk1.{hg}.{qg}")
                P.op("vector", lambda q: q.tensor_copy(grid2[64:96, 0:512:32], rtokb[64:96, 0:16]),
                     reads=["rtok0"], writes=["grid2a"], tag=f"{tag}.g2a.{hg}.{qg}")
                P.op("vector", lambda q: q.tensor_copy(grid2[0:32, 0:512:32], rtokb[0:32, 0:16]),
                     reads=["rtok1"], writes=["grid2b"], tag=f"{tag}.g2b.{hg}.{qg}")
                P.op("vector", lambda q, qg=qg: q.transpose(
                    rrow[64:96, (qg % 2) * 512:(qg % 2 + 1) * 512], grid2[64:96, 0:512]),
                    reads=["grid2a"], writes=["rrow"], tag=f"{tag}.g3a.{hg}.{qg}")
                P.op("vector", lambda q, qg=qg: q.transpose(
                    rrow[0:32, (qg % 2) * 512:(qg % 2 + 1) * 512], grid2[0:32, 0:512]),
                    reads=["grid2b"], writes=["rrow"], tag=f"{tag}.g3b.{hg}.{qg}")
                pump(4)
                if qg % 2 == 1 and not (hg == 7 and qg == 3):
                    hh = qg // 2
                    slot = hg % 2
                    d("sync", "rrout", hrr[slot][0:1, hh * 1024:(hh + 1) * 1024],
                      rrow[64:65, 0:1024],
                      reads=["rrow"], writes=[f"hrr{slot}"], tag=f"{tag}.rro0.{hg}.{hh}")
                    d("sync", "rrout", hrr[slot][1:2, hh * 1024:(hh + 1) * 1024],
                      rrow[0:1, 0:1024],
                      reads=["rrow"], writes=[f"hrr{slot}"], tag=f"{tag}.rro1.{hg}.{hh}")
                if hg == 7 and qg == 2:
                    slot = hg % 2
                    d("sync", "rrin", rexp[0:64, 0:1024],
                      hrr[slot][0:1, 0:1024].to_broadcast([64, 1024]),
                      reads=[f"hrr{slot}"], writes=["rexpb"], tag=f"{tag}.rri0h.{hg}")
                    d("sync", "rrin", rexp[64:128, 0:1024],
                      hrr[slot][1:2, 0:1024].to_broadcast([64, 1024]),
                      reads=[f"hrr{slot}"], writes=["rexpb"], tag=f"{tag}.rri1h.{hg}")
            slot = hg % 2
            if hg < 7:
                d("sync", "rrin", rexp[0:64, :], hrr[slot][0:1, :].to_broadcast([64, 2048]),
                  reads=[f"hrr{slot}"], writes=["rexpb"], tag=f"{tag}.rri0.{hg}")
                d("sync", "rrin", rexp[64:128, :], hrr[slot][1:2, :].to_broadcast([64, 2048]),
                  reads=[f"hrr{slot}"], writes=["rexpb"], tag=f"{tag}.rri1.{hg}")
            else:
                for qg2 in (2, 3):
                    sl = qg2 % 2
                    ps = (psA, psB)[qg2 - 2]
                    pk0, pk1 = (("psA0", "psA1"), ("psB0", "psB1"))[qg2 - 2]
                    P.op("tensor", lambda q, ps=ps, sl=sl: q.matmul(
                        ps[0:64, 0:512], rows[64:65, 0:64], rrow[64:65, sl * 512:(sl + 1) * 512],
                        start=True, stop=True),
                        reads=["rrow", "ones64", pk0], writes=[pk0], tag=f"{tag}.rbcA.{qg2}")
                    P.op("tensor", lambda q, ps=ps, sl=sl: q.matmul(
                        ps[64:128, 512:1024], onesr[0:1, 0:64], rrow[0:1, sl * 512:(sl + 1) * 512],
                        start=True, stop=True, tile_position=(0, 64)),
                        reads=["rrow", "onesr", pk1], writes=[pk1], tag=f"{tag}.rbcB.{qg2}")
                    P.op("vector", lambda q, ps=ps, qg2=qg2: q.tensor_copy(
                        rexp[0:64, qg2 * 512:(qg2 + 1) * 512], ps[0:64, 0:512]),
                        reads=[pk0], writes=["rexpb"], tag=f"{tag}.rbevA.{qg2}")
                    P.op("vector", lambda q, ps=ps, qg2=qg2: q.tensor_copy(
                        rexp[64:128, qg2 * 512:(qg2 + 1) * 512], ps[64:128, 512:1024]),
                        reads=[pk1], writes=["rexpb"], tag=f"{tag}.rbevB.{qg2}")

            def nrm(hg=hg):
                P.op("vector", lambda q, hg=hg: q.tensor_tensor(
                    aoT[:, hg * 2048:(hg + 1) * 2048], aoT[:, hg * 2048:(hg + 1) * 2048],
                    rexp[:, :], OP.mult),
                    reads=[f"aoT.{hg}", "rexpb"], writes=[f"aoT.{hg}"], tag=f"{tag}.nrm.{hg}")
            return nrm

        if not preloaded:
            attn_load(kind, tag)
        direct = deque()
        emit_qkv(0, direct)
        for g in range(4):
            if inline_ln_tag is not None:
                for t in range(g * 4, g * 4 + 4):
                    ln_tile_stats(inline_ln_tag, t)
            ln_group(ln_rest_tag, g)
            while direct:
                f = direct.popleft()
                if f is None:
                    break
                f()
        while direct:
            f = direct.popleft()
            if f is not None:
                f()
        pending_nrm = None
        for hg in range(NHG):
            if hg + 2 < NHG:
                wslices(hg + 2)
            if hg + 1 < NHG:
                emit_qkv(hg + 1, fillers)
            else:
                emit_wo_pass(fillers, (0, 6), False, "wo1")
            pending_nrm = emit_scores(hg, mid=((pending_nrm,) if pending_nrm else ()))
            flush()
        pending_nrm()
        if next_loader is not None:
            next_loader()
        tail = deque()
        emit_wo_pass(tail, (6, 8), kind == "a", "wo2")
        per_tile = len(tail) // NT
        for t in range(NT):
            for _ in range(per_tile):
                tail.popleft()()
            if hook is not None:
                hook(t)
        while tail:
            tail.popleft()()

    def ffn_load(tag):
        d("gpsimd", "ffnw", wpool[:, 8192:9216], hast, writes=["wp8"], tag=f"{tag}.dast")
        d("gpsimd", "ffnw", wpool[:, 9216:10240], hbst, writes=["wp9"], tag=f"{tag}.dbst")
        d("gpsimd", "ffnw", wpool[:, 10240:10304], hrouter, writes=["wp10"], tag=f"{tag}.drt")
        d("gpsimd", "ffnw", bbrow, hbb, writes=["bbrow"], tag=f"{tag}.dbb")
        d("gpsimd", "ffnw", rbrow, hrb, writes=["rbrow"], tag=f"{tag}.drb")
        d("gpsimd", "ffnw", wpool[:, 0:4096], hbase[:, 0:4096],
          writes=[f"wp{i}" for i in range(4)], tag=f"{tag}.dbase0")
        d("gpsimd", "ffnw", wpool[:, 4096:8192], hbase[:, 4096:8192],
          writes=[f"wp{i}" for i in range(4, 8)], tag=f"{tag}.dbase1")
        P.settle("ffnw", [f"wp{i}" for i in range(8)] + ["wp8", "wp9", "wp10", "bbrow", "rbrow"])

    def ffn(tag, hook=None, preloaded=False, next_loader=None, ln_rest_tag=None):
        if not preloaded:
            ffn_load(tag)

        def producer(g):
            pk = f"psC{g % 2}"
            pap = psC[:, 0:512] if g % 2 == 0 else psC[:, 512:1024]
            for c in range(8):
                P.op("tensor", lambda q, c=c, g=g, pap=pap: q.matmul(
                    pap, wpool[:, 8192 + c * 128:8192 + (c + 1) * 128],
                    xnT[:, c * 2048 + g * 512:c * 2048 + (g + 1) * 512],
                    start=(c == 0), stop=(c == 7)),
                    reads=["wp8", f"xnT.w{g}", pk], writes=[pk], tag=f"{tag}.ax.{g}.{c}")
            for t in range(g * 4, g * 4 + 4):
                bk = ("psD", "psE")[t % 2]
                for c in range(8):
                    P.op("tensor", lambda q, t=t, c=c, bk=bk: q.matmul(
                        bank[bk][:, 0:8], xnT[:, c * 2048 + t * 128:c * 2048 + (t + 1) * 128],
                        wpool[:, 10240 + c * 8:10240 + (c + 1) * 8],
                        start=(c == 0), stop=False),
                        reads=["wp10", f"xnT.w{g}", bk], writes=[bk], tag=f"{tag}.lg.{t}.{c}")
                P.op("tensor", lambda q, bk=bk: q.matmul(
                    bank[bk][:, 0:8], onesr[0:1, 0:128], onesr[0:1, 128:136],
                    start=False, stop=True),
                    reads=["onesr", "rbrow", bk], writes=[bk], tag=f"{tag}.lgb.{t}")
                P.op("vector", lambda q, bk=bk: q.tensor_copy(logb[:, :], bank[bk][:, 0:8]),
                     reads=[bk], writes=["logb"], tag=f"{tag}.lev.{t}")
                P.op("vector", lambda q: q.max(out=max8[:, :], in_=logb[:, :]),
                     reads=["logb"], writes=["max8"], tag=f"{tag}.max8.{t}")
                P.op("vector", lambda q: q.tensor_scalar(keep[:, :], logb[:, :], max8[:, 1:2],
                                                         None, OP.is_ge),
                     reads=["logb", "max8"], writes=["keep"], tag=f"{tag}.keep.{t}")
                P.op("vector", lambda q: q.tensor_scalar(negm1[:, :], max8[:, 0:1], -1.0,
                                                         None, OP.mult),
                     reads=["max8"], writes=["negm1"], tag=f"{tag}.nm1.{t}")
                P.op("scalar", lambda q: q.activation(expw[:, :], logb[:, :], AF.Exp,
                                                      bias=negm1[:, 0:1], scale=1.0),
                     reads=["logb", "negm1"], writes=["expw"], tag=f"{tag}.rexp.{t}")
                P.op("vector", lambda q: q.tensor_mul(expw[:, :], expw[:, :], keep[:, :]),
                     reads=["expw", "keep"], writes=["expw"], tag=f"{tag}.emask.{t}")
                P.op("vector", lambda q: q.reduce_sum(sumw[:, :], expw[:, :], AX),
                     reads=["expw"], writes=["sumw"], tag=f"{tag}.esum.{t}")
                P.op("vector", lambda q: q.reciprocal(sumw[:, :], sumw[:, :]),
                     reads=["sumw"], writes=["sumw"], tag=f"{tag}.erec.{t}")
                P.op("vector", lambda q, t=t: q.tensor_scalar(
                    wt8[:, (t % 4) * 8:(t % 4) * 8 + 8], expw[:, :], sumw[:, 0:1],
                    2.0, OP.mult, OP.mult),
                    reads=["expw", "sumw"], writes=[f"wt8.{t % 4}"], tag=f"{tag}.wt8.{t}")

        def producer_b(g):
            pk = f"psC{g % 2}"
            pap = psC[:, 0:512] if g % 2 == 0 else psC[:, 512:1024]
            for tt in range(4):
                P.op("tensor", lambda q, tt=tt: q.matmul(
                    psB[0:8, 0:128], wt8[:, tt * 8:(tt + 1) * 8], bigi[:, 384:512],
                    start=True, stop=True),
                    reads=[f"wt8.{tt}", "bigi", "psB0"], writes=["psB0"], tag=f"{tag}.wtT.{g}.{tt}")
                P.op("vector", lambda q, tt=tt: q.tensor_copy(
                    wtrow[0:8, tt * 128:(tt + 1) * 128], psB[0:8, 0:128]),
                    reads=["psB0"], writes=["wtrow"], tag=f"{tag}.wtev.{g}.{tt}")
            P.op("tensor", lambda q: q.matmul(psB[:, 512:1024], rep[0:8, :], wtrow[0:8, 0:512],
                                              start=True, stop=True),
                 reads=["rep", "wtrow", "psB1"], writes=["psB1"], tag=f"{tag}.repl.{g}")
            P.op("vector", lambda q: q.tensor_copy(wtrow[:, :], psB[:, 512:1024]),
                 reads=["psB1"], writes=["wexpS"], tag=f"{tag}.wexpev.{g}")
            P.op("vector", lambda q, g=g, pap=pap, pk=pk: q.tensor_tensor(
                axwS[:, (g % 2) * 512:(g % 2 + 1) * 512], pap, wtrow[:, :], OP.mult),
                reads=[pk, "wexpS"], writes=[f"axwS{g % 2}"], tag=f"{tag}.axw.{g}")

        def consumer(g):
            for t in range(g * 4, g * 4 + 4):
                for f in range(2):
                    bk = ("psA0", "psA1")[(t * 2 + f) % 2]
                    for c in range(8):
                        P.op("tensor", lambda q, t=t, f=f, c=c, bk=bk: q.matmul(
                            bank[bk], xnT[:, c * 2048 + t * 128:c * 2048 + (t + 1) * 128],
                            wpool[:, c * 1024 + f * 512:c * 1024 + (f + 1) * 512],
                            start=(c == 0), stop=False),
                            reads=[f"wp{c}", f"xnT.w{g}", bk], writes=[bk],
                            tag=f"{tag}.bs.{t}.{f}.{c}")
                    P.op("tensor", lambda q, t=t, f=f, g=g, bk=bk: q.matmul(
                        bank[bk], axwS[:, (g % 2) * 512 + (t % 4) * 128:(g % 2) * 512 + (t % 4 + 1) * 128],
                        wpool[:, 9216 + f * 512:9216 + (f + 1) * 512],
                        start=False, stop=False),
                        reads=[f"axwS{g % 2}", "wp9", bk], writes=[bk], tag=f"{tag}.cmb.{t}.{f}")
                    P.op("tensor", lambda q, f=f, bk=bk: q.matmul(
                        bank[bk], rows[64:65, 0:128], rows[64:65, 128 + f * 512:128 + (f + 1) * 512],
                        start=False, stop=True),
                        reads=["ones64", "bbrow", bk], writes=[bk], tag=f"{tag}.bsb.{t}.{f}")
                    P.op("vector", lambda q, t=t, f=f, bk=bk: q.tensor_tensor(
                        x[:, t * 1024 + f * 512:t * 1024 + (f + 1) * 512],
                        x[:, t * 1024 + f * 512:t * 1024 + (f + 1) * 512], bank[bk], OP.add),
                        reads=[bk, f"x.{t}"], writes=[f"x.{t}"], tag=f"{tag}.xadd.{t}.{f}")

        def hooks(g):
            if hook is not None:
                for t in range(g * 4, g * 4 + 4):
                    hook(t)

        for g in range(5):
            if g < 4:
                ln_group(ln_rest_tag, g)
                producer(g)
            if g > 0:
                consumer(g - 1)
            if g < 4:
                producer_b(g)
            if g > 0:
                hooks(g - 1)
        if next_loader is not None:
            next_loader()

    subs = [("attn", "a"), ("ffn", None), ("attn", "b"), ("ffn", None)][:n_sub]

    for t in range(NT):
        xin_tile(0, t)

    def make_last_hook(seq):
        def hook(t):
            xout_tile(seq, t)
            if seq == 0:
                xin_tile(1, t)
                ln_tile_stats("s1.0.ln", t)
        return hook

    for seq in range(2):
        for si, (sk, kind) in enumerate(subs):
            tg = f"s{seq}.{si}"

            if si + 1 < len(subs):
                hook = make_ln_hook(f"s{seq}.{si + 1}.ln")
            else:
                hook = make_last_hook(seq)
            nxt = subs[si + 1] if si + 1 < len(subs) else (subs[0] if seq == 0 else None)
            nxt_tag = f"s{seq}.{si + 1}" if si + 1 < len(subs) else f"s{seq + 1}.0"
            if nxt is None:
                nl = None
            elif nxt[0] == "ffn":
                nl = (lambda tgn=nxt_tag: ffn_load(tgn))
            else:
                nl = (lambda kk=nxt[1], tgn=nxt_tag: attn_load(kk, tgn))
            if sk == "attn":
                attn(seq, kind, tg, hook=hook, next_loader=nl,
                     preloaded=not (si == 0 and seq == 0),
                     inline_ln_tag=(tg + ".ln") if (si == 0 and seq == 0) else None,
                     ln_rest_tag=tg + ".ln")
            else:
                ffn(tg, hook=hook, next_loader=nl, preloaded=True, ln_rest_tag=tg + ".ln")

    P.finalize()

    blk = ctx.enter_context(nc.Block())

    def closing(q):
        for t in range(NT):
            q.wait_ge(dma_sems[f"xo{t}"], P.dma_cum[f"xo{t}"])

    P.emit(blk, eng_sems, dma_sems, post={"sync": closing})
    return nc, ctx



def kernel(**inp):
    n_sub = _N_SUB
    xfull = np.ascontiguousarray(np.asarray(inp["L0_attn_a_x"], dtype=np.float32))

    def f32(k):
        return np.asarray(inp[k], dtype=np.float32)

    WqT = _hg_stream(np.ascontiguousarray(f32("L0_attn_a_Wq").T))
    WkT = _hg_stream(np.ascontiguousarray(f32("L0_attn_a_Wk").T))
    WvT = _hg_stream(np.ascontiguousarray(f32("L0_attn_a_Wv").T))
    WoT = _tile2(np.ascontiguousarray(f32("L0_attn_a_Wo").T))
    WqB = _hg_stream(np.ascontiguousarray(f32("L0_attn_b_W_query").transpose(1, 0, 2).reshape(D, D)))
    WkB = _hg_stream(np.ascontiguousarray(f32("L0_attn_b_W_key").transpose(1, 0, 2).reshape(D, D)))
    WvB = _hg_stream(np.ascontiguousarray(f32("L0_attn_b_W_val").transpose(1, 0, 2).reshape(D, D)))
    WoB = _tile2(np.ascontiguousarray(f32("L0_attn_b_W_out").reshape(D, D)))
    baseT = _tile2(np.ascontiguousarray(f32("L0_ffn_base_w").T))
    astT = _tile2(np.ascontiguousarray(f32("L0_ffn_lora_A").transpose(1, 0, 2).reshape(D, 128)))
    bstT = _tile2(np.ascontiguousarray(f32("L0_ffn_lora_B").reshape(128, D)))
    routerT = _tile2(np.ascontiguousarray(f32("L0_ffn_router_w").T))

    bigi = np.zeros((128, 896), dtype=ml_dtypes.bfloat16)
    for p in range(128):
        bigi[p, 384 + p] = 1
    repm = np.zeros((8, 128), dtype=ml_dtypes.bfloat16)
    for e in range(8):
        repm[e, e * 16:(e + 1) * 16] = 1

    common = {
        "wq_a": WqT, "wk_a": WkT, "wv_a": WvT, "wo_a": WoT,
        "wq_b": WqB, "wk_b": WkB, "wv_b": WvB, "wo_b": WoB,
        "base": baseT, "ast": astT, "bst": bstT, "router": routerT,
        "base_b": f32("L0_ffn_base_b").reshape(1, D),
        "router_b": f32("L0_ffn_router_b").reshape(1, 8),
        "bo": f32("L0_attn_a_bo").reshape(1, D),
        "bvrow": f32("L0_attn_a_bv").reshape(1, D),
        "bqt": _bias_tile(f32("L0_attn_a_bq")),
        "bkt": _bias_tile(f32("L0_attn_a_bk")),
        "bvt": _bias_tile(f32("L0_attn_a_bv")),
        "head_mask": f32("L0_attn_a_head_mask").reshape(1, 16),
        "bigi": bigi, "repmat": repm,
    }
    common = {k: np.ascontiguousarray(v) for k, v in common.items()}

    in_maps = []
    for c in range(NCORES):
        m = dict(common)
        m["x"] = np.ascontiguousarray(xfull[2 * c:2 * c + 2])
        in_maps.append(m)

    nc, ctx = build_nc(n_sub)
    try:
        res = run_bass_kernel_spmd(nc, in_maps, core_ids=list(range(NCORES)))
    finally:
        ctx.close()
    out = np.concatenate([res.results[c]["y"] for c in range(NCORES)], axis=0)
    return np.ascontiguousarray(out.astype(np.float32))


# version 22
# speedup vs baseline: 1.0118x; 1.0118x over previous
import contextlib

import numpy as np
import ml_dtypes

import concourse.bass as bass
import concourse.mybir as mybir
from concourse.bass_utils import run_bass_kernel_spmd

F32 = mybir.dt.float32
BF16 = mybir.dt.bfloat16
OP = mybir.AluOpType
AF = mybir.ActivationFunctionType
AX = mybir.AxisListType.X

NCORES = 8
S = 2048
D = 1024
NT = S // 128
NHG = 8
NQG = 4
NKT = 16

_N_SUB = 4


def _tile2(m):
    r, c = m.shape
    return np.ascontiguousarray(m.reshape(r // 128, 128, c).transpose(1, 0, 2).reshape(128, (r // 128) * c))


def _hg_stream(wT):
    return np.ascontiguousarray(np.stack([_tile2(wT[:, hg * 128:(hg + 1) * 128]) for hg in range(8)]))


def _bias_tile(b):
    return np.ascontiguousarray(b.reshape(8, 128).T)



class _Op:
    __slots__ = ("eng", "fn", "deps", "drains", "eidx", "needs_inc", "ord", "is_dma", "sem", "tag")

    def __init__(self, eng, fn, tag=""):
        self.eng = eng
        self.fn = fn
        self.deps = []
        self.drains = []
        self.eidx = -1
        self.needs_inc = False
        self.ord = None
        self.is_dma = False
        self.sem = None
        self.tag = tag


class Prog:
    ENGS = ("tensor", "vector", "scalar", "sync", "gpsimd")
    DRAIN_ENGS = ("vector", "scalar", "gpsimd")

    def __init__(self):
        self.eng_ops = {e: [] for e in self.ENGS}
        self.bufw = {}
        self.bufr = {}
        self.dma_cum = {}

    def _resolve(self, eng, reads, writes):
        deps, drains = [], []

        def add(tok):
            if tok is None:
                return
            if tok[0] == "op" and tok[1].eng == eng and eng in self.DRAIN_ENGS:
                drains.append(tok[1])
            elif tok[0] == "op" and tok[1].eng == eng:
                pass
            else:
                deps.append(tok)

        for k in reads:
            add(self.bufw.get(k))
        for k in writes:
            add(self.bufw.get(k))
            for t in self.bufr.get(k, ()):
                add(t)
        seen, d2 = set(), []
        for t in deps:
            key = id(t[1]) if t[0] == "op" else (t[1], t[2])
            if key not in seen:
                seen.add(key)
                d2.append(t)
        return d2, list({id(o): o for o in drains}.values())

    def op(self, eng, fn, reads=(), writes=(), tag=""):
        o = _Op(eng, fn, tag)
        o.deps, o.drains = self._resolve(eng, reads, writes)
        tok = ("op", o)
        for k in reads:
            self.bufr.setdefault(k, []).append(tok)
        for k in writes:
            self.bufw[k] = tok
            self.bufr[k] = []
        o.eidx = len(self.eng_ops[eng])
        self.eng_ops[eng].append(o)
        return o

    def dma(self, queue, sem_name, fn, reads=(), writes=(), tag=""):
        o = _Op(queue, fn, tag)
        o.is_dma = True
        o.sem = sem_name
        o.deps, o.drains = self._resolve(queue, reads, writes)
        self.dma_cum[sem_name] = self.dma_cum.get(sem_name, 0) + 16
        tok = ("dma", sem_name, self.dma_cum[sem_name])
        for k in reads:
            self.bufr.setdefault(k, []).append(tok)
        for k in writes:
            self.bufw[k] = tok
            self.bufr[k] = []
        o.eidx = len(self.eng_ops[queue])
        self.eng_ops[queue].append(o)
        return o

    def settle(self, sem_name, keys):
        tok = ("dma", sem_name, self.dma_cum[sem_name])
        for k in keys:
            w = self.bufw.get(k)
            if w is not None and w[0] == "dma" and w[1] == sem_name:
                self.bufw[k] = tok
            rl = self.bufr.get(k)
            if rl:
                self.bufr[k] = [tok if (t[0] == "dma" and t[1] == sem_name) else t for t in rl]

    def finalize(self):
        for ops in self.eng_ops.values():
            for o in ops:
                for t in o.deps:
                    if t[0] == "op":
                        t[1].needs_inc = True
        for ops in self.eng_ops.values():
            n = 0
            for o in ops:
                if o.needs_inc:
                    if o.is_dma:
                        raise RuntimeError(f"dma op {o.tag} marked needs_inc")
                    n += 1
                    o.ord = n

    def emit(self, block, eng_sems, dma_sems, post=None):
        prog = self

        def runner(eng_name):
            def run(q):
                waited = {}
                last_drain = -1
                for pos, o in enumerate(prog.eng_ops[eng_name]):
                    if o.drains and max(dd.eidx for dd in o.drains) > last_drain:
                        q.drain()
                        last_drain = pos - 1
                    for t in o.deps:
                        if t[0] == "op":
                            dep = t[1]
                            if dep.ord is None:
                                raise RuntimeError(f"{o.tag} depends on un-inc'd {dep.tag}")
                            sem, val = eng_sems[dep.eng], dep.ord
                        else:
                            sem, val = dma_sems[t[1]], t[2]
                        sid = id(sem)
                        if waited.get(sid, -1) < val:
                            q.wait_ge(sem, val)
                            waited[sid] = val
                    inst = o.fn(q)
                    if o.needs_inc:
                        inst.then_inc(eng_sems[o.eng], 1)
                if post and eng_name in post:
                    post[eng_name](q)
            return run

        block.tensor(runner("tensor"))
        block.vector(runner("vector"))
        block.scalar(runner("scalar"))
        block.sync(runner("sync"))
        block.gpsimd(runner("gpsimd"))



def build_nc(n_sub=4):
    nc = bass.Bass("TRN2", target_bir_lowering=False)
    ctx = contextlib.ExitStack()

    hx = nc.dram_tensor("x", [2, S, D], F32, kind="ExternalInput").ap()
    hy = nc.dram_tensor("y", [2, S, D], F32, kind="ExternalOutput").ap()
    hwq = {k: nc.dram_tensor(f"wq_{k}", [8, 128, 1024], F32, kind="ExternalInput").ap() for k in "ab"}
    hwk = {k: nc.dram_tensor(f"wk_{k}", [8, 128, 1024], F32, kind="ExternalInput").ap() for k in "ab"}
    hwv = {k: nc.dram_tensor(f"wv_{k}", [8, 128, 1024], F32, kind="ExternalInput").ap() for k in "ab"}
    hwo = {k: nc.dram_tensor(f"wo_{k}", [128, 8192], F32, kind="ExternalInput").ap() for k in "ab"}
    hbase = nc.dram_tensor("base", [128, 8192], F32, kind="ExternalInput").ap()
    hast = nc.dram_tensor("ast", [128, 1024], F32, kind="ExternalInput").ap()
    hbst = nc.dram_tensor("bst", [128, 1024], F32, kind="ExternalInput").ap()
    hrouter = nc.dram_tensor("router", [128, 64], F32, kind="ExternalInput").ap()
    hbb = nc.dram_tensor("base_b", [1, D], F32, kind="ExternalInput").ap()
    hrb = nc.dram_tensor("router_b", [1, 8], F32, kind="ExternalInput").ap()
    hbo = nc.dram_tensor("bo", [1, D], F32, kind="ExternalInput").ap()
    hbvr = nc.dram_tensor("bvrow", [1, D], F32, kind="ExternalInput").ap()
    hbqt = nc.dram_tensor("bqt", [128, 8], F32, kind="ExternalInput").ap()
    hbkt = nc.dram_tensor("bkt", [128, 8], F32, kind="ExternalInput").ap()
    hbvt = nc.dram_tensor("bvt", [128, 8], F32, kind="ExternalInput").ap()
    hmask = nc.dram_tensor("head_mask", [1, 16], F32, kind="ExternalInput").ap()
    hbigi = nc.dram_tensor("bigi", [128, 896], BF16, kind="ExternalInput").ap()
    hrep = nc.dram_tensor("repmat", [8, 128], BF16, kind="ExternalInput").ap()
    hrr = [nc.dram_tensor(f"hbm_rr{i}", [2, 2048], BF16, kind="Internal").ap() for i in range(2)]

    def alloc(name, shape, dtype):
        return ctx.enter_context(nc.sbuf_tensor("s_" + name, shape, dtype))

    x = alloc("xbuf", [128, NT * 1024], F32)
    xnT = alloc("xnT", [128, 8 * 2048], BF16)
    aoT = alloc("aoT", [128, 8 * 2048], BF16)
    qt = alloc("qt", [128, 4096], BF16)
    kt_ = alloc("kt", [128, 4096], BF16)
    vaug = alloc("vaug", [128, 2 * NT * 192], BF16)
    pbuf = alloc("pbuf", [128, 2048], BF16)
    wpool = alloc("wpool", [128, 14 * 1024], BF16)
    rexp = alloc("rexp", [128, 2048], BF16)
    rrow = alloc("rrow", [128, 1024], BF16)
    sumsT = alloc("sumsT", [128, 512], F32)
    grid2 = alloc("grid2", [128, 512], BF16)
    rtokb = alloc("rtokb", [128, 16], F32)
    xns = alloc("xns", [128, 1024], BF16)
    bn6 = alloc("bn6", [128, 12], F32)
    mv = alloc("mv", [128, 32], F32)
    lnv = alloc("lnv", [128, 16], F32)
    rstd = alloc("rstd", [128, 16], F32)
    bigi = alloc("bigi", [128, 896], BF16)
    rep = alloc("rep", [128, 128], BF16)
    e2 = alloc("e2", [128, 2], BF16)
    onesr = alloc("onesr", [1, 144], BF16)
    rbrow = onesr[0:1, 128:136]
    maskb = alloc("maskb", [128, 16], F32)
    negbb = alloc("negbb", [128, 16], F32)
    bqt = alloc("bqt", [128, 8], F32)
    bkt = alloc("bkt", [128, 8], F32)
    rows = alloc("rows", [128, 1152], BF16)
    borow = rows[0:1, 128:1152]
    bvrow = rows[32:33, 128:1152]
    bbrow = rows[64:65, 128:1152]
    logb = alloc("logb", [128, 8], F32)
    max8 = alloc("max8", [128, 8], F32)
    negm1 = alloc("negm1", [128, 1], F32)
    expw = alloc("expw", [128, 8], F32)
    keep = alloc("keep", [128, 8], F32)
    sumw = alloc("sumw", [128, 1], F32)
    wt8 = alloc("wt8", [128, 32], BF16)
    wtrow = alloc("wtrow", [128, 512], BF16)
    axwS = alloc("axwS", [128, 1024], BF16)
    mq = alloc("mq", [128, 1], F32)
    mq4 = alloc("mq4", [128, 4], F32)
    mk4 = alloc("mk4", [128, 4], F32)
    mk = alloc("mk", [128, 1], F32)
    tprod = alloc("tprod", [128, 1], BF16)
    nbtmp = alloc("nbtmp", [1, 2], F32)
    nb1 = alloc("nb1", [1, 1], F32)
    nbbf = alloc("nbbf", [1, 1], BF16)
    epsb = alloc("epsb", [128, 1], F32)

    psA = ctx.enter_context(nc.psum_tensor("psA", [128, 1024], F32))
    psB = ctx.enter_context(nc.psum_tensor("psB", [128, 1024], F32))
    psC = ctx.enter_context(nc.psum_tensor("psC", [128, 1024], F32))
    psD = ctx.enter_context(nc.psum_tensor("psD", [128, 512], F32))
    psE = ctx.enter_context(nc.psum_tensor("psE", [128, 512], F32))
    bank = {
        "psA0": psA[:, 0:512], "psA1": psA[:, 512:1024],
        "psB0": psB[:, 0:512], "psB1": psB[:, 512:1024],
        "psC0": psC[:, 0:512], "psC1": psC[:, 512:1024],
        "psD": psD[:, :], "psE": psE[:, :],
    }

    eng_sems = {e: ctx.enter_context(nc.semaphore(f"sem_{e}")) for e in Prog.ENGS}
    dma_names = (["consts", "constg", "ffnw", "wo", "rrout", "rrin"]
                 + [f"ws{i}" for i in range(6)] + ["wsa0", "wsa1"]
                 + [f"xi{t}" for t in range(NT)] + [f"xo{t}" for t in range(NT)])
    dma_sems = {n: ctx.enter_context(nc.semaphore(f"dma_{n}")) for n in dma_names}

    P = Prog()

    def d(queue, sem, out_ap, in_ap, reads=(), writes=(), tag=""):
        def fn(q, out_ap=out_ap, in_ap=in_ap, sem=sem):
            return q.dma_start(out=out_ap, in_=in_ap).then_inc(dma_sems[sem], 16)
        return P.dma(queue, sem, fn, reads=reads, writes=writes, tag=tag)

    d("sync", "consts", bigi[:, :], hbigi, writes=["bigi"], tag="c.bigi")
    d("sync", "consts", rep[0:8, :], hrep, writes=["rep"], tag="c.rep")
    d("sync", "consts", maskb[:, :], hmask.to_broadcast([128, 16]), writes=["maskb"], tag="c.mask")
    d("sync", "consts", bqt[:, :], hbqt, writes=["bqt"], tag="c.bqt")
    d("sync", "consts", bkt[:, :], hbkt, writes=["bkt"], tag="c.bkt")
    d("gpsimd", "constg", borow, hbo, writes=["borow"], tag="c.bo")
    d("gpsimd", "constg", bvrow, hbvr, writes=["bvrow"], tag="c.bvr")
    P.settle("constg", ["borow", "bvrow"])
    P.settle("consts", ["bigi", "rep", "maskb", "bqt", "bkt"])

    P.op("vector", lambda q: q.memset(e2[:, :], 0.0), writes=["e2"], tag="ms.e2a")
    P.op("vector", lambda q: q.memset(e2[0:64, 0:1], 1.0), reads=["e2"], writes=["e2"], tag="ms.e2b")
    P.op("vector", lambda q: q.memset(e2[64:128, 1:2], 1.0), reads=["e2"], writes=["e2"], tag="ms.e2c")
    P.op("vector", lambda q: q.memset(onesr[0:1, 0:128], 1.0), writes=["onesr"], tag="ms.ones")
    P.op("vector", lambda q: q.memset(rows[0:1, 0:128], 1.0), writes=["ones0"], tag="ms.r0")
    P.op("vector", lambda q: q.memset(rows[32:33, 0:128], 1.0), writes=["ones32"], tag="ms.r32")
    P.op("vector", lambda q: q.memset(rows[64:65, 0:128], 1.0), writes=["ones64"], tag="ms.r64")
    P.op("vector", lambda q: q.memset(epsb[:, :], 1e-5), writes=["epsb"], tag="ms.eps")
    P.op("vector", lambda q: q.memset(grid2[:, :], 0.0), writes=["grid2a", "grid2b"], tag="ms.g2")
    for par in range(2):
        for t in range(NT):
            b0 = par * NT * 192 + t * 192
            P.op("vector", lambda q, b=b0: q.memset(vaug[:, b + 64:b + 96], 1.0),
                 writes=[f"vaug{par}.{t}"], tag=f"ms.v1.{par}.{t}")
            P.op("vector", lambda q, b=b0: q.memset(vaug[:, b + 96:b + 128], 0.0),
                 reads=[f"vaug{par}.{t}"], writes=[f"vaug{par}.{t}"], tag=f"ms.v0.{par}.{t}")

    def xin_tile(seq, t):
        def fn(q, t=t, seq=seq):
            return q.dma_start(out=x[:, t * 1024:(t + 1) * 1024],
                               in_=hx[seq, t * 128:(t + 1) * 128, :]).then_inc(dma_sems[f"xi{t}"], 16)
        P.dma("sync", f"xi{t}", fn, writes=[f"x.{t}"], tag=f"xin{seq}.{t}")

    def xout_tile(seq, t):
        def fn(q, t=t, seq=seq):
            return q.dma_start(out=hy[seq, t * 128:(t + 1) * 128, :],
                               in_=x[:, t * 1024:(t + 1) * 1024]).then_inc(dma_sems[f"xo{t}"], 16)
        P.dma("sync", f"xo{t}", fn, reads=[f"x.{t}"], tag=f"xout{seq}.{t}")

    xnT3 = xnT[:, :].rearrange("p (c j) -> p c j", c=8)

    from collections import deque

    def ln_tile_stats(tag, t):
        P.op("vector", lambda q, t=t: q.bn_stats(bn6[:, 0:6], x[:, t * 1024:t * 1024 + 512]),
             reads=[f"x.{t}"], writes=["bn6a"], tag=f"{tag}.bns0.{t}")
        P.op("vector", lambda q, t=t: q.bn_stats(bn6[:, 6:12], x[:, t * 1024 + 512:t * 1024 + 1024]),
             reads=[f"x.{t}"], writes=["bn6b"], tag=f"{tag}.bns1.{t}")
        P.op("vector", lambda q, t=t: q.bn_aggr(mv[:, 2 * t:2 * t + 2], bn6[:, 0:12]),
             reads=["bn6a", "bn6b"], writes=[f"mv.{t}"], tag=f"{tag}.bna.{t}")

    def ln_group(tag, g):
        P.op("scalar", lambda q, g=g: q.activation(lnv[:, 4 * g:4 * g + 4],
                                                   mv[:, 8 * g + 1:8 * g + 8:2],
                                                   AF.Ln, bias=epsb[:, 0:1], scale=1.0),
             reads=[f"mv.{t}" for t in range(g * 4, g * 4 + 4)] + ["epsb"],
             writes=[f"lnv.{g}"], tag=f"{tag}.lnv.{g}")
        P.op("scalar", lambda q, g=g: q.activation(rstd[:, 4 * g:4 * g + 4], lnv[:, 4 * g:4 * g + 4],
                                                   AF.Exp, bias=0.0, scale=-0.5),
             reads=[f"lnv.{g}"], writes=[f"rstd.{g}"], tag=f"{tag}.rstd.{g}")
        for t in range(g * 4, g * 4 + 4):
            P.op("vector", lambda q, t=t: q.tensor_scalar(
                xns[:, :], x[:, t * 1024:(t + 1) * 1024],
                mv[:, 2 * t:2 * t + 1], rstd[:, t:t + 1], OP.subtract, OP.mult),
                reads=[f"x.{t}", f"mv.{t}", f"rstd.{g}"], writes=["xns"],
                tag=f"{tag}.norm.{t}")
            for half in range(2):
                bk = ("psD", "psE")[(t * 2 + half) % 2]
                for c in range(4):
                    cc = half * 4 + c
                    P.op("tensor", lambda q, cc=cc, c=c, bk=bk: q.matmul(
                        bank[bk], xns[:, cc * 128:(cc + 1) * 128],
                        bigi[:, 384 - c * 128:896 - c * 128],
                        start=(c == 0), stop=(c == 3)),
                        reads=["xns", "bigi", bk], writes=[bk], tag=f"{tag}.T.{t}.{cc}")
                P.op("vector", lambda q, t=t, half=half, bk=bk: q.tensor_copy(
                    xnT3[:, 4 * half:4 * half + 4, t * 128:(t + 1) * 128], bank[bk]),
                    reads=[bk], writes=[f"xnT.w{t // 4}"], tag=f"{tag}.Tev.{t}.{half}")

    def ln_xnT(tag):
        for g in range(4):
            for t in range(g * 4, g * 4 + 4):
                ln_tile_stats(tag, t)
            ln_group(tag, g)

    def make_ln_hook(tag):
        def hook(t):
            ln_tile_stats(tag, t)
        return hook

    def attn_load(kind, tag, alt=False):
        d("gpsimd", "wo", wpool[:, 6 * 1024:14 * 1024], hwo[kind],
          writes=[f"wp{i}" for i in range(6, 14)], tag=f"{tag}.dwo")
        for hg01 in (0, 1):
            s0 = (hg01 % 2) * 3
            for j, (hn, hsrc) in enumerate((("wq", hwq), ("wk", hwk), ("wv", hwv))):
                if alt and hg01 == 0 and j < 2:
                    continue
                d("gpsimd", f"ws{s0 + j}", wpool[:, (s0 + j) * 1024:(s0 + j + 1) * 1024],
                  hsrc[kind][hg01], writes=[f"wp{s0 + j}"], tag=f"{tag}.dw{hn}.{hg01}")

    def attn(seq, kind, tag, hook=None, next_loader=None, preloaded=False, inline_ln_tag=None, ln_rest_tag=None):
        lcol = 0 if kind == "a" else 8
        fillers = deque()

        def pump(n):
            while n > 0 and fillers:
                f = fillers.popleft()
                if f is None:
                    continue
                f()
                n -= 1

        def flush():
            while fillers:
                f = fillers.popleft()
                if f is not None:
                    f()

        def wslices(hg):
            s0 = (hg % 2) * 3
            for j, (hn, hsrc) in enumerate((("wq", hwq), ("wk", hwk), ("wv", hwv))):
                d("gpsimd", f"ws{s0 + j}", wpool[:, (s0 + j) * 1024:(s0 + j + 1) * 1024],
                  hsrc[kind][hg], writes=[f"wp{s0 + j}"], tag=f"{tag}.dw{hn}.{hg}")

        def emit_qkv(hg, into):
            s0 = (hg % 2) * 3
            par = hg % 2
            if hg == 0 and preloaded:
                wq_ap = vaug[:, 3072:4096]
                wk_ap = vaug[:, 4096:5120]
                wqk, wkk = "wstq", "wstk"
            else:
                wq_ap = wpool[:, s0 * 1024:(s0 + 1) * 1024]
                wk_ap = wpool[:, (s0 + 1) * 1024:(s0 + 2) * 1024]
                wqk, wkk = f"wp{s0}", f"wp{s0 + 1}"
            wv_ap = wpool[:, (s0 + 2) * 1024:(s0 + 3) * 1024]
            wvk = f"wp{s0 + 2}"
            for g in range(4):
                for mi, (wap, wkey, dstb, dkey, bt, bkey) in enumerate((
                        (wq_ap, wqk, qt, "qt", bqt if kind == "a" else None, "bqt"),
                        (wk_ap, wkk, kt_, "kt", bkt if kind == "a" else None, "bkt"))):
                    bk = ("psD", "psE")[(2 * g + mi) % 2]
                    for c in range(8):
                        into.append(lambda wap=wap, c=c, g=g, bk=bk, wkey=wkey: P.op(
                            "tensor", lambda q: q.matmul(
                                bank[bk], wap[:, c * 128:(c + 1) * 128],
                                xnT[:, c * 2048 + g * 512:c * 2048 + (g + 1) * 512],
                                start=(c == 0), stop=(c == 7)),
                            reads=[wkey, f"xnT.w{g}", bk], writes=[bk],
                            tag=f"{tag}.qk.{hg}.{g}.{c}"))
                    if bt is not None:
                        into.append(lambda dstb=dstb, g=g, bk=bk, bt=bt, hg=hg, par=par,
                                    dkey=dkey, bkey=bkey: P.op(
                            "vector", lambda q: q.tensor_scalar(
                                dstb[:, par * 2048 + g * 512:par * 2048 + (g + 1) * 512],
                                bank[bk], bt[:, hg:hg + 1], None, OP.add),
                            reads=[bk, bkey], writes=[f"{dkey}{par}.{g}"],
                            tag=f"{tag}.qkev.{hg}.{g}"))
                    else:
                        into.append(lambda dstb=dstb, g=g, bk=bk, par=par, dkey=dkey: P.op(
                            "vector", lambda q: q.tensor_copy(
                                dstb[:, par * 2048 + g * 512:par * 2048 + (g + 1) * 512],
                                bank[bk]),
                            reads=[bk], writes=[f"{dkey}{par}.{g}"],
                            tag=f"{tag}.qkev.{hg}.{g}"))
                if seq == 0:
                    into.append(lambda par=par, g=g: P.op(
                        "vector", lambda q: q.tensor_reduce(
                            mq4[:, g:g + 1], qt[:, par * 2048 + g * 512:par * 2048 + (g + 1) * 512],
                            AX, OP.max, apply_absolute_value=True),
                        reads=[f"qt{par}.{g}"], writes=[f"mq4.{g}"], tag=f"{tag}.mqp.{hg}.{g}"))
                    into.append(lambda par=par, g=g: P.op(
                        "vector", lambda q: q.tensor_reduce(
                            mk4[:, g:g + 1], kt_[:, par * 2048 + g * 512:par * 2048 + (g + 1) * 512],
                            AX, OP.max, apply_absolute_value=True),
                        reads=[f"kt{par}.{g}"], writes=[f"mk4.{g}"], tag=f"{tag}.mkp.{hg}.{g}"))
                for t in range(g * 4, g * 4 + 4):
                    bk = ("psD", "psE")[t % 2]
                    for c in range(8):
                        into.append(lambda t=t, c=c, wv_ap=wv_ap, bk=bk, wvk=wvk, kind=kind: P.op(
                            "tensor", lambda q: q.matmul(
                                bank[bk][:, 0:128], xnT[:, c * 2048 + t * 128:c * 2048 + (t + 1) * 128],
                                wv_ap[:, c * 128:(c + 1) * 128],
                                start=(c == 0), stop=(c == 7 and kind != "a")),
                            reads=[wvk, f"xnT.w{t // 4}", bk], writes=[bk],
                            tag=f"{tag}.v.{hg}.{t}.{c}"))
                    if kind == "a":
                        into.append(lambda hg=hg, bk=bk, t=t: P.op(
                            "tensor", lambda q: q.matmul(
                                bank[bk][:, 0:128], rows[32:33, 0:128],
                                rows[32:33, 128 + hg * 128:128 + (hg + 1) * 128],
                                start=False, stop=True),
                            reads=["ones32", "bvrow", bk], writes=[bk], tag=f"{tag}.vb.{hg}.{t}"))

                    def vev_mk(t=t, bk=bk, par=par):
                        def vev(q):
                            dst = bass.AP(tensor=vaug[:, 0:1].tensor, offset=par * NT * 192 + t * 192,
                                          ap=[[2 * NT * 192, 128], [128, 2], [1, 64]])
                            return q.tensor_copy(dst, bank[bk][:, 0:128])
                        return vev
                    into.append(lambda t=t, bk=bk, par=par: P.op(
                        "vector", vev_mk(t, bk, par), reads=[bk], writes=[f"vaug{par}.{t}"],
                        tag=f"{tag}.vev.{hg}.{t}"))
                into.append(None)
            if seq == 0:
                into.append(lambda: P.op(
                    "vector", lambda q: q.reduce_max(mq[:, :], mq4[:, :], AX),
                    reads=[f"mq4.{g}" for g in range(4)], writes=["mq"], tag=f"{tag}.mq.{hg}"))
                into.append(lambda: P.op(
                    "vector", lambda q: q.reduce_max(mk[:, :], mk4[:, :], AX),
                    reads=[f"mk4.{g}" for g in range(4)], writes=["mk"], tag=f"{tag}.mk.{hg}"))
                into.append(lambda: P.op(
                    "vector", lambda q: q.tensor_mul(tprod[:, :], mq[:, :], mk[:, :]),
                    reads=["mq", "mk"], writes=["tprod"], tag=f"{tag}.tp.{hg}"))
                into.append(lambda: P.op(
                    "tensor", lambda q: q.matmul(psE[0:1, 0:2], tprod[:, :], e2[:, :],
                                                 start=True, stop=True),
                    reads=["tprod", "e2", "psE"], writes=["psE"], tag=f"{tag}.bsum.{hg}"))
                into.append(lambda: P.op(
                    "vector", lambda q: q.tensor_scalar(nbtmp[0:1, :], psE[0:1, 0:2],
                                                        480.0, None, OP.min),
                    reads=["psE"], writes=["nbtmp"], tag=f"{tag}.bclamp.{hg}"))
                into.append(lambda: P.op(
                    "vector", lambda q: q.reduce_max(nb1[0:1, :], nbtmp[0:1, :], AX),
                    reads=["nbtmp"], writes=["nb1"], tag=f"{tag}.bmax.{hg}"))
                into.append(lambda: P.op(
                    "vector", lambda q: q.tensor_copy(nbbf[0:1, :], nb1[0:1, :]),
                    reads=["nb1"], writes=["nbbf"], tag=f"{tag}.bbf.{hg}"))
                into.append(lambda: P.op(
                    "tensor", lambda q: q.matmul(psE[:, 8:9], onesr[0:1, 0:128], nbbf[0:1, :],
                                                 start=True, stop=True),
                    reads=["nbbf", "onesr", "psE"], writes=["psE"], tag=f"{tag}.bbc.{hg}"))
                into.append(lambda hg=hg: P.op(
                    "vector", lambda q: q.tensor_scalar(
                        negbb[:, lcol + hg:lcol + hg + 1], psE[:, 8:9], -0.125, None, OP.mult),
                    reads=["psE"], writes=[f"negbb.{lcol + hg}"], tag=f"{tag}.bneg.{hg}"))

        def emit_wo_pass(into, cs, with_bias, passtag):
            c0, c1 = cs
            for t in range(NT):
                for f in range(2):
                    bk = ("psD", "psE")[(t * 2 + f) % 2]
                    for c in range(c0, c1):
                        laststop = (c == c1 - 1 and not with_bias)
                        into.append(lambda t=t, f=f, c=c, bk=bk, laststop=laststop, c0=c0: P.op(
                            "tensor", lambda q: q.matmul(
                                bank[bk], aoT[:, c * 2048 + t * 128:c * 2048 + (t + 1) * 128],
                                wpool[:, 6 * 1024 + c * 1024 + f * 512:
                                       6 * 1024 + c * 1024 + (f + 1) * 512],
                                start=(c == c0), stop=laststop),
                            reads=[f"aoT.{c}", f"wp{6 + c}", bk], writes=[bk],
                            tag=f"{tag}.{passtag}.{t}.{f}.{c}"))
                    if with_bias:
                        into.append(lambda f=f, bk=bk, t=t: P.op(
                            "tensor", lambda q: q.matmul(
                                bank[bk], rows[0:1, 0:128], rows[0:1, 128 + f * 512:128 + (f + 1) * 512],
                                start=False, stop=True),
                            reads=["ones0", "borow", bk], writes=[bk],
                            tag=f"{tag}.{passtag}b.{t}.{f}"))
                    into.append(lambda t=t, f=f, bk=bk: P.op(
                        "vector", lambda q: q.tensor_tensor(
                            x[:, t * 1024 + f * 512:t * 1024 + (f + 1) * 512],
                            x[:, t * 1024 + f * 512:t * 1024 + (f + 1) * 512],
                            bank[bk], OP.add),
                        reads=[bk, f"x.{t}"], writes=[f"x.{t}"],
                        tag=f"{tag}.{passtag}add.{t}.{f}"))

        def emit_scores(hg, mid=()):
            par = hg % 2
            qo = par * 2048
            for qg in range(NQG):
                if qg == 2:
                    for m in mid:
                        m()
                def emit_st(kt):
                    kpar = kt % 2
                    ps = psA if kpar == 0 else psB
                    sk0, sk1 = (("psA0", "psA1") if kpar == 0 else ("psB0", "psB1"))
                    P.op("tensor", lambda q, kt=kt, qg=qg, ps=ps, qo=qo: q.matmul(
                        ps[:, 0:512], kt_[0:64, qo + kt * 128:qo + (kt + 1) * 128],
                        qt[0:64, qo + qg * 512:qo + (qg + 1) * 512],
                        start=True, stop=True, tile_position=(0, 0)),
                        reads=[f"kt{par}.{kt // 4}", f"qt{par}.{qg}", sk0], writes=[sk0],
                        tag=f"{tag}.s0.{hg}.{qg}.{kt}")
                    P.op("tensor", lambda q, kt=kt, qg=qg, ps=ps, qo=qo: q.matmul(
                        ps[:, 512:1024], kt_[64:128, qo + kt * 128:qo + (kt + 1) * 128],
                        qt[64:128, qo + qg * 512:qo + (qg + 1) * 512],
                        start=True, stop=True, tile_position=(64, 0)),
                        reads=[f"kt{par}.{kt // 4}", f"qt{par}.{qg}", sk1], writes=[sk1],
                        tag=f"{tag}.s1.{hg}.{qg}.{kt}")
                    P.op("scalar", lambda q, ps=ps, kpar=kpar, hg=hg: q.activation(
                        pbuf[:, kpar * 1024:(kpar + 1) * 1024], ps[:, 0:1024], AF.Exp,
                        bias=negbb[:, lcol + hg:lcol + hg + 1], scale=0.125),
                        reads=[sk0, sk1, f"negbb.{lcol + hg}"], writes=[f"pbuf.{kpar}"],
                        tag=f"{tag}.exp.{hg}.{qg}.{kt}")

                def emit_pv(kt):
                    kpar = kt % 2
                    P.op("tensor", lambda q, kt=kt, kpar=kpar, par=par: q.matmul(
                        psC[0:128, 0:512],
                        vaug[:, par * NT * 192 + kt * 192:par * NT * 192 + kt * 192 + 128],
                        pbuf[:, kpar * 1024:kpar * 1024 + 512],
                        start=(kt == 0), stop=(kt == NKT - 1)),
                        reads=[f"pbuf.{kpar}", f"vaug{par}.{kt}", "psC0"], writes=["psC0"],
                        tag=f"{tag}.pv0.{hg}.{qg}.{kt}")
                    P.op("tensor", lambda q, kt=kt, kpar=kpar, par=par: q.matmul(
                        psC[0:128, 512:1024],
                        vaug[:, par * NT * 192 + kt * 192 + 64:par * NT * 192 + (kt + 1) * 192],
                        pbuf[:, kpar * 1024 + 512:kpar * 1024 + 1024],
                        start=(kt == 0), stop=(kt == NKT - 1)),
                        reads=[f"pbuf.{kpar}", f"vaug{par}.{kt}", "psC1"], writes=["psC1"],
                        tag=f"{tag}.pv1.{hg}.{qg}.{kt}")

                for kt in range(NKT + 1):
                    if kt < NKT:
                        emit_st(kt)
                    if kt > 0:
                        emit_pv(kt - 1)
                    pump(3 if kt % 4 == 3 else 4)
                P.op("scalar", lambda q, hg=hg, qg=qg: q.copy(
                    aoT[0:64, hg * 2048 + qg * 512:hg * 2048 + (qg + 1) * 512], psC[0:64, 0:512]),
                    reads=["psC0"], writes=[f"aoT.{hg}"], tag=f"{tag}.aev0.{hg}.{qg}")
                P.op("scalar", lambda q, hg=hg, qg=qg: q.copy(
                    aoT[64:128, hg * 2048 + qg * 512:hg * 2048 + (qg + 1) * 512],
                    psC[64:128, 512:1024]),
                    reads=["psC1"], writes=[f"aoT.{hg}"], tag=f"{tag}.aev1.{hg}.{qg}")
                P.op("vector", lambda q: q.transpose(sumsT[64:96, 0:512], psC[64:96, 0:512]),
                     reads=["psC0"], writes=["sumsT0"], tag=f"{tag}.st0.{hg}.{qg}")
                P.op("vector", lambda q: q.transpose(sumsT[0:32, 0:512], psC[0:32, 512:1024]),
                     reads=["psC1"], writes=["sumsT1"], tag=f"{tag}.st1.{hg}.{qg}")
                P.op("vector", lambda q: q.reciprocal(rtokb[64:96, 0:16], sumsT[64:96, 0:512:32]),
                     reads=["sumsT0"], writes=["rtok0"], tag=f"{tag}.rc0.{hg}.{qg}")
                P.op("vector", lambda q: q.reciprocal(rtokb[0:32, 0:16], sumsT[0:32, 0:512:32]),
                     reads=["sumsT1"], writes=["rtok1"], tag=f"{tag}.rc1.{hg}.{qg}")
                if kind == "a":
                    P.op("vector", lambda q, hg=hg: q.tensor_scalar(
                        rtokb[64:96, 0:16], rtokb[64:96, 0:16],
                        maskb[64:96, 2 * hg:2 * hg + 1], None, OP.mult),
                        reads=["rtok0", "maskb"], writes=["rtok0"], tag=f"{tag}.msk0.{hg}.{qg}")
                    P.op("vector", lambda q, hg=hg: q.tensor_scalar(
                        rtokb[0:32, 0:16], rtokb[0:32, 0:16],
                        maskb[0:32, 2 * hg + 1:2 * hg + 2], None, OP.mult),
                        reads=["rtok1", "maskb"], writes=["rtok1"], tag=f"{tag}.msk1.{hg}.{qg}")
                P.op("vector", lambda q: q.tensor_copy(grid2[64:96, 0:512:32], rtokb[64:96, 0:16]),
                     reads=["rtok0"], writes=["grid2a"], tag=f"{tag}.g2a.{hg}.{qg}")
                P.op("vector", lambda q: q.tensor_copy(grid2[0:32, 0:512:32], rtokb[0:32, 0:16]),
                     reads=["rtok1"], writes=["grid2b"], tag=f"{tag}.g2b.{hg}.{qg}")
                P.op("vector", lambda q, qg=qg: q.transpose(
                    rrow[64:96, (qg % 2) * 512:(qg % 2 + 1) * 512], grid2[64:96, 0:512]),
                    reads=["grid2a"], writes=["rrow"], tag=f"{tag}.g3a.{hg}.{qg}")
                P.op("vector", lambda q, qg=qg: q.transpose(
                    rrow[0:32, (qg % 2) * 512:(qg % 2 + 1) * 512], grid2[0:32, 0:512]),
                    reads=["grid2b"], writes=["rrow"], tag=f"{tag}.g3b.{hg}.{qg}")
                pump(4)
                if qg % 2 == 1 and not (hg == 7 and qg == 3):
                    hh = qg // 2
                    slot = hg % 2
                    d("sync", "rrout", hrr[slot][0:1, hh * 1024:(hh + 1) * 1024],
                      rrow[64:65, 0:1024],
                      reads=["rrow"], writes=[f"hrr{slot}"], tag=f"{tag}.rro0.{hg}.{hh}")
                    d("sync", "rrout", hrr[slot][1:2, hh * 1024:(hh + 1) * 1024],
                      rrow[0:1, 0:1024],
                      reads=["rrow"], writes=[f"hrr{slot}"], tag=f"{tag}.rro1.{hg}.{hh}")
                if hg == 7 and qg == 2:
                    slot = hg % 2
                    d("sync", "rrin", rexp[0:64, 0:1024],
                      hrr[slot][0:1, 0:1024].to_broadcast([64, 1024]),
                      reads=[f"hrr{slot}"], writes=["rexpb"], tag=f"{tag}.rri0h.{hg}")
                    d("sync", "rrin", rexp[64:128, 0:1024],
                      hrr[slot][1:2, 0:1024].to_broadcast([64, 1024]),
                      reads=[f"hrr{slot}"], writes=["rexpb"], tag=f"{tag}.rri1h.{hg}")
            slot = hg % 2
            if hg < 7:
                d("sync", "rrin", rexp[0:64, :], hrr[slot][0:1, :].to_broadcast([64, 2048]),
                  reads=[f"hrr{slot}"], writes=["rexpb"], tag=f"{tag}.rri0.{hg}")
                d("sync", "rrin", rexp[64:128, :], hrr[slot][1:2, :].to_broadcast([64, 2048]),
                  reads=[f"hrr{slot}"], writes=["rexpb"], tag=f"{tag}.rri1.{hg}")
            else:
                for qg2 in (2, 3):
                    sl = qg2 % 2
                    ps = (psA, psB)[qg2 - 2]
                    pk0, pk1 = (("psA0", "psA1"), ("psB0", "psB1"))[qg2 - 2]
                    P.op("tensor", lambda q, ps=ps, sl=sl: q.matmul(
                        ps[0:64, 0:512], rows[64:65, 0:64], rrow[64:65, sl * 512:(sl + 1) * 512],
                        start=True, stop=True),
                        reads=["rrow", "ones64", pk0], writes=[pk0], tag=f"{tag}.rbcA.{qg2}")
                    P.op("tensor", lambda q, ps=ps, sl=sl: q.matmul(
                        ps[64:128, 512:1024], onesr[0:1, 0:64], rrow[0:1, sl * 512:(sl + 1) * 512],
                        start=True, stop=True, tile_position=(0, 64)),
                        reads=["rrow", "onesr", pk1], writes=[pk1], tag=f"{tag}.rbcB.{qg2}")
                    P.op("vector", lambda q, ps=ps, qg2=qg2: q.tensor_copy(
                        rexp[0:64, qg2 * 512:(qg2 + 1) * 512], ps[0:64, 0:512]),
                        reads=[pk0], writes=["rexpb"], tag=f"{tag}.rbevA.{qg2}")
                    P.op("vector", lambda q, ps=ps, qg2=qg2: q.tensor_copy(
                        rexp[64:128, qg2 * 512:(qg2 + 1) * 512], ps[64:128, 512:1024]),
                        reads=[pk1], writes=["rexpb"], tag=f"{tag}.rbevB.{qg2}")

            def nrm(hg=hg):
                P.op("vector", lambda q, hg=hg: q.tensor_tensor(
                    aoT[:, hg * 2048:(hg + 1) * 2048], aoT[:, hg * 2048:(hg + 1) * 2048],
                    rexp[:, :], OP.mult),
                    reads=[f"aoT.{hg}", "rexpb"], writes=[f"aoT.{hg}"], tag=f"{tag}.nrm.{hg}")
            return nrm

        if not preloaded:
            attn_load(kind, tag)
        direct = deque()
        emit_qkv(0, direct)
        for g in range(4):
            if inline_ln_tag is not None:
                for t in range(g * 4, g * 4 + 4):
                    ln_tile_stats(inline_ln_tag, t)
            ln_group(ln_rest_tag, g)
            while direct:
                f = direct.popleft()
                if f is None:
                    break
                f()
        while direct:
            f = direct.popleft()
            if f is not None:
                f()
        if preloaded:
            for t in range(11):
                b0 = NT * 192 + t * 192
                P.op("vector", lambda q, b=b0: q.memset(vaug[:, b + 64:b + 96], 1.0),
                     reads=["wstq", "wstk"], writes=[f"vaug1.{t}", "wstq", "wstk"],
                     tag=f"{tag}.rms1.{t}")
                P.op("vector", lambda q, b=b0: q.memset(vaug[:, b + 96:b + 128], 0.0),
                     reads=[f"vaug1.{t}"], writes=[f"vaug1.{t}"], tag=f"{tag}.rms0.{t}")
        pending_nrm = None
        for hg in range(NHG):
            if hg + 2 < NHG:
                wslices(hg + 2)
            if hg + 1 < NHG:
                emit_qkv(hg + 1, fillers)
            else:
                emit_wo_pass(fillers, (0, 6), False, "wo1")
            pending_nrm = emit_scores(hg, mid=((pending_nrm,) if pending_nrm else ()))
            flush()
        pending_nrm()
        if next_loader is not None:
            next_loader()
        tail = deque()
        emit_wo_pass(tail, (6, 8), kind == "a", "wo2")
        per_tile = len(tail) // NT
        for t in range(NT):
            for _ in range(per_tile):
                tail.popleft()()
            if hook is not None:
                hook(t)
        while tail:
            tail.popleft()()

    def ffn_load(tag, stage_kind=None):
        if stage_kind is not None:
            vkeys = [f"vaug1.{t}" for t in range(11)]
            d("gpsimd", "wsa0", vaug[:, 3072:4096], hwq[stage_kind][0],
              reads=[], writes=["wstq"] + vkeys, tag=f"{tag}.stq")
            d("gpsimd", "wsa1", vaug[:, 4096:5120], hwk[stage_kind][0],
              reads=[], writes=["wstk"] + vkeys, tag=f"{tag}.stk")
        d("gpsimd", "ffnw", wpool[:, 8192:9216], hast, writes=["wp8"], tag=f"{tag}.dast")
        d("gpsimd", "ffnw", wpool[:, 9216:10240], hbst, writes=["wp9"], tag=f"{tag}.dbst")
        d("gpsimd", "ffnw", wpool[:, 10240:10304], hrouter, writes=["wp10"], tag=f"{tag}.drt")
        d("gpsimd", "ffnw", bbrow, hbb, writes=["bbrow"], tag=f"{tag}.dbb")
        d("gpsimd", "ffnw", rbrow, hrb, writes=["rbrow"], tag=f"{tag}.drb")
        d("gpsimd", "ffnw", wpool[:, 0:4096], hbase[:, 0:4096],
          writes=[f"wp{i}" for i in range(4)], tag=f"{tag}.dbase0")
        d("gpsimd", "ffnw", wpool[:, 4096:8192], hbase[:, 4096:8192],
          writes=[f"wp{i}" for i in range(4, 8)], tag=f"{tag}.dbase1")
        P.settle("ffnw", [f"wp{i}" for i in range(8)] + ["wp8", "wp9", "wp10", "bbrow", "rbrow"])

    def ffn(tag, hook=None, preloaded=False, next_loader=None, ln_rest_tag=None):
        if not preloaded:
            ffn_load(tag)

        def producer(g):
            pk = f"psC{g % 2}"
            pap = psC[:, 0:512] if g % 2 == 0 else psC[:, 512:1024]
            for c in range(8):
                P.op("tensor", lambda q, c=c, g=g, pap=pap: q.matmul(
                    pap, wpool[:, 8192 + c * 128:8192 + (c + 1) * 128],
                    xnT[:, c * 2048 + g * 512:c * 2048 + (g + 1) * 512],
                    start=(c == 0), stop=(c == 7)),
                    reads=["wp8", f"xnT.w{g}", pk], writes=[pk], tag=f"{tag}.ax.{g}.{c}")
            for t in range(g * 4, g * 4 + 4):
                bk = ("psD", "psE")[t % 2]
                for c in range(8):
                    P.op("tensor", lambda q, t=t, c=c, bk=bk: q.matmul(
                        bank[bk][:, 0:8], xnT[:, c * 2048 + t * 128:c * 2048 + (t + 1) * 128],
                        wpool[:, 10240 + c * 8:10240 + (c + 1) * 8],
                        start=(c == 0), stop=False),
                        reads=["wp10", f"xnT.w{g}", bk], writes=[bk], tag=f"{tag}.lg.{t}.{c}")
                P.op("tensor", lambda q, bk=bk: q.matmul(
                    bank[bk][:, 0:8], onesr[0:1, 0:128], onesr[0:1, 128:136],
                    start=False, stop=True),
                    reads=["onesr", "rbrow", bk], writes=[bk], tag=f"{tag}.lgb.{t}")
                P.op("vector", lambda q, bk=bk: q.tensor_copy(logb[:, :], bank[bk][:, 0:8]),
                     reads=[bk], writes=["logb"], tag=f"{tag}.lev.{t}")
                P.op("vector", lambda q: q.max(out=max8[:, :], in_=logb[:, :]),
                     reads=["logb"], writes=["max8"], tag=f"{tag}.max8.{t}")
                P.op("vector", lambda q: q.tensor_scalar(keep[:, :], logb[:, :], max8[:, 1:2],
                                                         None, OP.is_ge),
                     reads=["logb", "max8"], writes=["keep"], tag=f"{tag}.keep.{t}")
                P.op("vector", lambda q: q.tensor_scalar(negm1[:, :], max8[:, 0:1], -1.0,
                                                         None, OP.mult),
                     reads=["max8"], writes=["negm1"], tag=f"{tag}.nm1.{t}")
                P.op("scalar", lambda q: q.activation(expw[:, :], logb[:, :], AF.Exp,
                                                      bias=negm1[:, 0:1], scale=1.0),
                     reads=["logb", "negm1"], writes=["expw"], tag=f"{tag}.rexp.{t}")
                P.op("vector", lambda q: q.tensor_mul(expw[:, :], expw[:, :], keep[:, :]),
                     reads=["expw", "keep"], writes=["expw"], tag=f"{tag}.emask.{t}")
                P.op("vector", lambda q: q.reduce_sum(sumw[:, :], expw[:, :], AX),
                     reads=["expw"], writes=["sumw"], tag=f"{tag}.esum.{t}")
                P.op("vector", lambda q: q.reciprocal(sumw[:, :], sumw[:, :]),
                     reads=["sumw"], writes=["sumw"], tag=f"{tag}.erec.{t}")
                P.op("vector", lambda q, t=t: q.tensor_scalar(
                    wt8[:, (t % 4) * 8:(t % 4) * 8 + 8], expw[:, :], sumw[:, 0:1],
                    2.0, OP.mult, OP.mult),
                    reads=["expw", "sumw"], writes=[f"wt8.{t % 4}"], tag=f"{tag}.wt8.{t}")

        def producer_b(g):
            pk = f"psC{g % 2}"
            pap = psC[:, 0:512] if g % 2 == 0 else psC[:, 512:1024]
            for tt in range(4):
                P.op("tensor", lambda q, tt=tt: q.matmul(
                    psB[0:8, 0:128], wt8[:, tt * 8:(tt + 1) * 8], bigi[:, 384:512],
                    start=True, stop=True),
                    reads=[f"wt8.{tt}", "bigi", "psB0"], writes=["psB0"], tag=f"{tag}.wtT.{g}.{tt}")
                P.op("vector", lambda q, tt=tt: q.tensor_copy(
                    wtrow[0:8, tt * 128:(tt + 1) * 128], psB[0:8, 0:128]),
                    reads=["psB0"], writes=["wtrow"], tag=f"{tag}.wtev.{g}.{tt}")
            P.op("tensor", lambda q: q.matmul(psB[:, 512:1024], rep[0:8, :], wtrow[0:8, 0:512],
                                              start=True, stop=True),
                 reads=["rep", "wtrow", "psB1"], writes=["psB1"], tag=f"{tag}.repl.{g}")
            P.op("vector", lambda q: q.tensor_copy(wtrow[:, :], psB[:, 512:1024]),
                 reads=["psB1"], writes=["wexpS"], tag=f"{tag}.wexpev.{g}")
            P.op("vector", lambda q, g=g, pap=pap, pk=pk: q.tensor_tensor(
                axwS[:, (g % 2) * 512:(g % 2 + 1) * 512], pap, wtrow[:, :], OP.mult),
                reads=[pk, "wexpS"], writes=[f"axwS{g % 2}"], tag=f"{tag}.axw.{g}")

        def consumer(g):
            for t in range(g * 4, g * 4 + 4):
                for f in range(2):
                    bk = ("psA0", "psA1")[(t * 2 + f) % 2]
                    for c in range(8):
                        P.op("tensor", lambda q, t=t, f=f, c=c, bk=bk: q.matmul(
                            bank[bk], xnT[:, c * 2048 + t * 128:c * 2048 + (t + 1) * 128],
                            wpool[:, c * 1024 + f * 512:c * 1024 + (f + 1) * 512],
                            start=(c == 0), stop=False),
                            reads=[f"wp{c}", f"xnT.w{g}", bk], writes=[bk],
                            tag=f"{tag}.bs.{t}.{f}.{c}")
                    P.op("tensor", lambda q, t=t, f=f, g=g, bk=bk: q.matmul(
                        bank[bk], axwS[:, (g % 2) * 512 + (t % 4) * 128:(g % 2) * 512 + (t % 4 + 1) * 128],
                        wpool[:, 9216 + f * 512:9216 + (f + 1) * 512],
                        start=False, stop=False),
                        reads=[f"axwS{g % 2}", "wp9", bk], writes=[bk], tag=f"{tag}.cmb.{t}.{f}")
                    P.op("tensor", lambda q, f=f, bk=bk: q.matmul(
                        bank[bk], rows[64:65, 0:128], rows[64:65, 128 + f * 512:128 + (f + 1) * 512],
                        start=False, stop=True),
                        reads=["ones64", "bbrow", bk], writes=[bk], tag=f"{tag}.bsb.{t}.{f}")
                    P.op("vector", lambda q, t=t, f=f, bk=bk: q.tensor_tensor(
                        x[:, t * 1024 + f * 512:t * 1024 + (f + 1) * 512],
                        x[:, t * 1024 + f * 512:t * 1024 + (f + 1) * 512], bank[bk], OP.add),
                        reads=[bk, f"x.{t}"], writes=[f"x.{t}"], tag=f"{tag}.xadd.{t}.{f}")

        def hooks(g):
            if hook is not None:
                for t in range(g * 4, g * 4 + 4):
                    hook(t)

        for g in range(5):
            if g < 4:
                ln_group(ln_rest_tag, g)
                producer(g)
            if g > 0:
                consumer(g - 1)
            if g < 4:
                producer_b(g)
            if g > 0:
                hooks(g - 1)
        if next_loader is not None:
            next_loader()

    subs = [("attn", "a"), ("ffn", None), ("attn", "b"), ("ffn", None)][:n_sub]

    for t in range(NT):
        xin_tile(0, t)

    def make_last_hook(seq):
        def hook(t):
            xout_tile(seq, t)
            if seq == 0:
                xin_tile(1, t)
                ln_tile_stats("s1.0.ln", t)
        return hook

    for seq in range(2):
        for si, (sk, kind) in enumerate(subs):
            tg = f"s{seq}.{si}"

            if si + 1 < len(subs):
                hook = make_ln_hook(f"s{seq}.{si + 1}.ln")
            else:
                hook = make_last_hook(seq)
            nxt = subs[si + 1] if si + 1 < len(subs) else (subs[0] if seq == 0 else None)
            nxt_tag = f"s{seq}.{si + 1}" if si + 1 < len(subs) else f"s{seq + 1}.0"
            nxt2 = subs[si + 2] if si + 2 < len(subs) else (subs[0] if seq == 0 else None)
            if nxt is None:
                nl = None
            elif nxt[0] == "ffn":
                sk2 = nxt2[1] if (nxt2 is not None and nxt2[0] == "attn") else None
                nl = (lambda tgn=nxt_tag, k2=sk2: ffn_load(tgn, stage_kind=k2))
            else:
                nl = (lambda kk=nxt[1], tgn=nxt_tag: attn_load(kk, tgn, alt=True))
            if sk == "attn":
                attn(seq, kind, tg, hook=hook, next_loader=nl,
                     preloaded=not (si == 0 and seq == 0),
                     inline_ln_tag=(tg + ".ln") if (si == 0 and seq == 0) else None,
                     ln_rest_tag=tg + ".ln")
            else:
                ffn(tg, hook=hook, next_loader=nl, preloaded=True, ln_rest_tag=tg + ".ln")

    P.finalize()

    blk = ctx.enter_context(nc.Block())

    def closing(q):
        for t in range(NT):
            q.wait_ge(dma_sems[f"xo{t}"], P.dma_cum[f"xo{t}"])

    P.emit(blk, eng_sems, dma_sems, post={"sync": closing})
    return nc, ctx



def kernel(**inp):
    n_sub = _N_SUB
    xfull = np.ascontiguousarray(np.asarray(inp["L0_attn_a_x"], dtype=np.float32))

    def f32(k):
        return np.asarray(inp[k], dtype=np.float32)

    WqT = _hg_stream(np.ascontiguousarray(f32("L0_attn_a_Wq").T))
    WkT = _hg_stream(np.ascontiguousarray(f32("L0_attn_a_Wk").T))
    WvT = _hg_stream(np.ascontiguousarray(f32("L0_attn_a_Wv").T))
    WoT = _tile2(np.ascontiguousarray(f32("L0_attn_a_Wo").T))
    WqB = _hg_stream(np.ascontiguousarray(f32("L0_attn_b_W_query").transpose(1, 0, 2).reshape(D, D)))
    WkB = _hg_stream(np.ascontiguousarray(f32("L0_attn_b_W_key").transpose(1, 0, 2).reshape(D, D)))
    WvB = _hg_stream(np.ascontiguousarray(f32("L0_attn_b_W_val").transpose(1, 0, 2).reshape(D, D)))
    WoB = _tile2(np.ascontiguousarray(f32("L0_attn_b_W_out").reshape(D, D)))
    baseT = _tile2(np.ascontiguousarray(f32("L0_ffn_base_w").T))
    astT = _tile2(np.ascontiguousarray(f32("L0_ffn_lora_A").transpose(1, 0, 2).reshape(D, 128)))
    bstT = _tile2(np.ascontiguousarray(f32("L0_ffn_lora_B").reshape(128, D)))
    routerT = _tile2(np.ascontiguousarray(f32("L0_ffn_router_w").T))

    bigi = np.zeros((128, 896), dtype=ml_dtypes.bfloat16)
    for p in range(128):
        bigi[p, 384 + p] = 1
    repm = np.zeros((8, 128), dtype=ml_dtypes.bfloat16)
    for e in range(8):
        repm[e, e * 16:(e + 1) * 16] = 1

    common = {
        "wq_a": WqT, "wk_a": WkT, "wv_a": WvT, "wo_a": WoT,
        "wq_b": WqB, "wk_b": WkB, "wv_b": WvB, "wo_b": WoB,
        "base": baseT, "ast": astT, "bst": bstT, "router": routerT,
        "base_b": f32("L0_ffn_base_b").reshape(1, D),
        "router_b": f32("L0_ffn_router_b").reshape(1, 8),
        "bo": f32("L0_attn_a_bo").reshape(1, D),
        "bvrow": f32("L0_attn_a_bv").reshape(1, D),
        "bqt": _bias_tile(f32("L0_attn_a_bq")),
        "bkt": _bias_tile(f32("L0_attn_a_bk")),
        "bvt": _bias_tile(f32("L0_attn_a_bv")),
        "head_mask": f32("L0_attn_a_head_mask").reshape(1, 16),
        "bigi": bigi, "repmat": repm,
    }
    common = {k: np.ascontiguousarray(v) for k, v in common.items()}

    in_maps = []
    for c in range(NCORES):
        m = dict(common)
        m["x"] = np.ascontiguousarray(xfull[2 * c:2 * c + 2])
        in_maps.append(m)

    nc, ctx = build_nc(n_sub)
    try:
        res = run_bass_kernel_spmd(nc, in_maps, core_ids=list(range(NCORES)))
    finally:
        ctx.close()
    out = np.concatenate([res.results[c]["y"] for c in range(NCORES)], axis=0)
    return np.ascontiguousarray(out.astype(np.float32))


# version 23
# speedup vs baseline: 1.0163x; 1.0044x over previous
import contextlib

import numpy as np
import ml_dtypes

import concourse.bass as bass
import concourse.mybir as mybir
from concourse.bass_utils import run_bass_kernel_spmd

F32 = mybir.dt.float32
BF16 = mybir.dt.bfloat16
OP = mybir.AluOpType
AF = mybir.ActivationFunctionType
AX = mybir.AxisListType.X

NCORES = 8
S = 2048
D = 1024
NT = S // 128
NHG = 8
NQG = 4
NKT = 16

_N_SUB = 4


def _tile2(m):
    r, c = m.shape
    return np.ascontiguousarray(m.reshape(r // 128, 128, c).transpose(1, 0, 2).reshape(128, (r // 128) * c))


def _hg_stream(wT):
    return np.ascontiguousarray(np.stack([_tile2(wT[:, hg * 128:(hg + 1) * 128]) for hg in range(8)]))


def _bias_tile(b):
    return np.ascontiguousarray(b.reshape(8, 128).T)



class _Op:
    __slots__ = ("eng", "fn", "deps", "drains", "eidx", "needs_inc", "ord", "is_dma", "sem", "tag")

    def __init__(self, eng, fn, tag=""):
        self.eng = eng
        self.fn = fn
        self.deps = []
        self.drains = []
        self.eidx = -1
        self.needs_inc = False
        self.ord = None
        self.is_dma = False
        self.sem = None
        self.tag = tag


class Prog:
    ENGS = ("tensor", "vector", "scalar", "sync", "gpsimd")
    DRAIN_ENGS = ("vector", "scalar", "gpsimd")

    def __init__(self):
        self.eng_ops = {e: [] for e in self.ENGS}
        self.bufw = {}
        self.bufr = {}
        self.dma_cum = {}

    def _resolve(self, eng, reads, writes):
        deps, drains = [], []

        def add(tok):
            if tok is None:
                return
            if tok[0] == "op" and tok[1].eng == eng and eng in self.DRAIN_ENGS:
                drains.append(tok[1])
            elif tok[0] == "op" and tok[1].eng == eng:
                pass
            else:
                deps.append(tok)

        for k in reads:
            add(self.bufw.get(k))
        for k in writes:
            add(self.bufw.get(k))
            for t in self.bufr.get(k, ()):
                add(t)
        seen, d2 = set(), []
        for t in deps:
            key = id(t[1]) if t[0] == "op" else (t[1], t[2])
            if key not in seen:
                seen.add(key)
                d2.append(t)
        return d2, list({id(o): o for o in drains}.values())

    def op(self, eng, fn, reads=(), writes=(), tag=""):
        o = _Op(eng, fn, tag)
        o.deps, o.drains = self._resolve(eng, reads, writes)
        tok = ("op", o)
        for k in reads:
            self.bufr.setdefault(k, []).append(tok)
        for k in writes:
            self.bufw[k] = tok
            self.bufr[k] = []
        o.eidx = len(self.eng_ops[eng])
        self.eng_ops[eng].append(o)
        return o

    def dma(self, queue, sem_name, fn, reads=(), writes=(), tag=""):
        o = _Op(queue, fn, tag)
        o.is_dma = True
        o.sem = sem_name
        o.deps, o.drains = self._resolve(queue, reads, writes)
        self.dma_cum[sem_name] = self.dma_cum.get(sem_name, 0) + 16
        tok = ("dma", sem_name, self.dma_cum[sem_name])
        for k in reads:
            self.bufr.setdefault(k, []).append(tok)
        for k in writes:
            self.bufw[k] = tok
            self.bufr[k] = []
        o.eidx = len(self.eng_ops[queue])
        self.eng_ops[queue].append(o)
        return o

    def settle(self, sem_name, keys):
        tok = ("dma", sem_name, self.dma_cum[sem_name])
        for k in keys:
            w = self.bufw.get(k)
            if w is not None and w[0] == "dma" and w[1] == sem_name:
                self.bufw[k] = tok
            rl = self.bufr.get(k)
            if rl:
                self.bufr[k] = [tok if (t[0] == "dma" and t[1] == sem_name) else t for t in rl]

    def finalize(self):
        for ops in self.eng_ops.values():
            for o in ops:
                for t in o.deps:
                    if t[0] == "op":
                        t[1].needs_inc = True
        for ops in self.eng_ops.values():
            n = 0
            for o in ops:
                if o.needs_inc:
                    if o.is_dma:
                        raise RuntimeError(f"dma op {o.tag} marked needs_inc")
                    n += 1
                    o.ord = n

    def emit(self, block, eng_sems, dma_sems, post=None):
        prog = self

        def runner(eng_name):
            def run(q):
                waited = {}
                last_drain = -1
                for pos, o in enumerate(prog.eng_ops[eng_name]):
                    if o.drains and max(dd.eidx for dd in o.drains) > last_drain:
                        q.drain()
                        last_drain = pos - 1
                    for t in o.deps:
                        if t[0] == "op":
                            dep = t[1]
                            if dep.ord is None:
                                raise RuntimeError(f"{o.tag} depends on un-inc'd {dep.tag}")
                            sem, val = eng_sems[dep.eng], dep.ord
                        else:
                            sem, val = dma_sems[t[1]], t[2]
                        sid = id(sem)
                        if waited.get(sid, -1) < val:
                            q.wait_ge(sem, val)
                            waited[sid] = val
                    inst = o.fn(q)
                    if o.needs_inc:
                        inst.then_inc(eng_sems[o.eng], 1)
                if post and eng_name in post:
                    post[eng_name](q)
            return run

        block.tensor(runner("tensor"))
        block.vector(runner("vector"))
        block.scalar(runner("scalar"))
        block.sync(runner("sync"))
        block.gpsimd(runner("gpsimd"))



def build_nc(n_sub=4):
    nc = bass.Bass("TRN2", target_bir_lowering=False)
    ctx = contextlib.ExitStack()

    hx = nc.dram_tensor("x", [2, S, D], F32, kind="ExternalInput").ap()
    hy = nc.dram_tensor("y", [2, S, D], F32, kind="ExternalOutput").ap()
    hwq = {k: nc.dram_tensor(f"wq_{k}", [8, 128, 1024], F32, kind="ExternalInput").ap() for k in "ab"}
    hwk = {k: nc.dram_tensor(f"wk_{k}", [8, 128, 1024], F32, kind="ExternalInput").ap() for k in "ab"}
    hwv = {k: nc.dram_tensor(f"wv_{k}", [8, 128, 1024], F32, kind="ExternalInput").ap() for k in "ab"}
    hwo = {k: nc.dram_tensor(f"wo_{k}", [128, 8192], F32, kind="ExternalInput").ap() for k in "ab"}
    hbase = nc.dram_tensor("base", [128, 8192], F32, kind="ExternalInput").ap()
    hast = nc.dram_tensor("ast", [128, 1024], F32, kind="ExternalInput").ap()
    hbst = nc.dram_tensor("bst", [128, 1024], F32, kind="ExternalInput").ap()
    hrouter = nc.dram_tensor("router", [128, 64], F32, kind="ExternalInput").ap()
    hbb = nc.dram_tensor("base_b", [1, D], F32, kind="ExternalInput").ap()
    hrb = nc.dram_tensor("router_b", [1, 8], F32, kind="ExternalInput").ap()
    hbo = nc.dram_tensor("bo", [1, D], F32, kind="ExternalInput").ap()
    hbvr = nc.dram_tensor("bvrow", [1, D], F32, kind="ExternalInput").ap()
    hbqt = nc.dram_tensor("bqt", [128, 8], F32, kind="ExternalInput").ap()
    hbkt = nc.dram_tensor("bkt", [128, 8], F32, kind="ExternalInput").ap()
    hbvt = nc.dram_tensor("bvt", [128, 8], F32, kind="ExternalInput").ap()
    hmask = nc.dram_tensor("head_mask", [1, 16], F32, kind="ExternalInput").ap()
    hbigi = nc.dram_tensor("bigi", [128, 896], BF16, kind="ExternalInput").ap()
    hrep = nc.dram_tensor("repmat", [8, 128], BF16, kind="ExternalInput").ap()
    hrr = [nc.dram_tensor(f"hbm_rr{i}", [2, 2048], BF16, kind="Internal").ap() for i in range(2)]

    def alloc(name, shape, dtype):
        return ctx.enter_context(nc.sbuf_tensor("s_" + name, shape, dtype))

    x = alloc("xbuf", [128, NT * 1024], F32)
    xnT = alloc("xnT", [128, 8 * 2048], BF16)
    aoT = alloc("aoT", [128, 8 * 2048], BF16)
    qt = alloc("qt", [128, 4096], BF16)
    kt_ = alloc("kt", [128, 4096], BF16)
    vaug = alloc("vaug", [128, 2 * NT * 192], BF16)
    pbuf = alloc("pbuf", [128, 2048], BF16)
    wpool = alloc("wpool", [128, 14 * 1024], BF16)
    rexp = alloc("rexp", [128, 2048], BF16)
    rrow = alloc("rrow", [128, 1024], BF16)
    sumsT = alloc("sumsT", [128, 512], F32)
    grid2 = alloc("grid2", [128, 512], BF16)
    rtokb = alloc("rtokb", [128, 16], F32)
    xns = alloc("xns", [128, 1024], BF16)
    bn6 = alloc("bn6", [128, 12], F32)
    mv = alloc("mv", [128, 32], F32)
    lnv = alloc("lnv", [128, 16], F32)
    rstd = alloc("rstd", [128, 16], F32)
    bigi = alloc("bigi", [128, 896], BF16)
    rep = alloc("rep", [128, 128], BF16)
    e2 = alloc("e2", [128, 2], BF16)
    onesr = alloc("onesr", [1, 144], BF16)
    rbrow = onesr[0:1, 128:136]
    maskb = alloc("maskb", [128, 16], F32)
    negbb = alloc("negbb", [128, 16], F32)
    bqt = alloc("bqt", [128, 8], F32)
    bkt = alloc("bkt", [128, 8], F32)
    rows = alloc("rows", [128, 1152], BF16)
    borow = rows[0:1, 128:1152]
    bvrow = rows[32:33, 128:1152]
    bbrow = rows[64:65, 128:1152]
    logb = alloc("logb", [128, 8], F32)
    max8 = alloc("max8", [128, 8], F32)
    negm1 = alloc("negm1", [128, 1], F32)
    expw = alloc("expw", [128, 8], F32)
    keep = alloc("keep", [128, 8], F32)
    sumw = alloc("sumw", [128, 1], F32)
    wt8 = alloc("wt8", [128, 32], BF16)
    wtrow = alloc("wtrow", [128, 512], BF16)
    axwS = alloc("axwS", [128, 1024], BF16)
    mq = alloc("mq", [128, 1], F32)
    mq4 = alloc("mq4", [128, 4], F32)
    mk4 = alloc("mk4", [128, 4], F32)
    mk = alloc("mk", [128, 1], F32)
    tprod = alloc("tprod", [128, 1], BF16)
    nbtmp = alloc("nbtmp", [1, 2], F32)
    nb1 = alloc("nb1", [1, 1], F32)
    nbbf = alloc("nbbf", [1, 1], BF16)
    epsb = alloc("epsb", [128, 1], F32)

    psA = ctx.enter_context(nc.psum_tensor("psA", [128, 1024], F32))
    psB = ctx.enter_context(nc.psum_tensor("psB", [128, 1024], F32))
    psC = ctx.enter_context(nc.psum_tensor("psC", [128, 1024], F32))
    psD = ctx.enter_context(nc.psum_tensor("psD", [128, 512], F32))
    psE = ctx.enter_context(nc.psum_tensor("psE", [128, 512], F32))
    bank = {
        "psA0": psA[:, 0:512], "psA1": psA[:, 512:1024],
        "psB0": psB[:, 0:512], "psB1": psB[:, 512:1024],
        "psC0": psC[:, 0:512], "psC1": psC[:, 512:1024],
        "psD": psD[:, :], "psE": psE[:, :],
    }

    eng_sems = {e: ctx.enter_context(nc.semaphore(f"sem_{e}")) for e in Prog.ENGS}
    dma_names = (["consts", "constg", "ffnw", "wo", "rrout", "rrin"]
                 + [f"ws{i}" for i in range(6)]
                 + [f"xi{t}" for t in range(NT)] + [f"xo{t}" for t in range(NT)])
    dma_sems = {n: ctx.enter_context(nc.semaphore(f"dma_{n}")) for n in dma_names}

    P = Prog()

    def d(queue, sem, out_ap, in_ap, reads=(), writes=(), tag=""):
        def fn(q, out_ap=out_ap, in_ap=in_ap, sem=sem):
            return q.dma_start(out=out_ap, in_=in_ap).then_inc(dma_sems[sem], 16)
        return P.dma(queue, sem, fn, reads=reads, writes=writes, tag=tag)

    d("sync", "consts", bigi[:, :], hbigi, writes=["bigi"], tag="c.bigi")
    d("sync", "consts", rep[0:8, :], hrep, writes=["rep"], tag="c.rep")
    d("sync", "consts", maskb[:, :], hmask.to_broadcast([128, 16]), writes=["maskb"], tag="c.mask")
    d("sync", "consts", bqt[:, :], hbqt, writes=["bqt"], tag="c.bqt")
    d("sync", "consts", bkt[:, :], hbkt, writes=["bkt"], tag="c.bkt")
    d("gpsimd", "constg", borow, hbo, writes=["borow"], tag="c.bo")
    d("gpsimd", "constg", bvrow, hbvr, writes=["bvrow"], tag="c.bvr")
    P.settle("constg", ["borow", "bvrow"])
    P.settle("consts", ["bigi", "rep", "maskb", "bqt", "bkt"])

    P.op("vector", lambda q: q.memset(e2[:, :], 0.0), writes=["e2"], tag="ms.e2a")
    P.op("vector", lambda q: q.memset(e2[0:64, 0:1], 1.0), reads=["e2"], writes=["e2"], tag="ms.e2b")
    P.op("vector", lambda q: q.memset(e2[64:128, 1:2], 1.0), reads=["e2"], writes=["e2"], tag="ms.e2c")
    P.op("vector", lambda q: q.memset(onesr[0:1, 0:128], 1.0), writes=["onesr"], tag="ms.ones")
    P.op("vector", lambda q: q.memset(rows[0:1, 0:128], 1.0), writes=["ones0"], tag="ms.r0")
    P.op("vector", lambda q: q.memset(rows[32:33, 0:128], 1.0), writes=["ones32"], tag="ms.r32")
    P.op("vector", lambda q: q.memset(rows[64:65, 0:128], 1.0), writes=["ones64"], tag="ms.r64")
    P.op("vector", lambda q: q.memset(epsb[:, :], 1e-5), writes=["epsb"], tag="ms.eps")
    P.op("vector", lambda q: q.memset(grid2[:, :], 0.0), writes=["grid2a", "grid2b"], tag="ms.g2")
    for par in range(2):
        for t in range(NT):
            b0 = par * NT * 192 + t * 192
            P.op("vector", lambda q, b=b0: q.memset(vaug[:, b + 64:b + 96], 1.0),
                 writes=[f"vaug{par}.{t}"], tag=f"ms.v1.{par}.{t}")
            P.op("vector", lambda q, b=b0: q.memset(vaug[:, b + 96:b + 128], 0.0),
                 reads=[f"vaug{par}.{t}"], writes=[f"vaug{par}.{t}"], tag=f"ms.v0.{par}.{t}")

    def xin_tile(seq, t):
        def fn(q, t=t, seq=seq):
            return q.dma_start(out=x[:, t * 1024:(t + 1) * 1024],
                               in_=hx[seq, t * 128:(t + 1) * 128, :]).then_inc(dma_sems[f"xi{t}"], 16)
        P.dma("sync", f"xi{t}", fn, writes=[f"x.{t}"], tag=f"xin{seq}.{t}")

    def xout_tile(seq, t):
        def fn(q, t=t, seq=seq):
            return q.dma_start(out=hy[seq, t * 128:(t + 1) * 128, :],
                               in_=x[:, t * 1024:(t + 1) * 1024]).then_inc(dma_sems[f"xo{t}"], 16)
        P.dma("sync", f"xo{t}", fn, reads=[f"x.{t}"], tag=f"xout{seq}.{t}")

    xnT3 = xnT[:, :].rearrange("p (c j) -> p c j", c=8)

    from collections import deque

    def ln_tile_stats(tag, t):
        P.op("vector", lambda q, t=t: q.bn_stats(bn6[:, 0:6], x[:, t * 1024:t * 1024 + 512]),
             reads=[f"x.{t}"], writes=["bn6a"], tag=f"{tag}.bns0.{t}")
        P.op("vector", lambda q, t=t: q.bn_stats(bn6[:, 6:12], x[:, t * 1024 + 512:t * 1024 + 1024]),
             reads=[f"x.{t}"], writes=["bn6b"], tag=f"{tag}.bns1.{t}")
        P.op("vector", lambda q, t=t: q.bn_aggr(mv[:, 2 * t:2 * t + 2], bn6[:, 0:12]),
             reads=["bn6a", "bn6b"], writes=[f"mv.{t}"], tag=f"{tag}.bna.{t}")

    def ln_group(tag, g):
        P.op("scalar", lambda q, g=g: q.activation(lnv[:, 4 * g:4 * g + 4],
                                                   mv[:, 8 * g + 1:8 * g + 8:2],
                                                   AF.Ln, bias=epsb[:, 0:1], scale=1.0),
             reads=[f"mv.{t}" for t in range(g * 4, g * 4 + 4)] + ["epsb"],
             writes=[f"lnv.{g}"], tag=f"{tag}.lnv.{g}")
        P.op("scalar", lambda q, g=g: q.activation(rstd[:, 4 * g:4 * g + 4], lnv[:, 4 * g:4 * g + 4],
                                                   AF.Exp, bias=0.0, scale=-0.5),
             reads=[f"lnv.{g}"], writes=[f"rstd.{g}"], tag=f"{tag}.rstd.{g}")
        for t in range(g * 4, g * 4 + 4):
            P.op("vector", lambda q, t=t: q.tensor_scalar(
                xns[:, :], x[:, t * 1024:(t + 1) * 1024],
                mv[:, 2 * t:2 * t + 1], rstd[:, t:t + 1], OP.subtract, OP.mult),
                reads=[f"x.{t}", f"mv.{t}", f"rstd.{g}"], writes=["xns"],
                tag=f"{tag}.norm.{t}")
            for half in range(2):
                bk = ("psD", "psE")[(t * 2 + half) % 2]
                for c in range(4):
                    cc = half * 4 + c
                    P.op("tensor", lambda q, cc=cc, c=c, bk=bk: q.matmul(
                        bank[bk], xns[:, cc * 128:(cc + 1) * 128],
                        bigi[:, 384 - c * 128:896 - c * 128],
                        start=(c == 0), stop=(c == 3)),
                        reads=["xns", "bigi", bk], writes=[bk], tag=f"{tag}.T.{t}.{cc}")
                P.op("vector", lambda q, t=t, half=half, bk=bk: q.tensor_copy(
                    xnT3[:, 4 * half:4 * half + 4, t * 128:(t + 1) * 128], bank[bk]),
                    reads=[bk], writes=[f"xnT.w{t // 4}"], tag=f"{tag}.Tev.{t}.{half}")

    def ln_xnT(tag):
        for g in range(4):
            for t in range(g * 4, g * 4 + 4):
                ln_tile_stats(tag, t)
            ln_group(tag, g)

    def make_ln_hook(tag):
        def hook(t):
            ln_tile_stats(tag, t)
        return hook

    def attn_load(kind, tag):
        for hg01 in (0, 1):
            s0 = (hg01 % 2) * 3
            for j, (hn, hsrc) in enumerate((("wq", hwq), ("wk", hwk), ("wv", hwv))):
                d("gpsimd", f"ws{s0 + j}", wpool[:, (s0 + j) * 1024:(s0 + j + 1) * 1024],
                  hsrc[kind][hg01], writes=[f"wp{s0 + j}"], tag=f"{tag}.dw{hn}.{hg01}")
        d("gpsimd", "wo", wpool[:, 6 * 1024:14 * 1024], hwo[kind],
          writes=[f"wp{i}" for i in range(6, 14)], tag=f"{tag}.dwo")

    def attn_load_tail(kind, tag):
        for j, (hn, hsrc) in enumerate((("wq", hwq), ("wk", hwk), ("wv", hwv))):
            d("gpsimd", f"ws{3 + j}", wpool[:, (3 + j) * 1024:(4 + j) * 1024],
              hsrc[kind][1], writes=[f"wp{3 + j}"], tag=f"{tag}.dw{hn}.1")

    def attn_prefetch0(kind, tag):
        for j, (hn, hsrc) in enumerate((("wq", hwq), ("wk", hwk), ("wv", hwv))):
            d("gpsimd", f"ws{j}", wpool[:, (11 + j) * 1024:(12 + j) * 1024],
              hsrc[kind][0], writes=[f"wp{11 + j}"], tag=f"{tag}.pf{hn}.0")

    def attn(seq, kind, tag, hook=None, next_loader=None, preloaded=False, inline_ln_tag=None,
             ln_rest_tag=None, hg0_slots=None, wo_deferred=False):
        lcol = 0 if kind == "a" else 8
        fillers = deque()

        def pump(n):
            while n > 0 and fillers:
                f = fillers.popleft()
                if f is None:
                    continue
                f()
                n -= 1

        def flush():
            while fillers:
                f = fillers.popleft()
                if f is not None:
                    f()

        def wslices(hg):
            s0 = (hg % 2) * 3
            for j, (hn, hsrc) in enumerate((("wq", hwq), ("wk", hwk), ("wv", hwv))):
                d("gpsimd", f"ws{s0 + j}", wpool[:, (s0 + j) * 1024:(s0 + j + 1) * 1024],
                  hsrc[kind][hg], writes=[f"wp{s0 + j}"], tag=f"{tag}.dw{hn}.{hg}")

        def emit_qkv(hg, into):
            if hg == 0 and hg0_slots is not None:
                sq, sk_, sv = hg0_slots
            else:
                s0 = (hg % 2) * 3
                sq, sk_, sv = s0, s0 + 1, s0 + 2
            par = hg % 2
            wq_ap = wpool[:, sq * 1024:(sq + 1) * 1024]
            wk_ap = wpool[:, sk_ * 1024:(sk_ + 1) * 1024]
            wv_ap = wpool[:, sv * 1024:(sv + 1) * 1024]
            wqk, wkk, wvk = f"wp{sq}", f"wp{sk_}", f"wp{sv}"
            for g in range(4):
                for mi, (wap, wkey, dstb, dkey, bt, bkey) in enumerate((
                        (wq_ap, wqk, qt, "qt", bqt if kind == "a" else None, "bqt"),
                        (wk_ap, wkk, kt_, "kt", bkt if kind == "a" else None, "bkt"))):
                    bk = ("psD", "psE")[(2 * g + mi) % 2]
                    for c in range(8):
                        into.append(lambda wap=wap, c=c, g=g, bk=bk, wkey=wkey: P.op(
                            "tensor", lambda q: q.matmul(
                                bank[bk], wap[:, c * 128:(c + 1) * 128],
                                xnT[:, c * 2048 + g * 512:c * 2048 + (g + 1) * 512],
                                start=(c == 0), stop=(c == 7)),
                            reads=[wkey, f"xnT.w{g}", bk], writes=[bk],
                            tag=f"{tag}.qk.{hg}.{g}.{c}"))
                    if bt is not None:
                        into.append(lambda dstb=dstb, g=g, bk=bk, bt=bt, hg=hg, par=par,
                                    dkey=dkey, bkey=bkey: P.op(
                            "vector", lambda q: q.tensor_scalar(
                                dstb[:, par * 2048 + g * 512:par * 2048 + (g + 1) * 512],
                                bank[bk], bt[:, hg:hg + 1], None, OP.add),
                            reads=[bk, bkey], writes=[f"{dkey}{par}.{g}"],
                            tag=f"{tag}.qkev.{hg}.{g}"))
                    else:
                        into.append(lambda dstb=dstb, g=g, bk=bk, par=par, dkey=dkey: P.op(
                            "vector", lambda q: q.tensor_copy(
                                dstb[:, par * 2048 + g * 512:par * 2048 + (g + 1) * 512],
                                bank[bk]),
                            reads=[bk], writes=[f"{dkey}{par}.{g}"],
                            tag=f"{tag}.qkev.{hg}.{g}"))
                if seq == 0:
                    into.append(lambda par=par, g=g: P.op(
                        "vector", lambda q: q.tensor_reduce(
                            mq4[:, g:g + 1], qt[:, par * 2048 + g * 512:par * 2048 + (g + 1) * 512],
                            AX, OP.max, apply_absolute_value=True),
                        reads=[f"qt{par}.{g}"], writes=[f"mq4.{g}"], tag=f"{tag}.mqp.{hg}.{g}"))
                    into.append(lambda par=par, g=g: P.op(
                        "vector", lambda q: q.tensor_reduce(
                            mk4[:, g:g + 1], kt_[:, par * 2048 + g * 512:par * 2048 + (g + 1) * 512],
                            AX, OP.max, apply_absolute_value=True),
                        reads=[f"kt{par}.{g}"], writes=[f"mk4.{g}"], tag=f"{tag}.mkp.{hg}.{g}"))
                for t in range(g * 4, g * 4 + 4):
                    bk = ("psD", "psE")[t % 2]
                    for c in range(8):
                        into.append(lambda t=t, c=c, wv_ap=wv_ap, bk=bk, wvk=wvk, kind=kind: P.op(
                            "tensor", lambda q: q.matmul(
                                bank[bk][:, 0:128], xnT[:, c * 2048 + t * 128:c * 2048 + (t + 1) * 128],
                                wv_ap[:, c * 128:(c + 1) * 128],
                                start=(c == 0), stop=(c == 7 and kind != "a")),
                            reads=[wvk, f"xnT.w{t // 4}", bk], writes=[bk],
                            tag=f"{tag}.v.{hg}.{t}.{c}"))
                    if kind == "a":
                        into.append(lambda hg=hg, bk=bk, t=t: P.op(
                            "tensor", lambda q: q.matmul(
                                bank[bk][:, 0:128], rows[32:33, 0:128],
                                rows[32:33, 128 + hg * 128:128 + (hg + 1) * 128],
                                start=False, stop=True),
                            reads=["ones32", "bvrow", bk], writes=[bk], tag=f"{tag}.vb.{hg}.{t}"))

                    def vev_mk(t=t, bk=bk, par=par):
                        def vev(q):
                            dst = bass.AP(tensor=vaug[:, 0:1].tensor, offset=par * NT * 192 + t * 192,
                                          ap=[[2 * NT * 192, 128], [128, 2], [1, 64]])
                            return q.tensor_copy(dst, bank[bk][:, 0:128])
                        return vev
                    into.append(lambda t=t, bk=bk, par=par: P.op(
                        "vector", vev_mk(t, bk, par), reads=[bk], writes=[f"vaug{par}.{t}"],
                        tag=f"{tag}.vev.{hg}.{t}"))
                into.append(None)
            if seq == 0:
                into.append(lambda: P.op(
                    "vector", lambda q: q.reduce_max(mq[:, :], mq4[:, :], AX),
                    reads=[f"mq4.{g}" for g in range(4)], writes=["mq"], tag=f"{tag}.mq.{hg}"))
                into.append(lambda: P.op(
                    "vector", lambda q: q.reduce_max(mk[:, :], mk4[:, :], AX),
                    reads=[f"mk4.{g}" for g in range(4)], writes=["mk"], tag=f"{tag}.mk.{hg}"))
                into.append(lambda: P.op(
                    "vector", lambda q: q.tensor_mul(tprod[:, :], mq[:, :], mk[:, :]),
                    reads=["mq", "mk"], writes=["tprod"], tag=f"{tag}.tp.{hg}"))
                into.append(lambda: P.op(
                    "tensor", lambda q: q.matmul(psE[0:1, 0:2], tprod[:, :], e2[:, :],
                                                 start=True, stop=True),
                    reads=["tprod", "e2", "psE"], writes=["psE"], tag=f"{tag}.bsum.{hg}"))
                into.append(lambda: P.op(
                    "vector", lambda q: q.tensor_scalar(nbtmp[0:1, :], psE[0:1, 0:2],
                                                        480.0, None, OP.min),
                    reads=["psE"], writes=["nbtmp"], tag=f"{tag}.bclamp.{hg}"))
                into.append(lambda: P.op(
                    "vector", lambda q: q.reduce_max(nb1[0:1, :], nbtmp[0:1, :], AX),
                    reads=["nbtmp"], writes=["nb1"], tag=f"{tag}.bmax.{hg}"))
                into.append(lambda: P.op(
                    "vector", lambda q: q.tensor_copy(nbbf[0:1, :], nb1[0:1, :]),
                    reads=["nb1"], writes=["nbbf"], tag=f"{tag}.bbf.{hg}"))
                into.append(lambda: P.op(
                    "tensor", lambda q: q.matmul(psE[:, 8:9], onesr[0:1, 0:128], nbbf[0:1, :],
                                                 start=True, stop=True),
                    reads=["nbbf", "onesr", "psE"], writes=["psE"], tag=f"{tag}.bbc.{hg}"))
                into.append(lambda hg=hg: P.op(
                    "vector", lambda q: q.tensor_scalar(
                        negbb[:, lcol + hg:lcol + hg + 1], psE[:, 8:9], -0.125, None, OP.mult),
                    reads=["psE"], writes=[f"negbb.{lcol + hg}"], tag=f"{tag}.bneg.{hg}"))

        def emit_wo_pass(into, cs, with_bias, passtag):
            c0, c1 = cs
            for t in range(NT):
                for f in range(2):
                    bk = ("psD", "psE")[(t * 2 + f) % 2]
                    for c in range(c0, c1):
                        laststop = (c == c1 - 1 and not with_bias)
                        into.append(lambda t=t, f=f, c=c, bk=bk, laststop=laststop, c0=c0: P.op(
                            "tensor", lambda q: q.matmul(
                                bank[bk], aoT[:, c * 2048 + t * 128:c * 2048 + (t + 1) * 128],
                                wpool[:, 6 * 1024 + c * 1024 + f * 512:
                                       6 * 1024 + c * 1024 + (f + 1) * 512],
                                start=(c == c0), stop=laststop),
                            reads=[f"aoT.{c}", f"wp{6 + c}", bk], writes=[bk],
                            tag=f"{tag}.{passtag}.{t}.{f}.{c}"))
                    if with_bias:
                        into.append(lambda f=f, bk=bk, t=t: P.op(
                            "tensor", lambda q: q.matmul(
                                bank[bk], rows[0:1, 0:128], rows[0:1, 128 + f * 512:128 + (f + 1) * 512],
                                start=False, stop=True),
                            reads=["ones0", "borow", bk], writes=[bk],
                            tag=f"{tag}.{passtag}b.{t}.{f}"))
                    into.append(lambda t=t, f=f, bk=bk: P.op(
                        "vector", lambda q: q.tensor_tensor(
                            x[:, t * 1024 + f * 512:t * 1024 + (f + 1) * 512],
                            x[:, t * 1024 + f * 512:t * 1024 + (f + 1) * 512],
                            bank[bk], OP.add),
                        reads=[bk, f"x.{t}"], writes=[f"x.{t}"],
                        tag=f"{tag}.{passtag}add.{t}.{f}"))

        def emit_scores(hg, mid=()):
            par = hg % 2
            qo = par * 2048
            for qg in range(NQG):
                if qg == 2:
                    for m in mid:
                        m()
                def emit_st(kt):
                    kpar = kt % 2
                    ps = psA if kpar == 0 else psB
                    sk0, sk1 = (("psA0", "psA1") if kpar == 0 else ("psB0", "psB1"))
                    P.op("tensor", lambda q, kt=kt, qg=qg, ps=ps, qo=qo: q.matmul(
                        ps[:, 0:512], kt_[0:64, qo + kt * 128:qo + (kt + 1) * 128],
                        qt[0:64, qo + qg * 512:qo + (qg + 1) * 512],
                        start=True, stop=True, tile_position=(0, 0)),
                        reads=[f"kt{par}.{kt // 4}", f"qt{par}.{qg}", sk0], writes=[sk0],
                        tag=f"{tag}.s0.{hg}.{qg}.{kt}")
                    P.op("tensor", lambda q, kt=kt, qg=qg, ps=ps, qo=qo: q.matmul(
                        ps[:, 512:1024], kt_[64:128, qo + kt * 128:qo + (kt + 1) * 128],
                        qt[64:128, qo + qg * 512:qo + (qg + 1) * 512],
                        start=True, stop=True, tile_position=(64, 0)),
                        reads=[f"kt{par}.{kt // 4}", f"qt{par}.{qg}", sk1], writes=[sk1],
                        tag=f"{tag}.s1.{hg}.{qg}.{kt}")
                    P.op("scalar", lambda q, ps=ps, kpar=kpar, hg=hg: q.activation(
                        pbuf[:, kpar * 1024:(kpar + 1) * 1024], ps[:, 0:1024], AF.Exp,
                        bias=negbb[:, lcol + hg:lcol + hg + 1], scale=0.125),
                        reads=[sk0, sk1, f"negbb.{lcol + hg}"], writes=[f"pbuf.{kpar}"],
                        tag=f"{tag}.exp.{hg}.{qg}.{kt}")

                def emit_pv(kt):
                    kpar = kt % 2
                    P.op("tensor", lambda q, kt=kt, kpar=kpar, par=par: q.matmul(
                        psC[0:128, 0:512],
                        vaug[:, par * NT * 192 + kt * 192:par * NT * 192 + kt * 192 + 128],
                        pbuf[:, kpar * 1024:kpar * 1024 + 512],
                        start=(kt == 0), stop=(kt == NKT - 1)),
                        reads=[f"pbuf.{kpar}", f"vaug{par}.{kt}", "psC0"], writes=["psC0"],
                        tag=f"{tag}.pv0.{hg}.{qg}.{kt}")
                    P.op("tensor", lambda q, kt=kt, kpar=kpar, par=par: q.matmul(
                        psC[0:128, 512:1024],
                        vaug[:, par * NT * 192 + kt * 192 + 64:par * NT * 192 + (kt + 1) * 192],
                        pbuf[:, kpar * 1024 + 512:kpar * 1024 + 1024],
                        start=(kt == 0), stop=(kt == NKT - 1)),
                        reads=[f"pbuf.{kpar}", f"vaug{par}.{kt}", "psC1"], writes=["psC1"],
                        tag=f"{tag}.pv1.{hg}.{qg}.{kt}")

                for kt in range(NKT + 1):
                    if kt < NKT:
                        emit_st(kt)
                    if kt > 0:
                        emit_pv(kt - 1)
                    pump(3 if kt % 4 == 3 else 4)
                P.op("scalar", lambda q, hg=hg, qg=qg: q.copy(
                    aoT[0:64, hg * 2048 + qg * 512:hg * 2048 + (qg + 1) * 512], psC[0:64, 0:512]),
                    reads=["psC0"], writes=[f"aoT.{hg}"], tag=f"{tag}.aev0.{hg}.{qg}")
                P.op("scalar", lambda q, hg=hg, qg=qg: q.copy(
                    aoT[64:128, hg * 2048 + qg * 512:hg * 2048 + (qg + 1) * 512],
                    psC[64:128, 512:1024]),
                    reads=["psC1"], writes=[f"aoT.{hg}"], tag=f"{tag}.aev1.{hg}.{qg}")
                P.op("vector", lambda q: q.transpose(sumsT[64:96, 0:512], psC[64:96, 0:512]),
                     reads=["psC0"], writes=["sumsT0"], tag=f"{tag}.st0.{hg}.{qg}")
                P.op("vector", lambda q: q.transpose(sumsT[0:32, 0:512], psC[0:32, 512:1024]),
                     reads=["psC1"], writes=["sumsT1"], tag=f"{tag}.st1.{hg}.{qg}")
                P.op("vector", lambda q: q.reciprocal(rtokb[64:96, 0:16], sumsT[64:96, 0:512:32]),
                     reads=["sumsT0"], writes=["rtok0"], tag=f"{tag}.rc0.{hg}.{qg}")
                P.op("vector", lambda q: q.reciprocal(rtokb[0:32, 0:16], sumsT[0:32, 0:512:32]),
                     reads=["sumsT1"], writes=["rtok1"], tag=f"{tag}.rc1.{hg}.{qg}")
                if kind == "a":
                    P.op("vector", lambda q, hg=hg: q.tensor_scalar(
                        rtokb[64:96, 0:16], rtokb[64:96, 0:16],
                        maskb[64:96, 2 * hg:2 * hg + 1], None, OP.mult),
                        reads=["rtok0", "maskb"], writes=["rtok0"], tag=f"{tag}.msk0.{hg}.{qg}")
                    P.op("vector", lambda q, hg=hg: q.tensor_scalar(
                        rtokb[0:32, 0:16], rtokb[0:32, 0:16],
                        maskb[0:32, 2 * hg + 1:2 * hg + 2], None, OP.mult),
                        reads=["rtok1", "maskb"], writes=["rtok1"], tag=f"{tag}.msk1.{hg}.{qg}")
                P.op("vector", lambda q: q.tensor_copy(grid2[64:96, 0:512:32], rtokb[64:96, 0:16]),
                     reads=["rtok0"], writes=["grid2a"], tag=f"{tag}.g2a.{hg}.{qg}")
                P.op("vector", lambda q: q.tensor_copy(grid2[0:32, 0:512:32], rtokb[0:32, 0:16]),
                     reads=["rtok1"], writes=["grid2b"], tag=f"{tag}.g2b.{hg}.{qg}")
                P.op("vector", lambda q, qg=qg: q.transpose(
                    rrow[64:96, (qg % 2) * 512:(qg % 2 + 1) * 512], grid2[64:96, 0:512]),
                    reads=["grid2a"], writes=["rrow"], tag=f"{tag}.g3a.{hg}.{qg}")
                P.op("vector", lambda q, qg=qg: q.transpose(
                    rrow[0:32, (qg % 2) * 512:(qg % 2 + 1) * 512], grid2[0:32, 0:512]),
                    reads=["grid2b"], writes=["rrow"], tag=f"{tag}.g3b.{hg}.{qg}")
                pump(4)
                if qg % 2 == 1 and not (hg == 7 and qg == 3):
                    hh = qg // 2
                    slot = hg % 2
                    d("sync", "rrout", hrr[slot][0:1, hh * 1024:(hh + 1) * 1024],
                      rrow[64:65, 0:1024],
                      reads=["rrow"], writes=[f"hrr{slot}"], tag=f"{tag}.rro0.{hg}.{hh}")
                    d("sync", "rrout", hrr[slot][1:2, hh * 1024:(hh + 1) * 1024],
                      rrow[0:1, 0:1024],
                      reads=["rrow"], writes=[f"hrr{slot}"], tag=f"{tag}.rro1.{hg}.{hh}")
                if hg == 7 and qg == 2:
                    slot = hg % 2
                    d("sync", "rrin", rexp[0:64, 0:1024],
                      hrr[slot][0:1, 0:1024].to_broadcast([64, 1024]),
                      reads=[f"hrr{slot}"], writes=["rexpb"], tag=f"{tag}.rri0h.{hg}")
                    d("sync", "rrin", rexp[64:128, 0:1024],
                      hrr[slot][1:2, 0:1024].to_broadcast([64, 1024]),
                      reads=[f"hrr{slot}"], writes=["rexpb"], tag=f"{tag}.rri1h.{hg}")
            slot = hg % 2
            if hg < 7:
                d("sync", "rrin", rexp[0:64, :], hrr[slot][0:1, :].to_broadcast([64, 2048]),
                  reads=[f"hrr{slot}"], writes=["rexpb"], tag=f"{tag}.rri0.{hg}")
                d("sync", "rrin", rexp[64:128, :], hrr[slot][1:2, :].to_broadcast([64, 2048]),
                  reads=[f"hrr{slot}"], writes=["rexpb"], tag=f"{tag}.rri1.{hg}")
            else:
                for qg2 in (2, 3):
                    sl = qg2 % 2
                    ps = (psA, psB)[qg2 - 2]
                    pk0, pk1 = (("psA0", "psA1"), ("psB0", "psB1"))[qg2 - 2]
                    P.op("tensor", lambda q, ps=ps, sl=sl: q.matmul(
                        ps[0:64, 0:512], rows[64:65, 0:64], rrow[64:65, sl * 512:(sl + 1) * 512],
                        start=True, stop=True),
                        reads=["rrow", "ones64", pk0], writes=[pk0], tag=f"{tag}.rbcA.{qg2}")
                    P.op("tensor", lambda q, ps=ps, sl=sl: q.matmul(
                        ps[64:128, 512:1024], onesr[0:1, 0:64], rrow[0:1, sl * 512:(sl + 1) * 512],
                        start=True, stop=True, tile_position=(0, 64)),
                        reads=["rrow", "onesr", pk1], writes=[pk1], tag=f"{tag}.rbcB.{qg2}")
                    P.op("vector", lambda q, ps=ps, qg2=qg2: q.tensor_copy(
                        rexp[0:64, qg2 * 512:(qg2 + 1) * 512], ps[0:64, 0:512]),
                        reads=[pk0], writes=["rexpb"], tag=f"{tag}.rbevA.{qg2}")
                    P.op("vector", lambda q, ps=ps, qg2=qg2: q.tensor_copy(
                        rexp[64:128, qg2 * 512:(qg2 + 1) * 512], ps[64:128, 512:1024]),
                        reads=[pk1], writes=["rexpb"], tag=f"{tag}.rbevB.{qg2}")

            def nrm(hg=hg):
                P.op("vector", lambda q, hg=hg: q.tensor_tensor(
                    aoT[:, hg * 2048:(hg + 1) * 2048], aoT[:, hg * 2048:(hg + 1) * 2048],
                    rexp[:, :], OP.mult),
                    reads=[f"aoT.{hg}", "rexpb"], writes=[f"aoT.{hg}"], tag=f"{tag}.nrm.{hg}")
            return nrm

        if not preloaded:
            attn_load(kind, tag)
        direct = deque()
        emit_qkv(0, direct)
        for g in range(4):
            if inline_ln_tag is not None:
                for t in range(g * 4, g * 4 + 4):
                    ln_tile_stats(inline_ln_tag, t)
            ln_group(ln_rest_tag, g)
            while direct:
                f = direct.popleft()
                if f is None:
                    break
                f()
        while direct:
            f = direct.popleft()
            if f is not None:
                f()
        if wo_deferred:
            d("gpsimd", "wo", wpool[:, 6 * 1024:14 * 1024], hwo[kind],
              writes=[f"wp{i}" for i in range(6, 14)], tag=f"{tag}.dwo")
        pending_nrm = None
        for hg in range(NHG):
            if hg + 2 < NHG:
                wslices(hg + 2)
            if hg + 1 < NHG:
                emit_qkv(hg + 1, fillers)
            else:
                emit_wo_pass(fillers, (0, 6), False, "wo1")
            pending_nrm = emit_scores(hg, mid=((pending_nrm,) if pending_nrm else ()))
            flush()
        pending_nrm()
        if next_loader is not None:
            next_loader()
        tail = deque()
        emit_wo_pass(tail, (6, 8), kind == "a", "wo2")
        per_tile = len(tail) // NT
        for t in range(NT):
            for _ in range(per_tile):
                tail.popleft()()
            if hook is not None:
                hook(t)
        while tail:
            tail.popleft()()

    def ffn_load(tag):
        d("gpsimd", "ffnw", wpool[:, 8192:9216], hast, writes=["wp8"], tag=f"{tag}.dast")
        d("gpsimd", "ffnw", wpool[:, 9216:10240], hbst, writes=["wp9"], tag=f"{tag}.dbst")
        d("gpsimd", "ffnw", wpool[:, 10240:10304], hrouter, writes=["wp10"], tag=f"{tag}.drt")
        d("gpsimd", "ffnw", bbrow, hbb, writes=["bbrow"], tag=f"{tag}.dbb")
        d("gpsimd", "ffnw", rbrow, hrb, writes=["rbrow"], tag=f"{tag}.drb")
        d("gpsimd", "ffnw", wpool[:, 0:4096], hbase[:, 0:4096],
          writes=[f"wp{i}" for i in range(4)], tag=f"{tag}.dbase0")
        d("gpsimd", "ffnw", wpool[:, 4096:8192], hbase[:, 4096:8192],
          writes=[f"wp{i}" for i in range(4, 8)], tag=f"{tag}.dbase1")
        P.settle("ffnw", [f"wp{i}" for i in range(8)] + ["wp8", "wp9", "wp10", "bbrow", "rbrow"])

    def ffn(tag, hook=None, preloaded=False, next_loader=None, ln_rest_tag=None, prefetch=None):
        if not preloaded:
            ffn_load(tag)
        if prefetch is not None:
            prefetch()

        def producer(g):
            pk = f"psC{g % 2}"
            pap = psC[:, 0:512] if g % 2 == 0 else psC[:, 512:1024]
            for c in range(8):
                P.op("tensor", lambda q, c=c, g=g, pap=pap: q.matmul(
                    pap, wpool[:, 8192 + c * 128:8192 + (c + 1) * 128],
                    xnT[:, c * 2048 + g * 512:c * 2048 + (g + 1) * 512],
                    start=(c == 0), stop=(c == 7)),
                    reads=["wp8", f"xnT.w{g}", pk], writes=[pk], tag=f"{tag}.ax.{g}.{c}")
            for t in range(g * 4, g * 4 + 4):
                bk = ("psD", "psE")[t % 2]
                for c in range(8):
                    P.op("tensor", lambda q, t=t, c=c, bk=bk: q.matmul(
                        bank[bk][:, 0:8], xnT[:, c * 2048 + t * 128:c * 2048 + (t + 1) * 128],
                        wpool[:, 10240 + c * 8:10240 + (c + 1) * 8],
                        start=(c == 0), stop=False),
                        reads=["wp10", f"xnT.w{g}", bk], writes=[bk], tag=f"{tag}.lg.{t}.{c}")
                P.op("tensor", lambda q, bk=bk: q.matmul(
                    bank[bk][:, 0:8], onesr[0:1, 0:128], onesr[0:1, 128:136],
                    start=False, stop=True),
                    reads=["onesr", "rbrow", bk], writes=[bk], tag=f"{tag}.lgb.{t}")
                P.op("vector", lambda q, bk=bk: q.tensor_copy(logb[:, :], bank[bk][:, 0:8]),
                     reads=[bk], writes=["logb"], tag=f"{tag}.lev.{t}")
                P.op("vector", lambda q: q.max(out=max8[:, :], in_=logb[:, :]),
                     reads=["logb"], writes=["max8"], tag=f"{tag}.max8.{t}")
                P.op("vector", lambda q: q.tensor_scalar(keep[:, :], logb[:, :], max8[:, 1:2],
                                                         None, OP.is_ge),
                     reads=["logb", "max8"], writes=["keep"], tag=f"{tag}.keep.{t}")
                P.op("vector", lambda q: q.tensor_scalar(negm1[:, :], max8[:, 0:1], -1.0,
                                                         None, OP.mult),
                     reads=["max8"], writes=["negm1"], tag=f"{tag}.nm1.{t}")
                P.op("scalar", lambda q: q.activation(expw[:, :], logb[:, :], AF.Exp,
                                                      bias=negm1[:, 0:1], scale=1.0),
                     reads=["logb", "negm1"], writes=["expw"], tag=f"{tag}.rexp.{t}")
                P.op("vector", lambda q: q.tensor_mul(expw[:, :], expw[:, :], keep[:, :]),
                     reads=["expw", "keep"], writes=["expw"], tag=f"{tag}.emask.{t}")
                P.op("vector", lambda q: q.reduce_sum(sumw[:, :], expw[:, :], AX),
                     reads=["expw"], writes=["sumw"], tag=f"{tag}.esum.{t}")
                P.op("vector", lambda q: q.reciprocal(sumw[:, :], sumw[:, :]),
                     reads=["sumw"], writes=["sumw"], tag=f"{tag}.erec.{t}")
                P.op("vector", lambda q, t=t: q.tensor_scalar(
                    wt8[:, (t % 4) * 8:(t % 4) * 8 + 8], expw[:, :], sumw[:, 0:1],
                    2.0, OP.mult, OP.mult),
                    reads=["expw", "sumw"], writes=[f"wt8.{t % 4}"], tag=f"{tag}.wt8.{t}")

        def producer_b(g):
            pk = f"psC{g % 2}"
            pap = psC[:, 0:512] if g % 2 == 0 else psC[:, 512:1024]
            for tt in range(4):
                P.op("tensor", lambda q, tt=tt: q.matmul(
                    psB[0:8, 0:128], wt8[:, tt * 8:(tt + 1) * 8], bigi[:, 384:512],
                    start=True, stop=True),
                    reads=[f"wt8.{tt}", "bigi", "psB0"], writes=["psB0"], tag=f"{tag}.wtT.{g}.{tt}")
                P.op("vector", lambda q, tt=tt: q.tensor_copy(
                    wtrow[0:8, tt * 128:(tt + 1) * 128], psB[0:8, 0:128]),
                    reads=["psB0"], writes=["wtrow"], tag=f"{tag}.wtev.{g}.{tt}")
            P.op("tensor", lambda q: q.matmul(psB[:, 512:1024], rep[0:8, :], wtrow[0:8, 0:512],
                                              start=True, stop=True),
                 reads=["rep", "wtrow", "psB1"], writes=["psB1"], tag=f"{tag}.repl.{g}")
            P.op("vector", lambda q: q.tensor_copy(wtrow[:, :], psB[:, 512:1024]),
                 reads=["psB1"], writes=["wexpS"], tag=f"{tag}.wexpev.{g}")
            P.op("vector", lambda q, g=g, pap=pap, pk=pk: q.tensor_tensor(
                axwS[:, (g % 2) * 512:(g % 2 + 1) * 512], pap, wtrow[:, :], OP.mult),
                reads=[pk, "wexpS"], writes=[f"axwS{g % 2}"], tag=f"{tag}.axw.{g}")

        def consumer(g):
            for t in range(g * 4, g * 4 + 4):
                for f in range(2):
                    bk = ("psA0", "psA1")[(t * 2 + f) % 2]
                    for c in range(8):
                        P.op("tensor", lambda q, t=t, f=f, c=c, bk=bk: q.matmul(
                            bank[bk], xnT[:, c * 2048 + t * 128:c * 2048 + (t + 1) * 128],
                            wpool[:, c * 1024 + f * 512:c * 1024 + (f + 1) * 512],
                            start=(c == 0), stop=False),
                            reads=[f"wp{c}", f"xnT.w{g}", bk], writes=[bk],
                            tag=f"{tag}.bs.{t}.{f}.{c}")
                    P.op("tensor", lambda q, t=t, f=f, g=g, bk=bk: q.matmul(
                        bank[bk], axwS[:, (g % 2) * 512 + (t % 4) * 128:(g % 2) * 512 + (t % 4 + 1) * 128],
                        wpool[:, 9216 + f * 512:9216 + (f + 1) * 512],
                        start=False, stop=False),
                        reads=[f"axwS{g % 2}", "wp9", bk], writes=[bk], tag=f"{tag}.cmb.{t}.{f}")
                    P.op("tensor", lambda q, f=f, bk=bk: q.matmul(
                        bank[bk], rows[64:65, 0:128], rows[64:65, 128 + f * 512:128 + (f + 1) * 512],
                        start=False, stop=True),
                        reads=["ones64", "bbrow", bk], writes=[bk], tag=f"{tag}.bsb.{t}.{f}")
                    P.op("vector", lambda q, t=t, f=f, bk=bk: q.tensor_tensor(
                        x[:, t * 1024 + f * 512:t * 1024 + (f + 1) * 512],
                        x[:, t * 1024 + f * 512:t * 1024 + (f + 1) * 512], bank[bk], OP.add),
                        reads=[bk, f"x.{t}"], writes=[f"x.{t}"], tag=f"{tag}.xadd.{t}.{f}")

        def hooks(g):
            if hook is not None:
                for t in range(g * 4, g * 4 + 4):
                    hook(t)

        for g in range(5):
            if g < 4:
                ln_group(ln_rest_tag, g)
                producer(g)
            if g > 0:
                consumer(g - 1)
            if g < 4:
                producer_b(g)
            if g > 0:
                hooks(g - 1)
        if next_loader is not None:
            next_loader()

    subs = [("attn", "a"), ("ffn", None), ("attn", "b"), ("ffn", None)][:n_sub]

    for t in range(NT):
        xin_tile(0, t)

    def make_last_hook(seq):
        def hook(t):
            xout_tile(seq, t)
            if seq == 0:
                xin_tile(1, t)
                ln_tile_stats("s1.0.ln", t)
        return hook

    sched = []
    for seq in range(2):
        for si, (sk, kind) in enumerate(subs):
            sched.append((seq, si, sk, kind, f"s{seq}.{si}"))

    for idx, (seq, si, sk, kind, tg) in enumerate(sched):
        if si + 1 < len(subs):
            hook = make_ln_hook(f"s{seq}.{si + 1}.ln")
        else:
            hook = make_last_hook(seq)
        nxt = sched[idx + 1] if idx + 1 < len(sched) else None
        pf = None
        if nxt is None:
            nl = None
        elif nxt[2] == "ffn":
            nl = (lambda tgn=nxt[4]: ffn_load(tgn))
        elif sk == "ffn":
            nl = (lambda kk=nxt[3], tgn=nxt[4]: attn_load_tail(kk, tgn))
            pf = (lambda kk=nxt[3], tgn=nxt[4]: attn_prefetch0(kk, tgn))
        else:
            nl = (lambda kk=nxt[3], tgn=nxt[4]: attn_load(kk, tgn))
        if sk == "attn":
            boundary = idx > 0 and sched[idx - 1][2] == "ffn"
            attn(seq, kind, tg, hook=hook, next_loader=nl,
                 preloaded=idx > 0,
                 inline_ln_tag=(tg + ".ln") if idx == 0 else None,
                 ln_rest_tag=tg + ".ln",
                 hg0_slots=(11, 12, 13) if boundary else None,
                 wo_deferred=boundary)
        else:
            ffn(tg, hook=hook, next_loader=nl, preloaded=True, ln_rest_tag=tg + ".ln",
                prefetch=pf)

    P.finalize()

    blk = ctx.enter_context(nc.Block())

    def closing(q):
        for t in range(NT):
            q.wait_ge(dma_sems[f"xo{t}"], P.dma_cum[f"xo{t}"])

    P.emit(blk, eng_sems, dma_sems, post={"sync": closing})
    return nc, ctx



def kernel(**inp):
    n_sub = _N_SUB
    xfull = np.ascontiguousarray(np.asarray(inp["L0_attn_a_x"], dtype=np.float32))

    def f32(k):
        return np.asarray(inp[k], dtype=np.float32)

    WqT = _hg_stream(np.ascontiguousarray(f32("L0_attn_a_Wq").T))
    WkT = _hg_stream(np.ascontiguousarray(f32("L0_attn_a_Wk").T))
    WvT = _hg_stream(np.ascontiguousarray(f32("L0_attn_a_Wv").T))
    WoT = _tile2(np.ascontiguousarray(f32("L0_attn_a_Wo").T))
    WqB = _hg_stream(np.ascontiguousarray(f32("L0_attn_b_W_query").transpose(1, 0, 2).reshape(D, D)))
    WkB = _hg_stream(np.ascontiguousarray(f32("L0_attn_b_W_key").transpose(1, 0, 2).reshape(D, D)))
    WvB = _hg_stream(np.ascontiguousarray(f32("L0_attn_b_W_val").transpose(1, 0, 2).reshape(D, D)))
    WoB = _tile2(np.ascontiguousarray(f32("L0_attn_b_W_out").reshape(D, D)))
    baseT = _tile2(np.ascontiguousarray(f32("L0_ffn_base_w").T))
    astT = _tile2(np.ascontiguousarray(f32("L0_ffn_lora_A").transpose(1, 0, 2).reshape(D, 128)))
    bstT = _tile2(np.ascontiguousarray(f32("L0_ffn_lora_B").reshape(128, D)))
    routerT = _tile2(np.ascontiguousarray(f32("L0_ffn_router_w").T))

    bigi = np.zeros((128, 896), dtype=ml_dtypes.bfloat16)
    for p in range(128):
        bigi[p, 384 + p] = 1
    repm = np.zeros((8, 128), dtype=ml_dtypes.bfloat16)
    for e in range(8):
        repm[e, e * 16:(e + 1) * 16] = 1

    common = {
        "wq_a": WqT, "wk_a": WkT, "wv_a": WvT, "wo_a": WoT,
        "wq_b": WqB, "wk_b": WkB, "wv_b": WvB, "wo_b": WoB,
        "base": baseT, "ast": astT, "bst": bstT, "router": routerT,
        "base_b": f32("L0_ffn_base_b").reshape(1, D),
        "router_b": f32("L0_ffn_router_b").reshape(1, 8),
        "bo": f32("L0_attn_a_bo").reshape(1, D),
        "bvrow": f32("L0_attn_a_bv").reshape(1, D),
        "bqt": _bias_tile(f32("L0_attn_a_bq")),
        "bkt": _bias_tile(f32("L0_attn_a_bk")),
        "bvt": _bias_tile(f32("L0_attn_a_bv")),
        "head_mask": f32("L0_attn_a_head_mask").reshape(1, 16),
        "bigi": bigi, "repmat": repm,
    }
    common = {k: np.ascontiguousarray(v) for k, v in common.items()}

    in_maps = []
    for c in range(NCORES):
        m = dict(common)
        m["x"] = np.ascontiguousarray(xfull[2 * c:2 * c + 2])
        in_maps.append(m)

    nc, ctx = build_nc(n_sub)
    try:
        res = run_bass_kernel_spmd(nc, in_maps, core_ids=list(range(NCORES)))
    finally:
        ctx.close()
    out = np.concatenate([res.results[c]["y"] for c in range(NCORES)], axis=0)
    return np.ascontiguousarray(out.astype(np.float32))


# version 25 (final)
# speedup vs baseline: 1.0710x; 1.0539x over previous
import contextlib

import numpy as np
import ml_dtypes

import concourse.bass as bass
import concourse.mybir as mybir
from concourse.bass_utils import run_bass_kernel_spmd

F32 = mybir.dt.float32
BF16 = mybir.dt.bfloat16
OP = mybir.AluOpType
AF = mybir.ActivationFunctionType
AX = mybir.AxisListType.X

NCORES = 8
S = 2048
D = 1024
NT = S // 128
NHG = 8
NQG = 4
NKT = 16

_N_SUB = 4


def _tile2(m):
    r, c = m.shape
    return np.ascontiguousarray(m.reshape(r // 128, 128, c).transpose(1, 0, 2).reshape(128, (r // 128) * c))


def _hg_stream(wT):
    return np.ascontiguousarray(np.stack([_tile2(wT[:, hg * 128:(hg + 1) * 128]) for hg in range(8)]))


def _bias_tile(b):
    return np.ascontiguousarray(b.reshape(8, 128).T)



class _Op:
    __slots__ = ("eng", "fn", "deps", "drains", "eidx", "needs_inc", "ord", "is_dma", "sem", "tag")

    def __init__(self, eng, fn, tag=""):
        self.eng = eng
        self.fn = fn
        self.deps = []
        self.drains = []
        self.eidx = -1
        self.needs_inc = False
        self.ord = None
        self.is_dma = False
        self.sem = None
        self.tag = tag


class Prog:
    ENGS = ("tensor", "vector", "scalar", "sync", "gpsimd")
    DRAIN_ENGS = ("vector", "scalar", "gpsimd")

    def __init__(self):
        self.eng_ops = {e: [] for e in self.ENGS}
        self.bufw = {}
        self.bufr = {}
        self.dma_cum = {}

    def _resolve(self, eng, reads, writes):
        deps, drains = [], []

        def add(tok):
            if tok is None:
                return
            if tok[0] == "op" and tok[1].eng == eng and eng in self.DRAIN_ENGS:
                drains.append(tok[1])
            elif tok[0] == "op" and tok[1].eng == eng:
                pass
            else:
                deps.append(tok)

        for k in reads:
            add(self.bufw.get(k))
        for k in writes:
            add(self.bufw.get(k))
            for t in self.bufr.get(k, ()):
                add(t)
        seen, d2 = set(), []
        for t in deps:
            key = id(t[1]) if t[0] == "op" else (t[1], t[2])
            if key not in seen:
                seen.add(key)
                d2.append(t)
        return d2, list({id(o): o for o in drains}.values())

    def op(self, eng, fn, reads=(), writes=(), tag=""):
        o = _Op(eng, fn, tag)
        o.deps, o.drains = self._resolve(eng, reads, writes)
        tok = ("op", o)
        for k in reads:
            self.bufr.setdefault(k, []).append(tok)
        for k in writes:
            self.bufw[k] = tok
            self.bufr[k] = []
        o.eidx = len(self.eng_ops[eng])
        self.eng_ops[eng].append(o)
        return o

    def dma(self, queue, sem_name, fn, reads=(), writes=(), tag=""):
        o = _Op(queue, fn, tag)
        o.is_dma = True
        o.sem = sem_name
        o.deps, o.drains = self._resolve(queue, reads, writes)
        self.dma_cum[sem_name] = self.dma_cum.get(sem_name, 0) + 16
        tok = ("dma", sem_name, self.dma_cum[sem_name])
        for k in reads:
            self.bufr.setdefault(k, []).append(tok)
        for k in writes:
            self.bufw[k] = tok
            self.bufr[k] = []
        o.eidx = len(self.eng_ops[queue])
        self.eng_ops[queue].append(o)
        return o

    def settle(self, sem_name, keys):
        tok = ("dma", sem_name, self.dma_cum[sem_name])
        for k in keys:
            w = self.bufw.get(k)
            if w is not None and w[0] == "dma" and w[1] == sem_name:
                self.bufw[k] = tok
            rl = self.bufr.get(k)
            if rl:
                self.bufr[k] = [tok if (t[0] == "dma" and t[1] == sem_name) else t for t in rl]

    def finalize(self):
        for ops in self.eng_ops.values():
            for o in ops:
                for t in o.deps:
                    if t[0] == "op":
                        t[1].needs_inc = True
        for ops in self.eng_ops.values():
            n = 0
            for o in ops:
                if o.needs_inc:
                    if o.is_dma:
                        raise RuntimeError(f"dma op {o.tag} marked needs_inc")
                    n += 1
                    o.ord = n

    def emit(self, block, eng_sems, dma_sems, post=None):
        prog = self

        def runner(eng_name):
            def run(q):
                waited = {}
                last_drain = -1
                for pos, o in enumerate(prog.eng_ops[eng_name]):
                    if o.drains and max(dd.eidx for dd in o.drains) > last_drain:
                        q.drain()
                        last_drain = pos - 1
                    for t in o.deps:
                        if t[0] == "op":
                            dep = t[1]
                            if dep.ord is None:
                                raise RuntimeError(f"{o.tag} depends on un-inc'd {dep.tag}")
                            sem, val = eng_sems[dep.eng], dep.ord
                        else:
                            sem, val = dma_sems[t[1]], t[2]
                        sid = id(sem)
                        if waited.get(sid, -1) < val:
                            q.wait_ge(sem, val)
                            waited[sid] = val
                    inst = o.fn(q)
                    if o.needs_inc:
                        inst.then_inc(eng_sems[o.eng], 1)
                if post and eng_name in post:
                    post[eng_name](q)
            return run

        block.tensor(runner("tensor"))
        block.vector(runner("vector"))
        block.scalar(runner("scalar"))
        block.sync(runner("sync"))
        block.gpsimd(runner("gpsimd"))



def build_nc(n_sub=4):
    nc = bass.Bass("TRN2", target_bir_lowering=False)
    ctx = contextlib.ExitStack()

    hx = nc.dram_tensor("x", [2, S, D], F32, kind="ExternalInput").ap()
    hy = nc.dram_tensor("y", [2, S, D], F32, kind="ExternalOutput").ap()
    hwq = {k: nc.dram_tensor(f"wq_{k}", [8, 128, 1024], F32, kind="ExternalInput").ap() for k in "ab"}
    hwk = {k: nc.dram_tensor(f"wk_{k}", [8, 128, 1024], F32, kind="ExternalInput").ap() for k in "ab"}
    hwv = {k: nc.dram_tensor(f"wv_{k}", [8, 128, 1024], F32, kind="ExternalInput").ap() for k in "ab"}
    hwo = {k: nc.dram_tensor(f"wo_{k}", [128, 8192], F32, kind="ExternalInput").ap() for k in "ab"}
    hbase = nc.dram_tensor("base", [128, 8192], F32, kind="ExternalInput").ap()
    hast = nc.dram_tensor("ast", [128, 1024], F32, kind="ExternalInput").ap()
    hbst = nc.dram_tensor("bst", [128, 1024], F32, kind="ExternalInput").ap()
    hrouter = nc.dram_tensor("router", [128, 64], F32, kind="ExternalInput").ap()
    hbb = nc.dram_tensor("base_b", [1, D], F32, kind="ExternalInput").ap()
    hrb = nc.dram_tensor("router_b", [1, 8], F32, kind="ExternalInput").ap()
    hbo = nc.dram_tensor("bo", [1, D], F32, kind="ExternalInput").ap()
    hbvr = nc.dram_tensor("bvrow", [1, D], F32, kind="ExternalInput").ap()
    hbqt = nc.dram_tensor("bqt", [128, 8], F32, kind="ExternalInput").ap()
    hbkt = nc.dram_tensor("bkt", [128, 8], F32, kind="ExternalInput").ap()
    hbvt = nc.dram_tensor("bvt", [128, 8], F32, kind="ExternalInput").ap()
    hmask = nc.dram_tensor("head_mask", [1, 16], F32, kind="ExternalInput").ap()
    hbigi = nc.dram_tensor("bigi", [128, 896], BF16, kind="ExternalInput").ap()
    hrep = nc.dram_tensor("repmat", [8, 128], BF16, kind="ExternalInput").ap()
    hrr = [nc.dram_tensor(f"hbm_rr{i}", [2, 2048], BF16, kind="Internal").ap() for i in range(2)]

    def alloc(name, shape, dtype):
        return ctx.enter_context(nc.sbuf_tensor("s_" + name, shape, dtype))

    x = alloc("xbuf", [128, NT * 1024], F32)
    xnT = alloc("xnT", [128, 8 * 2048], BF16)
    aoT = alloc("aoT", [128, 8 * 2048], BF16)
    qt = alloc("qt", [128, 4096], BF16)
    kt_ = alloc("kt", [128, 4096], BF16)
    vaug = alloc("vaug", [128, 2 * NT * 192], BF16)
    pbuf = alloc("pbuf", [128, 2048], BF16)
    wpool = alloc("wpool", [128, 14 * 1024], BF16)
    rexp = alloc("rexp", [128, 2048], BF16)
    rrow = alloc("rrow", [128, 1024], BF16)
    sumsT = alloc("sumsT", [128, 512], F32)
    grid2 = alloc("grid2", [128, 512], BF16)
    rtokb = alloc("rtokb", [128, 16], F32)
    xns = alloc("xns", [128, 1024], BF16)
    bn6 = alloc("bn6", [128, 12], F32)
    mv = alloc("mv", [128, 32], F32)
    lnv = alloc("lnv", [128, 16], F32)
    rstd = alloc("rstd", [128, 16], F32)
    nmr = alloc("nmr", [128, 16], F32)
    nrstd = alloc("nrstd", [128, 16], F32)
    bigi = alloc("bigi", [128, 896], BF16)
    rep = alloc("rep", [128, 128], BF16)
    e2 = alloc("e2", [128, 2], BF16)
    onesr = alloc("onesr", [1, 144], BF16)
    rbrow = onesr[0:1, 128:136]
    maskb = alloc("maskb", [128, 16], F32)
    negbb = alloc("negbb", [128, 16], F32)
    bqt = alloc("bqt", [128, 8], F32)
    bkt = alloc("bkt", [128, 8], F32)
    rows = alloc("rows", [128, 1152], BF16)
    borow = rows[0:1, 128:1152]
    bvrow = rows[32:33, 128:1152]
    bbrow = rows[64:65, 128:1152]
    logb = alloc("logb", [128, 8], F32)
    max8 = alloc("max8", [128, 8], F32)
    negm1 = alloc("negm1", [128, 1], F32)
    expw = alloc("expw", [128, 8], F32)
    keep = alloc("keep", [128, 8], F32)
    sumw = alloc("sumw", [128, 1], F32)
    wt8 = alloc("wt8", [128, 32], BF16)
    wtrow = alloc("wtrow", [128, 512], BF16)
    axwS = alloc("axwS", [128, 1024], BF16)
    mq = alloc("mq", [128, 1], F32)
    mq4 = alloc("mq4", [128, 4], F32)
    mk4 = alloc("mk4", [128, 4], F32)
    mk = alloc("mk", [128, 1], F32)
    tprod = alloc("tprod", [128, 1], BF16)
    nbtmp = alloc("nbtmp", [1, 2], F32)
    nb1 = alloc("nb1", [1, 1], F32)
    nbbf = alloc("nbbf", [1, 1], BF16)
    epsb = alloc("epsb", [128, 1], F32)

    psA = ctx.enter_context(nc.psum_tensor("psA", [128, 1024], F32))
    psB = ctx.enter_context(nc.psum_tensor("psB", [128, 1024], F32))
    psC = ctx.enter_context(nc.psum_tensor("psC", [128, 1024], F32))
    psD = ctx.enter_context(nc.psum_tensor("psD", [128, 512], F32))
    psE = ctx.enter_context(nc.psum_tensor("psE", [128, 512], F32))
    bank = {
        "psA0": psA[:, 0:512], "psA1": psA[:, 512:1024],
        "psB0": psB[:, 0:512], "psB1": psB[:, 512:1024],
        "psC0": psC[:, 0:512], "psC1": psC[:, 512:1024],
        "psD": psD[:, :], "psE": psE[:, :],
    }

    eng_sems = {e: ctx.enter_context(nc.semaphore(f"sem_{e}")) for e in Prog.ENGS}
    dma_names = (["consts", "constg", "ffnw", "wo", "rrout", "rrin"]
                 + [f"ws{i}" for i in range(6)]
                 + [f"xi{t}" for t in range(NT)] + [f"xo{t}" for t in range(NT)])
    dma_sems = {n: ctx.enter_context(nc.semaphore(f"dma_{n}")) for n in dma_names}

    P = Prog()

    def d(queue, sem, out_ap, in_ap, reads=(), writes=(), tag=""):
        def fn(q, out_ap=out_ap, in_ap=in_ap, sem=sem):
            return q.dma_start(out=out_ap, in_=in_ap).then_inc(dma_sems[sem], 16)
        return P.dma(queue, sem, fn, reads=reads, writes=writes, tag=tag)

    d("sync", "consts", bigi[:, :], hbigi, writes=["bigi"], tag="c.bigi")
    d("sync", "consts", rep[0:8, :], hrep, writes=["rep"], tag="c.rep")
    d("sync", "consts", maskb[:, :], hmask.to_broadcast([128, 16]), writes=["maskb"], tag="c.mask")
    d("sync", "consts", bqt[:, :], hbqt, writes=["bqt"], tag="c.bqt")
    d("sync", "consts", bkt[:, :], hbkt, writes=["bkt"], tag="c.bkt")
    d("gpsimd", "constg", borow, hbo, writes=["borow"], tag="c.bo")
    d("gpsimd", "constg", bvrow, hbvr, writes=["bvrow"], tag="c.bvr")
    P.settle("constg", ["borow", "bvrow"])
    P.settle("consts", ["bigi", "rep", "maskb", "bqt", "bkt"])

    P.op("vector", lambda q: q.memset(e2[:, :], 0.0), writes=["e2"], tag="ms.e2a")
    P.op("vector", lambda q: q.memset(e2[0:64, 0:1], 1.0), reads=["e2"], writes=["e2"], tag="ms.e2b")
    P.op("vector", lambda q: q.memset(e2[64:128, 1:2], 1.0), reads=["e2"], writes=["e2"], tag="ms.e2c")
    P.op("vector", lambda q: q.memset(onesr[0:1, 0:128], 1.0), writes=["onesr"], tag="ms.ones")
    P.op("vector", lambda q: q.memset(rows[0:1, 0:128], 1.0), writes=["ones0"], tag="ms.r0")
    P.op("vector", lambda q: q.memset(rows[32:33, 0:128], 1.0), writes=["ones32"], tag="ms.r32")
    P.op("vector", lambda q: q.memset(rows[64:65, 0:128], 1.0), writes=["ones64"], tag="ms.r64")
    P.op("vector", lambda q: q.memset(epsb[:, :], 1e-5), writes=["epsb"], tag="ms.eps")
    P.op("vector", lambda q: q.memset(grid2[:, :], 0.0), writes=["grid2a", "grid2b"], tag="ms.g2")
    for par in range(2):
        for t in range(NT):
            b0 = par * NT * 192 + t * 192
            P.op("vector", lambda q, b=b0: q.memset(vaug[:, b + 64:b + 96], 1.0),
                 writes=[f"vaug{par}.{t}"], tag=f"ms.v1.{par}.{t}")
            P.op("vector", lambda q, b=b0: q.memset(vaug[:, b + 96:b + 128], 0.0),
                 reads=[f"vaug{par}.{t}"], writes=[f"vaug{par}.{t}"], tag=f"ms.v0.{par}.{t}")

    def xin_tile(seq, t):
        def fn(q, t=t, seq=seq):
            return q.dma_start(out=x[:, t * 1024:(t + 1) * 1024],
                               in_=hx[seq, t * 128:(t + 1) * 128, :]).then_inc(dma_sems[f"xi{t}"], 16)
        P.dma("sync", f"xi{t}", fn, writes=[f"x.{t}"], tag=f"xin{seq}.{t}")

    def xout_tile(seq, t):
        def fn(q, t=t, seq=seq):
            return q.dma_start(out=hy[seq, t * 128:(t + 1) * 128, :],
                               in_=x[:, t * 1024:(t + 1) * 1024]).then_inc(dma_sems[f"xo{t}"], 16)
        P.dma("sync", f"xo{t}", fn, reads=[f"x.{t}"], tag=f"xout{seq}.{t}")

    xnT3 = xnT[:, :].rearrange("p (c j) -> p c j", c=8)

    from collections import deque

    def ln_tile_stats(tag, t):
        P.op("vector", lambda q, t=t: q.bn_stats(bn6[:, 0:6], x[:, t * 1024:t * 1024 + 512]),
             reads=[f"x.{t}"], writes=["bn6a"], tag=f"{tag}.bns0.{t}")
        P.op("vector", lambda q, t=t: q.bn_stats(bn6[:, 6:12], x[:, t * 1024 + 512:t * 1024 + 1024]),
             reads=[f"x.{t}"], writes=["bn6b"], tag=f"{tag}.bns1.{t}")
        P.op("vector", lambda q, t=t: q.bn_aggr(mv[:, 2 * t:2 * t + 2], bn6[:, 0:12]),
             reads=["bn6a", "bn6b"], writes=[f"mv.{t}"], tag=f"{tag}.bna.{t}")

    def ln_group(tag, g):
        P.op("scalar", lambda q, g=g: q.activation(lnv[:, 4 * g:4 * g + 4],
                                                   mv[:, 8 * g + 1:8 * g + 8:2],
                                                   AF.Ln, bias=epsb[:, 0:1], scale=1.0),
             reads=[f"mv.{t}" for t in range(g * 4, g * 4 + 4)] + ["epsb"],
             writes=[f"lnv.{g}"], tag=f"{tag}.lnv.{g}")
        P.op("scalar", lambda q, g=g: q.activation(rstd[:, 4 * g:4 * g + 4], lnv[:, 4 * g:4 * g + 4],
                                                   AF.Exp, bias=0.0, scale=-0.5),
             reads=[f"lnv.{g}"], writes=[f"rstd.{g}"], tag=f"{tag}.rstd.{g}")

        P.op("scalar", lambda q, g=g: q.activation(
            nrstd[:, 4 * g:4 * g + 4], rstd[:, 4 * g:4 * g + 4], AF.Identity,
            bias=0.0, scale=-1.0),
            reads=[f"rstd.{g}"], writes=[f"nrstd.{g}"], tag=f"{tag}.nrstd.{g}")

        def emit_nmr(t):
            P.op("scalar", lambda q, t=t: q.activation(
                nmr[:, t:t + 1], mv[:, 2 * t:2 * t + 1], AF.Identity,
                bias=0.0, scale=nrstd[:, t:t + 1]),
                reads=[f"mv.{t}", f"nrstd.{g}"], writes=[f"nmr.{t % 16}"], tag=f"{tag}.nmr.{t}")

        def slot(t):
            return (xns[:, 0:1024], "xns") if t % 2 == 0 else (pbuf[:, 0:1024], "pbuf.0")

        def emit_norm(t):
            xd, xk = slot(t)
            P.op("scalar", lambda q, t=t, xd=xd: q.activation(
                xd, x[:, t * 1024:(t + 1) * 1024], AF.Identity,
                bias=nmr[:, t:t + 1], scale=rstd[:, t:t + 1]),
                reads=[f"x.{t}", f"nmr.{t % 16}", f"rstd.{g}"], writes=[xk],
                tag=f"{tag}.norm.{t}")

        def emit_T(t):
            xd, xk = slot(t)
            for half in range(2):
                bk = ("psD", "psE")[(t * 2 + half) % 2]
                for c in range(4):
                    cc = half * 4 + c
                    P.op("tensor", lambda q, cc=cc, c=c, bk=bk, xd=xd, xk=xk: q.matmul(
                        bank[bk], xd[:, cc * 128:(cc + 1) * 128],
                        bigi[:, 384 - c * 128:896 - c * 128],
                        start=(c == 0), stop=(c == 3)),
                        reads=[xk, "bigi", bk], writes=[bk], tag=f"{tag}.T.{t}.{cc}")
                P.op("scalar", lambda q, t=t, half=half, bk=bk: q.copy(
                    xnT3[:, 4 * half:4 * half + 4, t * 128:(t + 1) * 128], bank[bk]),
                    reads=[bk], writes=[f"xnT.w{t // 4}"], tag=f"{tag}.Tev.{t}.{half}")

        ts = list(range(g * 4, g * 4 + 4))
        emit_nmr(ts[0])
        emit_norm(ts[0])
        emit_nmr(ts[1])
        emit_norm(ts[1])
        for i, t in enumerate(ts):
            emit_T(t)
            if i + 2 < 4:
                emit_nmr(ts[i + 2])
                emit_norm(ts[i + 2])

    def ln_xnT(tag):
        for g in range(4):
            for t in range(g * 4, g * 4 + 4):
                ln_tile_stats(tag, t)
            ln_group(tag, g)

    def make_ln_hook(tag):
        def hook(t):
            ln_tile_stats(tag, t)
        return hook

    def attn_load(kind, tag):
        for hg01 in (0, 1):
            s0 = (hg01 % 2) * 3
            for j, (hn, hsrc) in enumerate((("wq", hwq), ("wk", hwk), ("wv", hwv))):
                d("gpsimd", f"ws{s0 + j}", wpool[:, (s0 + j) * 1024:(s0 + j + 1) * 1024],
                  hsrc[kind][hg01], writes=[f"wp{s0 + j}"], tag=f"{tag}.dw{hn}.{hg01}")
        d("gpsimd", "wo", wpool[:, 6 * 1024:14 * 1024], hwo[kind],
          writes=[f"wp{i}" for i in range(6, 14)], tag=f"{tag}.dwo")

    def attn_load_tail(kind, tag):
        for j, (hn, hsrc) in enumerate((("wq", hwq), ("wk", hwk), ("wv", hwv))):
            d("gpsimd", f"ws{3 + j}", wpool[:, (3 + j) * 1024:(4 + j) * 1024],
              hsrc[kind][1], writes=[f"wp{3 + j}"], tag=f"{tag}.dw{hn}.1")

    def attn_prefetch0(kind, tag):
        for j, (hn, hsrc) in enumerate((("wq", hwq), ("wk", hwk), ("wv", hwv))):
            d("gpsimd", f"ws{j}", wpool[:, (11 + j) * 1024:(12 + j) * 1024],
              hsrc[kind][0], writes=[f"wp{11 + j}"], tag=f"{tag}.pf{hn}.0")

    def attn(seq, kind, tag, hook=None, next_loader=None, preloaded=False, inline_ln_tag=None,
             ln_rest_tag=None, hg0_slots=None, wo_deferred=False):
        lcol = 0 if kind == "a" else 8
        fillers = deque()

        def pump(n):
            while n > 0 and fillers:
                f = fillers.popleft()
                if f is None:
                    continue
                f()
                n -= 1

        def flush():
            while fillers:
                f = fillers.popleft()
                if f is not None:
                    f()

        def wslices(hg):
            s0 = (hg % 2) * 3
            for j, (hn, hsrc) in enumerate((("wq", hwq), ("wk", hwk), ("wv", hwv))):
                d("gpsimd", f"ws{s0 + j}", wpool[:, (s0 + j) * 1024:(s0 + j + 1) * 1024],
                  hsrc[kind][hg], writes=[f"wp{s0 + j}"], tag=f"{tag}.dw{hn}.{hg}")

        def emit_qkv(hg, into):
            if hg == 0 and hg0_slots is not None:
                sq, sk_, sv = hg0_slots
            else:
                s0 = (hg % 2) * 3
                sq, sk_, sv = s0, s0 + 1, s0 + 2
            par = hg % 2
            wq_ap = wpool[:, sq * 1024:(sq + 1) * 1024]
            wk_ap = wpool[:, sk_ * 1024:(sk_ + 1) * 1024]
            wv_ap = wpool[:, sv * 1024:(sv + 1) * 1024]
            wqk, wkk, wvk = f"wp{sq}", f"wp{sk_}", f"wp{sv}"
            for g in range(4):
                for mi, (wap, wkey, dstb, dkey, bt, bkey) in enumerate((
                        (wq_ap, wqk, qt, "qt", bqt if kind == "a" else None, "bqt"),
                        (wk_ap, wkk, kt_, "kt", bkt if kind == "a" else None, "bkt"))):
                    bk = ("psD", "psE")[(2 * g + mi) % 2]
                    for c in range(8):
                        into.append(lambda wap=wap, c=c, g=g, bk=bk, wkey=wkey: P.op(
                            "tensor", lambda q: q.matmul(
                                bank[bk], wap[:, c * 128:(c + 1) * 128],
                                xnT[:, c * 2048 + g * 512:c * 2048 + (g + 1) * 512],
                                start=(c == 0), stop=(c == 7)),
                            reads=[wkey, f"xnT.w{g}", bk], writes=[bk],
                            tag=f"{tag}.qk.{hg}.{g}.{c}"))
                    if bt is not None:
                        into.append(lambda dstb=dstb, g=g, bk=bk, bt=bt, hg=hg, par=par,
                                    dkey=dkey, bkey=bkey: P.op(
                            "vector", lambda q: q.tensor_scalar(
                                dstb[:, par * 2048 + g * 512:par * 2048 + (g + 1) * 512],
                                bank[bk], bt[:, hg:hg + 1], None, OP.add),
                            reads=[bk, bkey], writes=[f"{dkey}{par}.{g}"],
                            tag=f"{tag}.qkev.{hg}.{g}"))
                    else:
                        into.append(lambda dstb=dstb, g=g, bk=bk, par=par, dkey=dkey: P.op(
                            "vector", lambda q: q.tensor_copy(
                                dstb[:, par * 2048 + g * 512:par * 2048 + (g + 1) * 512],
                                bank[bk]),
                            reads=[bk], writes=[f"{dkey}{par}.{g}"],
                            tag=f"{tag}.qkev.{hg}.{g}"))
                if seq == 0:
                    into.append(lambda par=par, g=g: P.op(
                        "vector", lambda q: q.tensor_reduce(
                            mq4[:, g:g + 1], qt[:, par * 2048 + g * 512:par * 2048 + (g + 1) * 512],
                            AX, OP.max, apply_absolute_value=True),
                        reads=[f"qt{par}.{g}"], writes=[f"mq4.{g}"], tag=f"{tag}.mqp.{hg}.{g}"))
                    into.append(lambda par=par, g=g: P.op(
                        "vector", lambda q: q.tensor_reduce(
                            mk4[:, g:g + 1], kt_[:, par * 2048 + g * 512:par * 2048 + (g + 1) * 512],
                            AX, OP.max, apply_absolute_value=True),
                        reads=[f"kt{par}.{g}"], writes=[f"mk4.{g}"], tag=f"{tag}.mkp.{hg}.{g}"))
                for t in range(g * 4, g * 4 + 4):
                    bk = ("psD", "psE")[t % 2]
                    for c in range(8):
                        into.append(lambda t=t, c=c, wv_ap=wv_ap, bk=bk, wvk=wvk, kind=kind: P.op(
                            "tensor", lambda q: q.matmul(
                                bank[bk][:, 0:128], xnT[:, c * 2048 + t * 128:c * 2048 + (t + 1) * 128],
                                wv_ap[:, c * 128:(c + 1) * 128],
                                start=(c == 0), stop=(c == 7 and kind != "a")),
                            reads=[wvk, f"xnT.w{t // 4}", bk], writes=[bk],
                            tag=f"{tag}.v.{hg}.{t}.{c}"))
                    if kind == "a":
                        into.append(lambda hg=hg, bk=bk, t=t: P.op(
                            "tensor", lambda q: q.matmul(
                                bank[bk][:, 0:128], rows[32:33, 0:128],
                                rows[32:33, 128 + hg * 128:128 + (hg + 1) * 128],
                                start=False, stop=True),
                            reads=["ones32", "bvrow", bk], writes=[bk], tag=f"{tag}.vb.{hg}.{t}"))

                    def vev_mk(t=t, bk=bk, par=par):
                        def vev(q):
                            dst = bass.AP(tensor=vaug[:, 0:1].tensor, offset=par * NT * 192 + t * 192,
                                          ap=[[2 * NT * 192, 128], [128, 2], [1, 64]])
                            return q.tensor_copy(dst, bank[bk][:, 0:128])
                        return vev
                    into.append(lambda t=t, bk=bk, par=par: P.op(
                        "vector", vev_mk(t, bk, par), reads=[bk], writes=[f"vaug{par}.{t}"],
                        tag=f"{tag}.vev.{hg}.{t}"))
                into.append(None)
            if seq == 0:
                into.append(lambda: P.op(
                    "vector", lambda q: q.reduce_max(mq[:, :], mq4[:, :], AX),
                    reads=[f"mq4.{g}" for g in range(4)], writes=["mq"], tag=f"{tag}.mq.{hg}"))
                into.append(lambda: P.op(
                    "vector", lambda q: q.reduce_max(mk[:, :], mk4[:, :], AX),
                    reads=[f"mk4.{g}" for g in range(4)], writes=["mk"], tag=f"{tag}.mk.{hg}"))
                into.append(lambda: P.op(
                    "vector", lambda q: q.tensor_mul(tprod[:, :], mq[:, :], mk[:, :]),
                    reads=["mq", "mk"], writes=["tprod"], tag=f"{tag}.tp.{hg}"))
                into.append(lambda: P.op(
                    "tensor", lambda q: q.matmul(psE[0:1, 0:2], tprod[:, :], e2[:, :],
                                                 start=True, stop=True),
                    reads=["tprod", "e2", "psE"], writes=["psE"], tag=f"{tag}.bsum.{hg}"))
                into.append(lambda: P.op(
                    "vector", lambda q: q.tensor_scalar(nbtmp[0:1, :], psE[0:1, 0:2],
                                                        480.0, None, OP.min),
                    reads=["psE"], writes=["nbtmp"], tag=f"{tag}.bclamp.{hg}"))
                into.append(lambda: P.op(
                    "vector", lambda q: q.reduce_max(nb1[0:1, :], nbtmp[0:1, :], AX),
                    reads=["nbtmp"], writes=["nb1"], tag=f"{tag}.bmax.{hg}"))
                into.append(lambda: P.op(
                    "vector", lambda q: q.tensor_copy(nbbf[0:1, :], nb1[0:1, :]),
                    reads=["nb1"], writes=["nbbf"], tag=f"{tag}.bbf.{hg}"))
                into.append(lambda: P.op(
                    "tensor", lambda q: q.matmul(psE[:, 8:9], onesr[0:1, 0:128], nbbf[0:1, :],
                                                 start=True, stop=True),
                    reads=["nbbf", "onesr", "psE"], writes=["psE"], tag=f"{tag}.bbc.{hg}"))
                into.append(lambda hg=hg: P.op(
                    "vector", lambda q: q.tensor_scalar(
                        negbb[:, lcol + hg:lcol + hg + 1], psE[:, 8:9], -0.125, None, OP.mult),
                    reads=["psE"], writes=[f"negbb.{lcol + hg}"], tag=f"{tag}.bneg.{hg}"))

        def emit_wo_pass(into, cs, with_bias, passtag):
            c0, c1 = cs
            for t in range(NT):
                for f in range(2):
                    bk = ("psD", "psE")[(t * 2 + f) % 2]
                    for c in range(c0, c1):
                        laststop = (c == c1 - 1 and not with_bias)
                        into.append(lambda t=t, f=f, c=c, bk=bk, laststop=laststop, c0=c0: P.op(
                            "tensor", lambda q: q.matmul(
                                bank[bk], aoT[:, c * 2048 + t * 128:c * 2048 + (t + 1) * 128],
                                wpool[:, 6 * 1024 + c * 1024 + f * 512:
                                       6 * 1024 + c * 1024 + (f + 1) * 512],
                                start=(c == c0), stop=laststop),
                            reads=[f"aoT.{c}", f"wp{6 + c}", bk], writes=[bk],
                            tag=f"{tag}.{passtag}.{t}.{f}.{c}"))
                    if with_bias:
                        into.append(lambda f=f, bk=bk, t=t: P.op(
                            "tensor", lambda q: q.matmul(
                                bank[bk], rows[0:1, 0:128], rows[0:1, 128 + f * 512:128 + (f + 1) * 512],
                                start=False, stop=True),
                            reads=["ones0", "borow", bk], writes=[bk],
                            tag=f"{tag}.{passtag}b.{t}.{f}"))
                    into.append(lambda t=t, f=f, bk=bk: P.op(
                        "vector", lambda q: q.tensor_tensor(
                            x[:, t * 1024 + f * 512:t * 1024 + (f + 1) * 512],
                            x[:, t * 1024 + f * 512:t * 1024 + (f + 1) * 512],
                            bank[bk], OP.add),
                        reads=[bk, f"x.{t}"], writes=[f"x.{t}"],
                        tag=f"{tag}.{passtag}add.{t}.{f}"))

        def emit_scores(hg, mid=()):
            par = hg % 2
            qo = par * 2048
            for qg in range(NQG):
                if qg == 2:
                    for m in mid:
                        m()
                def emit_st(kt):
                    kpar = kt % 2
                    ps = psA if kpar == 0 else psB
                    sk0, sk1 = (("psA0", "psA1") if kpar == 0 else ("psB0", "psB1"))
                    P.op("tensor", lambda q, kt=kt, qg=qg, ps=ps, qo=qo: q.matmul(
                        ps[:, 0:512], kt_[0:64, qo + kt * 128:qo + (kt + 1) * 128],
                        qt[0:64, qo + qg * 512:qo + (qg + 1) * 512],
                        start=True, stop=True, tile_position=(0, 0)),
                        reads=[f"kt{par}.{kt // 4}", f"qt{par}.{qg}", sk0], writes=[sk0],
                        tag=f"{tag}.s0.{hg}.{qg}.{kt}")
                    P.op("tensor", lambda q, kt=kt, qg=qg, ps=ps, qo=qo: q.matmul(
                        ps[:, 512:1024], kt_[64:128, qo + kt * 128:qo + (kt + 1) * 128],
                        qt[64:128, qo + qg * 512:qo + (qg + 1) * 512],
                        start=True, stop=True, tile_position=(64, 0)),
                        reads=[f"kt{par}.{kt // 4}", f"qt{par}.{qg}", sk1], writes=[sk1],
                        tag=f"{tag}.s1.{hg}.{qg}.{kt}")
                    P.op("scalar", lambda q, ps=ps, kpar=kpar, hg=hg: q.activation(
                        pbuf[:, kpar * 1024:(kpar + 1) * 1024], ps[:, 0:1024], AF.Exp,
                        bias=negbb[:, lcol + hg:lcol + hg + 1], scale=0.125),
                        reads=[sk0, sk1, f"negbb.{lcol + hg}"], writes=[f"pbuf.{kpar}"],
                        tag=f"{tag}.exp.{hg}.{qg}.{kt}")

                def emit_pv(kt):
                    kpar = kt % 2
                    P.op("tensor", lambda q, kt=kt, kpar=kpar, par=par: q.matmul(
                        psC[0:128, 0:512],
                        vaug[:, par * NT * 192 + kt * 192:par * NT * 192 + kt * 192 + 128],
                        pbuf[:, kpar * 1024:kpar * 1024 + 512],
                        start=(kt == 0), stop=(kt == NKT - 1)),
                        reads=[f"pbuf.{kpar}", f"vaug{par}.{kt}", "psC0"], writes=["psC0"],
                        tag=f"{tag}.pv0.{hg}.{qg}.{kt}")
                    P.op("tensor", lambda q, kt=kt, kpar=kpar, par=par: q.matmul(
                        psC[0:128, 512:1024],
                        vaug[:, par * NT * 192 + kt * 192 + 64:par * NT * 192 + (kt + 1) * 192],
                        pbuf[:, kpar * 1024 + 512:kpar * 1024 + 1024],
                        start=(kt == 0), stop=(kt == NKT - 1)),
                        reads=[f"pbuf.{kpar}", f"vaug{par}.{kt}", "psC1"], writes=["psC1"],
                        tag=f"{tag}.pv1.{hg}.{qg}.{kt}")

                for kt in range(NKT + 1):
                    if kt < NKT:
                        emit_st(kt)
                    if kt > 0:
                        emit_pv(kt - 1)
                    pump(3 if kt % 4 == 3 else 4)
                P.op("scalar", lambda q, hg=hg, qg=qg: q.copy(
                    aoT[0:64, hg * 2048 + qg * 512:hg * 2048 + (qg + 1) * 512], psC[0:64, 0:512]),
                    reads=["psC0"], writes=[f"aoT.{hg}"], tag=f"{tag}.aev0.{hg}.{qg}")
                P.op("scalar", lambda q, hg=hg, qg=qg: q.copy(
                    aoT[64:128, hg * 2048 + qg * 512:hg * 2048 + (qg + 1) * 512],
                    psC[64:128, 512:1024]),
                    reads=["psC1"], writes=[f"aoT.{hg}"], tag=f"{tag}.aev1.{hg}.{qg}")
                P.op("vector", lambda q: q.transpose(sumsT[64:96, 0:512], psC[64:96, 0:512]),
                     reads=["psC0"], writes=["sumsT0"], tag=f"{tag}.st0.{hg}.{qg}")
                P.op("vector", lambda q: q.transpose(sumsT[0:32, 0:512], psC[0:32, 512:1024]),
                     reads=["psC1"], writes=["sumsT1"], tag=f"{tag}.st1.{hg}.{qg}")
                P.op("vector", lambda q: q.reciprocal(rtokb[64:96, 0:16], sumsT[64:96, 0:512:32]),
                     reads=["sumsT0"], writes=["rtok0"], tag=f"{tag}.rc0.{hg}.{qg}")
                P.op("vector", lambda q: q.reciprocal(rtokb[0:32, 0:16], sumsT[0:32, 0:512:32]),
                     reads=["sumsT1"], writes=["rtok1"], tag=f"{tag}.rc1.{hg}.{qg}")
                if kind == "a":
                    P.op("vector", lambda q, hg=hg: q.tensor_scalar(
                        rtokb[64:96, 0:16], rtokb[64:96, 0:16],
                        maskb[64:96, 2 * hg:2 * hg + 1], None, OP.mult),
                        reads=["rtok0", "maskb"], writes=["rtok0"], tag=f"{tag}.msk0.{hg}.{qg}")
                    P.op("vector", lambda q, hg=hg: q.tensor_scalar(
                        rtokb[0:32, 0:16], rtokb[0:32, 0:16],
                        maskb[0:32, 2 * hg + 1:2 * hg + 2], None, OP.mult),
                        reads=["rtok1", "maskb"], writes=["rtok1"], tag=f"{tag}.msk1.{hg}.{qg}")
                P.op("vector", lambda q: q.tensor_copy(grid2[64:96, 0:512:32], rtokb[64:96, 0:16]),
                     reads=["rtok0"], writes=["grid2a"], tag=f"{tag}.g2a.{hg}.{qg}")
                P.op("vector", lambda q: q.tensor_copy(grid2[0:32, 0:512:32], rtokb[0:32, 0:16]),
                     reads=["rtok1"], writes=["grid2b"], tag=f"{tag}.g2b.{hg}.{qg}")
                P.op("vector", lambda q, qg=qg: q.transpose(
                    rrow[64:96, (qg % 2) * 512:(qg % 2 + 1) * 512], grid2[64:96, 0:512]),
                    reads=["grid2a"], writes=["rrow"], tag=f"{tag}.g3a.{hg}.{qg}")
                P.op("vector", lambda q, qg=qg: q.transpose(
                    rrow[0:32, (qg % 2) * 512:(qg % 2 + 1) * 512], grid2[0:32, 0:512]),
                    reads=["grid2b"], writes=["rrow"], tag=f"{tag}.g3b.{hg}.{qg}")
                pump(4)
                if qg % 2 == 1 and not (hg == 7 and qg == 3):
                    hh = qg // 2
                    slot = hg % 2
                    d("sync", "rrout", hrr[slot][0:1, hh * 1024:(hh + 1) * 1024],
                      rrow[64:65, 0:1024],
                      reads=["rrow"], writes=[f"hrr{slot}"], tag=f"{tag}.rro0.{hg}.{hh}")
                    d("sync", "rrout", hrr[slot][1:2, hh * 1024:(hh + 1) * 1024],
                      rrow[0:1, 0:1024],
                      reads=["rrow"], writes=[f"hrr{slot}"], tag=f"{tag}.rro1.{hg}.{hh}")
                if hg == 7 and qg == 2:
                    slot = hg % 2
                    d("sync", "rrin", rexp[0:64, 0:1024],
                      hrr[slot][0:1, 0:1024].to_broadcast([64, 1024]),
                      reads=[f"hrr{slot}"], writes=["rexpb"], tag=f"{tag}.rri0h.{hg}")
                    d("sync", "rrin", rexp[64:128, 0:1024],
                      hrr[slot][1:2, 0:1024].to_broadcast([64, 1024]),
                      reads=[f"hrr{slot}"], writes=["rexpb"], tag=f"{tag}.rri1h.{hg}")
            slot = hg % 2
            if hg < 7:
                d("sync", "rrin", rexp[0:64, :], hrr[slot][0:1, :].to_broadcast([64, 2048]),
                  reads=[f"hrr{slot}"], writes=["rexpb"], tag=f"{tag}.rri0.{hg}")
                d("sync", "rrin", rexp[64:128, :], hrr[slot][1:2, :].to_broadcast([64, 2048]),
                  reads=[f"hrr{slot}"], writes=["rexpb"], tag=f"{tag}.rri1.{hg}")
            else:
                for qg2 in (2, 3):
                    sl = qg2 % 2
                    ps = (psA, psB)[qg2 - 2]
                    pk0, pk1 = (("psA0", "psA1"), ("psB0", "psB1"))[qg2 - 2]
                    P.op("tensor", lambda q, ps=ps, sl=sl: q.matmul(
                        ps[0:64, 0:512], rows[64:65, 0:64], rrow[64:65, sl * 512:(sl + 1) * 512],
                        start=True, stop=True),
                        reads=["rrow", "ones64", pk0], writes=[pk0], tag=f"{tag}.rbcA.{qg2}")
                    P.op("tensor", lambda q, ps=ps, sl=sl: q.matmul(
                        ps[64:128, 512:1024], onesr[0:1, 0:64], rrow[0:1, sl * 512:(sl + 1) * 512],
                        start=True, stop=True, tile_position=(0, 64)),
                        reads=["rrow", "onesr", pk1], writes=[pk1], tag=f"{tag}.rbcB.{qg2}")
                    P.op("vector", lambda q, ps=ps, qg2=qg2: q.tensor_copy(
                        rexp[0:64, qg2 * 512:(qg2 + 1) * 512], ps[0:64, 0:512]),
                        reads=[pk0], writes=["rexpb"], tag=f"{tag}.rbevA.{qg2}")
                    P.op("vector", lambda q, ps=ps, qg2=qg2: q.tensor_copy(
                        rexp[64:128, qg2 * 512:(qg2 + 1) * 512], ps[64:128, 512:1024]),
                        reads=[pk1], writes=["rexpb"], tag=f"{tag}.rbevB.{qg2}")

            def nrm(hg=hg):
                P.op("vector", lambda q, hg=hg: q.tensor_tensor(
                    aoT[:, hg * 2048:(hg + 1) * 2048], aoT[:, hg * 2048:(hg + 1) * 2048],
                    rexp[:, :], OP.mult),
                    reads=[f"aoT.{hg}", "rexpb"], writes=[f"aoT.{hg}"], tag=f"{tag}.nrm.{hg}")
            return nrm

        if not preloaded:
            attn_load(kind, tag)
        direct = deque()
        emit_qkv(0, direct)
        for g in range(4):
            if inline_ln_tag is not None:
                for t in range(g * 4, g * 4 + 4):
                    ln_tile_stats(inline_ln_tag, t)
            ln_group(ln_rest_tag, g)
            while direct:
                f = direct.popleft()
                if f is None:
                    break
                f()
        while direct:
            f = direct.popleft()
            if f is not None:
                f()
        if wo_deferred:
            d("gpsimd", "wo", wpool[:, 6 * 1024:14 * 1024], hwo[kind],
              writes=[f"wp{i}" for i in range(6, 14)], tag=f"{tag}.dwo")
        pending_nrm = None
        for hg in range(NHG):
            if hg + 2 < NHG:
                wslices(hg + 2)
            if hg + 1 < NHG:
                emit_qkv(hg + 1, fillers)
            else:
                emit_wo_pass(fillers, (0, 6), False, "wo1")
            pending_nrm = emit_scores(hg, mid=((pending_nrm,) if pending_nrm else ()))
            flush()
        pending_nrm()
        if next_loader is not None:
            next_loader()
        tail = deque()
        emit_wo_pass(tail, (6, 8), kind == "a", "wo2")
        per_tile = len(tail) // NT
        for t in range(NT):
            for _ in range(per_tile):
                tail.popleft()()
            if hook is not None:
                hook(t)
        while tail:
            tail.popleft()()

    def ffn_load(tag):
        d("gpsimd", "ffnw", wpool[:, 8192:9216], hast, writes=["wp8"], tag=f"{tag}.dast")
        d("gpsimd", "ffnw", wpool[:, 9216:10240], hbst, writes=["wp9"], tag=f"{tag}.dbst")
        d("gpsimd", "ffnw", wpool[:, 10240:10304], hrouter, writes=["wp10"], tag=f"{tag}.drt")
        d("gpsimd", "ffnw", bbrow, hbb, writes=["bbrow"], tag=f"{tag}.dbb")
        d("gpsimd", "ffnw", rbrow, hrb, writes=["rbrow"], tag=f"{tag}.drb")
        d("gpsimd", "ffnw", wpool[:, 0:4096], hbase[:, 0:4096],
          writes=[f"wp{i}" for i in range(4)], tag=f"{tag}.dbase0")
        d("gpsimd", "ffnw", wpool[:, 4096:8192], hbase[:, 4096:8192],
          writes=[f"wp{i}" for i in range(4, 8)], tag=f"{tag}.dbase1")
        P.settle("ffnw", [f"wp{i}" for i in range(8)] + ["wp8", "wp9", "wp10", "bbrow", "rbrow"])

    def ffn(tag, hook=None, preloaded=False, next_loader=None, ln_rest_tag=None, prefetch=None):
        if not preloaded:
            ffn_load(tag)
        if prefetch is not None:
            prefetch()

        def producer(g):
            pk = f"psC{g % 2}"
            pap = psC[:, 0:512] if g % 2 == 0 else psC[:, 512:1024]
            for c in range(8):
                P.op("tensor", lambda q, c=c, g=g, pap=pap: q.matmul(
                    pap, wpool[:, 8192 + c * 128:8192 + (c + 1) * 128],
                    xnT[:, c * 2048 + g * 512:c * 2048 + (g + 1) * 512],
                    start=(c == 0), stop=(c == 7)),
                    reads=["wp8", f"xnT.w{g}", pk], writes=[pk], tag=f"{tag}.ax.{g}.{c}")
            for t in range(g * 4, g * 4 + 4):
                bk = ("psD", "psE")[t % 2]
                for c in range(8):
                    P.op("tensor", lambda q, t=t, c=c, bk=bk: q.matmul(
                        bank[bk][:, 0:8], xnT[:, c * 2048 + t * 128:c * 2048 + (t + 1) * 128],
                        wpool[:, 10240 + c * 8:10240 + (c + 1) * 8],
                        start=(c == 0), stop=False),
                        reads=["wp10", f"xnT.w{g}", bk], writes=[bk], tag=f"{tag}.lg.{t}.{c}")
                P.op("tensor", lambda q, bk=bk: q.matmul(
                    bank[bk][:, 0:8], onesr[0:1, 0:128], onesr[0:1, 128:136],
                    start=False, stop=True),
                    reads=["onesr", "rbrow", bk], writes=[bk], tag=f"{tag}.lgb.{t}")
                P.op("vector", lambda q, bk=bk: q.tensor_copy(logb[:, :], bank[bk][:, 0:8]),
                     reads=[bk], writes=["logb"], tag=f"{tag}.lev.{t}")
                P.op("vector", lambda q: q.max(out=max8[:, :], in_=logb[:, :]),
                     reads=["logb"], writes=["max8"], tag=f"{tag}.max8.{t}")
                P.op("vector", lambda q: q.tensor_scalar(keep[:, :], logb[:, :], max8[:, 1:2],
                                                         None, OP.is_ge),
                     reads=["logb", "max8"], writes=["keep"], tag=f"{tag}.keep.{t}")
                P.op("vector", lambda q: q.tensor_scalar(negm1[:, :], max8[:, 0:1], -1.0,
                                                         None, OP.mult),
                     reads=["max8"], writes=["negm1"], tag=f"{tag}.nm1.{t}")
                P.op("scalar", lambda q: q.activation(expw[:, :], logb[:, :], AF.Exp,
                                                      bias=negm1[:, 0:1], scale=1.0),
                     reads=["logb", "negm1"], writes=["expw"], tag=f"{tag}.rexp.{t}")
                P.op("vector", lambda q: q.tensor_mul(expw[:, :], expw[:, :], keep[:, :]),
                     reads=["expw", "keep"], writes=["expw"], tag=f"{tag}.emask.{t}")
                P.op("vector", lambda q: q.reduce_sum(sumw[:, :], expw[:, :], AX),
                     reads=["expw"], writes=["sumw"], tag=f"{tag}.esum.{t}")
                P.op("vector", lambda q: q.reciprocal(sumw[:, :], sumw[:, :]),
                     reads=["sumw"], writes=["sumw"], tag=f"{tag}.erec.{t}")
                P.op("vector", lambda q, t=t: q.tensor_scalar(
                    wt8[:, (t % 4) * 8:(t % 4) * 8 + 8], expw[:, :], sumw[:, 0:1],
                    2.0, OP.mult, OP.mult),
                    reads=["expw", "sumw"], writes=[f"wt8.{t % 4}"], tag=f"{tag}.wt8.{t}")

        def producer_b(g):
            pk = f"psC{g % 2}"
            pap = psC[:, 0:512] if g % 2 == 0 else psC[:, 512:1024]
            for tt in range(4):
                P.op("tensor", lambda q, tt=tt: q.matmul(
                    psB[0:8, 0:128], wt8[:, tt * 8:(tt + 1) * 8], bigi[:, 384:512],
                    start=True, stop=True),
                    reads=[f"wt8.{tt}", "bigi", "psB0"], writes=["psB0"], tag=f"{tag}.wtT.{g}.{tt}")
                P.op("vector", lambda q, tt=tt: q.tensor_copy(
                    wtrow[0:8, tt * 128:(tt + 1) * 128], psB[0:8, 0:128]),
                    reads=["psB0"], writes=["wtrow"], tag=f"{tag}.wtev.{g}.{tt}")
            P.op("tensor", lambda q: q.matmul(psB[:, 512:1024], rep[0:8, :], wtrow[0:8, 0:512],
                                              start=True, stop=True),
                 reads=["rep", "wtrow", "psB1"], writes=["psB1"], tag=f"{tag}.repl.{g}")
            P.op("vector", lambda q: q.tensor_copy(wtrow[:, :], psB[:, 512:1024]),
                 reads=["psB1"], writes=["wexpS"], tag=f"{tag}.wexpev.{g}")
            P.op("vector", lambda q, g=g, pap=pap, pk=pk: q.tensor_tensor(
                axwS[:, (g % 2) * 512:(g % 2 + 1) * 512], pap, wtrow[:, :], OP.mult),
                reads=[pk, "wexpS"], writes=[f"axwS{g % 2}"], tag=f"{tag}.axw.{g}")

        def consumer(g):
            for t in range(g * 4, g * 4 + 4):
                for f in range(2):
                    bk = ("psA0", "psA1")[(t * 2 + f) % 2]
                    for c in range(8):
                        P.op("tensor", lambda q, t=t, f=f, c=c, bk=bk: q.matmul(
                            bank[bk], xnT[:, c * 2048 + t * 128:c * 2048 + (t + 1) * 128],
                            wpool[:, c * 1024 + f * 512:c * 1024 + (f + 1) * 512],
                            start=(c == 0), stop=False),
                            reads=[f"wp{c}", f"xnT.w{g}", bk], writes=[bk],
                            tag=f"{tag}.bs.{t}.{f}.{c}")
                    P.op("tensor", lambda q, t=t, f=f, g=g, bk=bk: q.matmul(
                        bank[bk], axwS[:, (g % 2) * 512 + (t % 4) * 128:(g % 2) * 512 + (t % 4 + 1) * 128],
                        wpool[:, 9216 + f * 512:9216 + (f + 1) * 512],
                        start=False, stop=False),
                        reads=[f"axwS{g % 2}", "wp9", bk], writes=[bk], tag=f"{tag}.cmb.{t}.{f}")
                    P.op("tensor", lambda q, f=f, bk=bk: q.matmul(
                        bank[bk], rows[64:65, 0:128], rows[64:65, 128 + f * 512:128 + (f + 1) * 512],
                        start=False, stop=True),
                        reads=["ones64", "bbrow", bk], writes=[bk], tag=f"{tag}.bsb.{t}.{f}")
                    P.op("vector", lambda q, t=t, f=f, bk=bk: q.tensor_tensor(
                        x[:, t * 1024 + f * 512:t * 1024 + (f + 1) * 512],
                        x[:, t * 1024 + f * 512:t * 1024 + (f + 1) * 512], bank[bk], OP.add),
                        reads=[bk, f"x.{t}"], writes=[f"x.{t}"], tag=f"{tag}.xadd.{t}.{f}")

        def hooks(g):
            if hook is not None:
                for t in range(g * 4, g * 4 + 4):
                    hook(t)

        for g in range(5):
            if g < 4:
                ln_group(ln_rest_tag, g)
                producer(g)
            if g > 0:
                consumer(g - 1)
            if g < 4:
                producer_b(g)
            if g > 0:
                hooks(g - 1)
        if next_loader is not None:
            next_loader()

    subs = [("attn", "a"), ("ffn", None), ("attn", "b"), ("ffn", None)][:n_sub]

    for t in range(NT):
        xin_tile(0, t)

    def make_last_hook(seq):
        def hook(t):
            xout_tile(seq, t)
            if seq == 0:
                xin_tile(1, t)
                ln_tile_stats("s1.0.ln", t)
        return hook

    sched = []
    for seq in range(2):
        for si, (sk, kind) in enumerate(subs):
            sched.append((seq, si, sk, kind, f"s{seq}.{si}"))

    for idx, (seq, si, sk, kind, tg) in enumerate(sched):
        if si + 1 < len(subs):
            hook = make_ln_hook(f"s{seq}.{si + 1}.ln")
        else:
            hook = make_last_hook(seq)
        nxt = sched[idx + 1] if idx + 1 < len(sched) else None
        pf = None
        if nxt is None:
            nl = None
        elif nxt[2] == "ffn":
            nl = (lambda tgn=nxt[4]: ffn_load(tgn))
        elif sk == "ffn":
            nl = (lambda kk=nxt[3], tgn=nxt[4]: attn_load_tail(kk, tgn))
            pf = (lambda kk=nxt[3], tgn=nxt[4]: attn_prefetch0(kk, tgn))
        else:
            nl = (lambda kk=nxt[3], tgn=nxt[4]: attn_load(kk, tgn))
        if sk == "attn":
            boundary = idx > 0 and sched[idx - 1][2] == "ffn"
            attn(seq, kind, tg, hook=hook, next_loader=nl,
                 preloaded=idx > 0,
                 inline_ln_tag=(tg + ".ln") if idx == 0 else None,
                 ln_rest_tag=tg + ".ln",
                 hg0_slots=(11, 12, 13) if boundary else None,
                 wo_deferred=boundary)
        else:
            ffn(tg, hook=hook, next_loader=nl, preloaded=True, ln_rest_tag=tg + ".ln",
                prefetch=pf)

    P.finalize()

    blk = ctx.enter_context(nc.Block())

    def closing(q):
        for t in range(NT):
            q.wait_ge(dma_sems[f"xo{t}"], P.dma_cum[f"xo{t}"])

    P.emit(blk, eng_sems, dma_sems, post={"sync": closing})
    return nc, ctx



def kernel(**inp):
    n_sub = _N_SUB
    xfull = np.ascontiguousarray(np.asarray(inp["L0_attn_a_x"], dtype=np.float32))

    def f32(k):
        return np.asarray(inp[k], dtype=np.float32)

    WqT = _hg_stream(np.ascontiguousarray(f32("L0_attn_a_Wq").T))
    WkT = _hg_stream(np.ascontiguousarray(f32("L0_attn_a_Wk").T))
    WvT = _hg_stream(np.ascontiguousarray(f32("L0_attn_a_Wv").T))
    WoT = _tile2(np.ascontiguousarray(f32("L0_attn_a_Wo").T))
    WqB = _hg_stream(np.ascontiguousarray(f32("L0_attn_b_W_query").transpose(1, 0, 2).reshape(D, D)))
    WkB = _hg_stream(np.ascontiguousarray(f32("L0_attn_b_W_key").transpose(1, 0, 2).reshape(D, D)))
    WvB = _hg_stream(np.ascontiguousarray(f32("L0_attn_b_W_val").transpose(1, 0, 2).reshape(D, D)))
    WoB = _tile2(np.ascontiguousarray(f32("L0_attn_b_W_out").reshape(D, D)))
    baseT = _tile2(np.ascontiguousarray(f32("L0_ffn_base_w").T))
    astT = _tile2(np.ascontiguousarray(f32("L0_ffn_lora_A").transpose(1, 0, 2).reshape(D, 128)))
    bstT = _tile2(np.ascontiguousarray(f32("L0_ffn_lora_B").reshape(128, D)))
    routerT = _tile2(np.ascontiguousarray(f32("L0_ffn_router_w").T))

    bigi = np.zeros((128, 896), dtype=ml_dtypes.bfloat16)
    for p in range(128):
        bigi[p, 384 + p] = 1
    repm = np.zeros((8, 128), dtype=ml_dtypes.bfloat16)
    for e in range(8):
        repm[e, e * 16:(e + 1) * 16] = 1

    common = {
        "wq_a": WqT, "wk_a": WkT, "wv_a": WvT, "wo_a": WoT,
        "wq_b": WqB, "wk_b": WkB, "wv_b": WvB, "wo_b": WoB,
        "base": baseT, "ast": astT, "bst": bstT, "router": routerT,
        "base_b": f32("L0_ffn_base_b").reshape(1, D),
        "router_b": f32("L0_ffn_router_b").reshape(1, 8),
        "bo": f32("L0_attn_a_bo").reshape(1, D),
        "bvrow": f32("L0_attn_a_bv").reshape(1, D),
        "bqt": _bias_tile(f32("L0_attn_a_bq")),
        "bkt": _bias_tile(f32("L0_attn_a_bk")),
        "bvt": _bias_tile(f32("L0_attn_a_bv")),
        "head_mask": f32("L0_attn_a_head_mask").reshape(1, 16),
        "bigi": bigi, "repmat": repm,
    }
    common = {k: np.ascontiguousarray(v) for k, v in common.items()}

    in_maps = []
    for c in range(NCORES):
        m = dict(common)
        m["x"] = np.ascontiguousarray(xfull[2 * c:2 * c + 2])
        in_maps.append(m)

    nc, ctx = build_nc(n_sub)
    try:
        res = run_bass_kernel_spmd(nc, in_maps, core_ids=list(range(NCORES)))
    finally:
        ctx.close()
    out = np.concatenate([res.results[c]["y"] for c in range(NCORES)], axis=0)
    return np.ascontiguousarray(out.astype(np.float32))
